# version 09 (final)
# baseline (speedup 1.0000x reference)
import numpy as np
import ml_dtypes

import concourse.bass as bass
import concourse.mybir as mybir
from concourse.bass_utils import run_bass_kernel_spmd

F32 = mybir.dt.float32
BF16 = mybir.dt.bfloat16
I32 = mybir.dt.int32
AF = mybir.ActivationFunctionType
ALU = mybir.AluOpType
BF = ml_dtypes.bfloat16
Pool = mybir.EngineType.Pool
SPE = mybir.EngineType.SP
ACTE = mybir.EngineType.Activation

D = 4096
L = 2048
NT = 32
NH = 16
TOK = 1024
T2 = 512
SC = 0.08838834764831845
ENGS = ("tensor", "vector", "scalar", "sync", "gpsimd")
RG = [[0, 1], [2, 3], [4, 5], [6, 7]]


def _swz(m):
    c = m.shape[1]
    return np.ascontiguousarray(
        m.reshape(NT, 128, c).transpose(1, 0, 2)).reshape(128, NT * c)


class Build:
    def __init__(self, nc):
        self.nc = nc
        self.q = {e: [] for e in ENGS}
        self.sem = {}
        self.cnt = {}
        self.last_wait = {}

    def add_sem(self, name):
        self.sem[name] = self.nc.alloc_semaphore(name=name)
        self.cnt[name] = 0

    def op(self, eng, fn, inc=None, dv=1):
        if inc is None:
            self.q[eng].append(fn)
            return None
        self.cnt[inc] += dv
        s = self.sem[inc]

        def run(E, fn=fn, s=s, dv=dv):
            fn(E).then_inc(s, dv)
        self.q[eng].append(run)
        return self.cnt[inc]

    def wait(self, eng, name, val):
        if val is None or val <= 0:
            return
        k = (eng, name)
        if self.last_wait.get(k, 0) >= val:
            return
        self.last_wait[k] = val
        s = self.sem[name]
        self.q[eng].append(lambda E, s=s, v=val: E.wait_ge(s, v))


def build_program():
    nc = bass.Bass("TRN2", target_bir_lowering=False,
                   detect_race_conditions=False)

    din = lambda n, shp, dt=F32: nc.dram_tensor(n, shp, dt, kind="ExternalInput")
    xt = din("xt", [128, NT, L])
    xr = din("xr", [128, NT, TOK])
    wqkv = din("wqkv", [NH, 3, 128, NT * 128])
    bqkv_c = din("bqkv_c", [128, NH * 3])
    wo = din("wo", [NT, 128, NT * 128])
    bo_c = din("bo_c", [128, NT])
    wb = din("wb", [NT, 128, NT * 128])
    bb_c = din("bb_c", [128, NT])
    lora = din("lora", [128, NT * 36])
    bcat = din("bcat", [32, D])
    brpr = din("brpr", [36, 2])
    rep4 = din("rep4", [36, 32], BF16)
    cmsk = din("cmsk", [128, 128])
    cmsk01 = din("cmsk01", [128, 128], BF16)
    idm = din("idm", [128, 128], BF16)
    pk = din("pk", [1, 2], I32)

    out = nc.dram_tensor("out", [128, NT, TOK], F32, kind="ExternalOutput")
    buf2 = nc.dram_tensor("buf2", [2, 128, NH, TOK], BF16)
    agin1 = nc.dram_tensor("agin1", [128, 8, TOK], BF16)
    agin2 = nc.dram_tensor("agin2", [128, 6, TOK], BF16)
    agin3 = nc.dram_tensor("agin3", [128, 2, TOK], BF16)
    agout1 = nc.dram_tensor("agout1", [2, 128, 8, TOK], BF16)
    agout2 = nc.dram_tensor("agout2", [2, 128, 6, TOK], BF16)
    agout3 = nc.dram_tensor("agout3", [2, 128, 2, TOK], BF16)

    sb = nc.alloc_sbuf_tensor
    xbf = sb("xbf", [128, NT * L], BF16)
    rbc = sb("rbc", [128, L], BF16)
    kb = sb("kb", [128, L], BF16)
    qc = sb("qc", [128, 2 * 512], BF16)
    vfm = sb("vfm", [128, L], BF16)
    vtm = sb("vtm", [128, L], BF16)
    wr = sb("wr", [128, 3 * NT * 128], BF16)
    pt = sb("pt", [128, 4 * 512], BF16)
    ho = sb("ho", [128, L], BF16)
    cmask_s = sb("cmask_s", [128, 128], F32)
    cmask01_s = sb("cmask01_s", [128, 128], BF16)
    idm_s = sb("idm_s", [128, 128], BF16)
    cst = sb("cst", [128, 8], F32)
    onec = sb("onec", [128, 1], BF16)
    oner = sb("oner", [1, 128], BF16)
    bqkv_s = sb("bqkv_s", [128, NH * 3], F32)
    bo_s = sb("bo_s", [128, NT], F32)
    bb_s = sb("bb_s", [128, NT], F32)
    brpr_s = sb("brpr_s", [36, 2], F32)
    abias_s = sb("abias_s", [36, 1], F32)
    rep4_s = sb("rep4_s", [36, 32], BF16)
    pk_s = sb("pk_s", [1, 2], I32)
    ones128 = sb("ones128", [128, 128], BF16)
    vtmp = sb("vtmp", [128, 2 * 512], F32)
    xsqt = sb("xsqt", [128, 2 * 2048], BF16)
    prow = sb("prow", [1, 512], F32)
    r0w = sb("r0w", [1, 512], BF16)
    xst = sb("xst", [128, TOK], F32)
    outst = sb("outst", [128, 2 * 512], F32)
    nrmb = sb("nrmb", [128, 2 * 512], BF16)

    at = nc.alloc_sbuf_tensor_at
    adr = lambda t: nc.lookup_mloc(t).addr
    hT = at("hT", [128, NT * TOK], BF16, offset=adr(xbf))
    x1b = at("x1b", [128, NT * TOK], BF16, offset=adr(xbf) + 65536)
    bcat_s = at("bcat_s", [32, D], BF16, offset=adr(rbc))
    m2b = at("m2b", [128, TOK], BF16, offset=adr(qc))
    r2bc = at("r2bc", [128, TOK], BF16, offset=adr(qc) + 2048)
    s2row = at("s2row", [1, TOK], BF16, offset=adr(qc) + 4096)
    zw = at("zw", [32, TOK], BF16, offset=adr(qc) + 6144)
    zr = at("zr", [36, T2], F32, offset=adr(pt))
    exps = at("exps", [36, T2], BF16, offset=adr(pt) + 2048)
    loraT = at("loraT", [128, NT * 36], BF16, offset=adr(pt) + 3072)

    pb = [nc.alloc_psum_tensor(f"pb{i}", [128, 512], F32) for i in range(7)]
    pbt = nc.alloc_psum_tensor("pbt", [128, 1024], BF16)

    bd = Build(nc)
    for s in ("sg", "ss", "ct", "cv", "cs", "cg", "cc", "cc2", "cc3",
              "sa", "sa2", "sa3"):
        bd.add_sem(s)
    W = bd.wait

    def g_dma(dst, src, fence=True):
        v = bd.op("gpsimd", lambda E, d=dst, s=src: E.dma_start(out=d, in_=s),
                  inc="sg", dv=16)
        if fence:
            W("gpsimd", "sg", v)
        return v

    def s_dma(dst, src, fence=True):
        v = bd.op("sync", lambda E, d=dst, s=src: E.dma_start(out=d, in_=s),
                  inc="ss", dv=16)
        if fence:
            W("sync", "ss", v)
        return v

    def a_dma(dst, src, fence=True):
        v = bd.op("scalar", lambda E, d=dst, s=src: E.dma_start(out=d, in_=s),
                  inc="sa", dv=16)
        if fence:
            W("scalar", "sa", v)
        return v

    def t_mm(o, l, r, st, sp, inc=None):
        return bd.op("tensor", lambda E, o=o, l=l, r=r, st=st, sp=sp:
                     E.matmul(o, l, r, start=st, stop=sp), inc=inc)

    def t_tr(o, i, inc=None):
        return bd.op("tensor", lambda E, o=o, i=i: E.transpose(o, i, idm_s[:]),
                     inc=inc)

    def v_tt(o, a, b2, alu, inc=None):
        return bd.op("vector", lambda E, o=o, a=a, b=b2, al=alu:
                     E.tensor_tensor(o, a, b, al), inc=inc)

    def v_copy(o, i, inc=None):
        return bd.op("vector", lambda E, o=o, i=i: E.tensor_copy(o, i), inc=inc)

    def v_tsc(o, i, c, alu, inc=None):
        return bd.op("vector", lambda E, o=o, i=i, c=c, al=alu:
                     E.tensor_single_scalar(o, i, c, al), inc=inc)

    def v_recip(o, i, inc=None):
        def f(E, o=o, i=i):
            with nc.allow_low_precision("bf16 softmax-normaliser reciprocal"):
                return E.reciprocal(o, i)
        return bd.op("vector", f, inc=inc)

    def s_act(o, i, f, bias=0.0, scale=1.0, inc=None):
        return bd.op("scalar", lambda E, o=o, i=i, f=f, b=bias, sc=scale:
                     E.activation(o, i, f, bias=b, scale=sc), inc=inc)

    xtile = lambda t: xbf[:, L * t:L * (t + 1)]
    htile = lambda t: hT[:, TOK * t:TOK * (t + 1)]
    x1tl = lambda t: x1b[:, TOK * t:TOK * (t + 1)]

    CN, CEPS, CE12, C2, CID = (cst[:, i:i + 1] for i in range(5))

    ss_cm = s_dma(cmask_s[:], cmsk[:])
    ss_cm01 = s_dma(cmask01_s[:], cmsk01[:])
    ss_id = s_dma(idm_s[:], idm[:])
    ss_bq = s_dma(bqkv_s[:], bqkv_c[:])
    ss_bo = s_dma(bo_s[:], bo_c[:])
    ss_bb = s_dma(bb_s[:], bb_c[:])
    ss_br = s_dma(brpr_s[:], brpr[:])
    ss_rep = s_dma(rep4_s[:], rep4[:])
    ss_pk = s_dma(pk_s[:], pk[:])
    for i, v in enumerate((-25.0, 1e-5, 1e-12, 2.0, 1.0 / D)):
        bd.op("gpsimd", lambda E, i=i, v=v: E.memset(cst[:, i:i + 1], v), inc="cg")
    bd.op("gpsimd", lambda E: E.memset(onec[:], 1.0), inc="cg")
    bd.op("gpsimd", lambda E: E.memset(oner[:], 1.0), inc="cg")
    cg_ones = bd.op("gpsimd", lambda E: E.memset(ones128[:], 1.0), inc="cg")
    W("vector", "ss", ss_br)
    W("vector", "cg", cg_ones)
    cv_ab0 = bd.op("vector", lambda E: E.tensor_scalar_add(
        abias_s[32:36, 0:1], brpr_s[32:36, 1:2], cst[32:36, 2:3]), inc="cv")
    W("scalar", "cv", cv_ab0)
    cs_ab = s_act(abias_s[32:36, 0:1], abias_s[32:36, 0:1], AF.Ln, inc="cs")
    W("vector", "cs", cs_ab)
    cv_ab = v_tt(abias_s[32:36, 0:1], abias_s[32:36, 0:1], brpr_s[32:36, 0:1],
                 ALU.add, inc="cv")

    sg_x = []
    for t in range(NT):
        v = g_dma(xtile(t), xt[:, t, :], fence=(t % 4 == 3))
        sg_x.append(v)
    for t in range(NT):
        sg_x[t] = sg_x[min(NT - 1, (t // 4) * 4 + 3)]

    slot_free = [0, 0, 0]

    def wslot(i):
        return wr[:, NT * 128 * (i % 3):NT * 128 * ((i % 3) + 1)]

    sg_w = {}

    def load_ring(i, src_ap):
        W("gpsimd", "ct", slot_free[i % 3])
        sg_w[i] = g_dma(wslot(i), src_ap)

    for p in range(3):
        load_ring(p, wqkv[0, p, :, :])

    ct_sq_hist = [0, 0]
    for t in range(NT):
        W("tensor", "sg", sg_x[t])
        if t == 0:
            W("tensor", "cg", cg_ones)
        for c in range(4):
            t_mm(pb[c][:, :], ones128[:, :],
                 xtile(t)[:, 512 * c:512 * (c + 1)], t == 0, t == NT - 1)
        sl_ = t % 2
        xo_ = xsqt[:, 2048 * sl_:2048 * (sl_ + 1)]
        if sl_ == 0:
            W("vector", "sg", sg_x[t])
            W("vector", "ct", ct_sq_hist[0])
            sq_v = v_tt(xo_, xtile(t)[:, :], xtile(t)[:, :], ALU.mult,
                        inc="cv")
            W("tensor", "cv", sq_v)
        else:
            W("scalar", "sg", sg_x[t])
            W("scalar", "ct", ct_sq_hist[1])
            sq_s = s_act(xo_, xtile(t)[:, :], AF.Square, inc="cs")
            W("tensor", "cs", sq_s)
        for c in range(4):
            rp = 32 * (c % 2)
            ct_sq = t_mm(pb[5 + c // 2][rp:rp + 1, :], onec[:, :],
                         xo_[:, 512 * c:512 * (c + 1)], t == 0, t == NT - 1,
                         inc=("ct" if c == 3 else None))
        ct_sq_hist[sl_] = ct_sq

    ct_stream = bd.cnt["ct"]
    cv_center_l = [None] * 4
    cs_rr_prev = [0]
    ct_bc_prev = [0]
    pb4_free_s = 0
    cv_e2_last = 0
    bank_free = [0, 0, 0, 0]
    for c in range(4):
        cs_ = 512 * c
        W("vector", "ct", ct_stream)
        W("vector", "cs", cs_rr_prev[0])
        cv_m = bd.op("vector", lambda E, c=c: E.tensor_scalar_mul(
            outst[:, 0:512], pb[c][:, :], cst[:, 4:5]), inc="cv")
        bank_free[c] = cv_m
        cv_mb = v_copy(m2b[:, 0:512], outst[:, 0:512], inc="cv")
        rp = 32 * (c % 2)
        cv_e2 = bd.op("vector", lambda E, c=c, rp=rp: E.tensor_scalar_mul(
            outst[0:1, 512:1024], pb[5 + c // 2][rp:rp + 1, :],
            cst[0:1, 4:5]), inc="cv")
        cv_e2_last = cv_e2
        if c == 1:
            cv_e2_pb5 = cv_e2
        v_tt(prow[0:1, :], outst[0:1, 0:512], outst[0:1, 0:512], ALU.mult)
        cv_var = v_tt(outst[0:1, 512:1024], outst[0:1, 512:1024], prow[0:1, :],
                      ALU.subtract, inc="cv")
        W("scalar", "cv", cv_var)
        W("scalar", "ct", ct_bc_prev[0])
        s_act(prow[0:1, :], outst[0:1, 512:1024], AF.Ln, bias=cst[0:1, 1:2])
        cs_rr = s_act(r0w[0:1, :], prow[0:1, :], AF.Exp, scale=-0.5, inc="cs")
        cs_rr_prev[0] = cs_rr
        W("tensor", "cs", cs_rr)
        W("tensor", "cv", pb4_free_s)
        ct_bc = t_mm(pb[4][:, :], oner[:, :], r0w[0:1, :], True, True,
                     inc="ct")
        ct_bc_prev[0] = ct_bc
        W("vector", "ct", ct_bc)
        cv_rbc = v_copy(rbc[:, cs_:cs_ + 512], pb[4][:, :], inc="cv")
        pb4_free_s = cv_rbc
        W("vector", "cv", cv_mb)
        W("vector", "cs", cs_rr)
        lst = []
        for t in range(NT):
            lst.append(v_tt(xtile(t)[:, cs_:cs_ + 512],
                            xtile(t)[:, cs_:cs_ + 512], m2b[:, 0:512],
                            ALU.subtract, inc="cv"))
        cv_center_l[c] = lst


    bg = {0: ("cv", bank_free[0]), 1: ("cv", bank_free[1]),
          2: ("cv", bank_free[2]), 3: ("cv", bank_free[3]),
          4: ("cv", pb4_free_s), 5: ("cv", cv_e2_pb5), 6: ("cv", cv_e2_last)}
    rb_free = [0, 0]
    cv_vtm_done = [0]
    vtmp_cv = [0, 0]
    pbt_free = 0
    ho_free = [0, 0, 0, 0]
    vtmp_free = [0, 0]
    gidx = 0
    ss_ho = {}
    ct_xdead = 0
    first_diag = [True]
    first_tr = [True]

    def Wg(eng, gate):
        W(eng, gate[0], gate[1])

    def proj_group(ridx, c, dst, bias_col, tile_gates=None):
        nonlocal gidx
        bk = gidx % 2
        bank = pb[bk]
        Wg("tensor", bg[bk])
        if ridx in sg_w:
            W("tensor", "sg", sg_w.pop(ridx))
        for t in range(NT):
            if tile_gates is not None:
                W("tensor", "cv", tile_gates[t])
            ct_l = t_mm(bank[:, :], wslot(ridx)[:, 128 * t:128 * (t + 1)],
                        xtile(t)[:, 512 * c:512 * (c + 1)], t == 0, t == NT - 1,
                        inc=("ct" if t == NT - 1 else None))
        sl = gidx % 2
        W("vector", "ct", ct_l)
        W("vector", "cs", vtmp_free[sl])
        cv_e = v_tt(vtmp[:, 512 * sl:512 * (sl + 1)], bank[:, :],
                    rbc[:, 512 * c:512 * (c + 1)], ALU.mult, inc="cv")
        bg[bk] = ("cv", cv_e)
        vtmp_cv[sl] = cv_e
        W("scalar", "cv", cv_e)
        W("scalar", "ss", ss_bq)
        cs_e = bd.op("scalar", lambda E, o=dst, i=vtmp[:, 512 * sl:512 * (sl + 1)],
                     b=bias_col: E.activation(o, i, AF.Identity, bias=b, scale=1.0),
                     inc="cs")
        vtmp_free[sl] = cs_e
        gidx += 1
        return ct_l, cs_e

    def attn_pair(h, p):
        cA, cB = 2 * p, 2 * p + 1
        cfg = {"A": dict(c=cA, sbk=2, obk=4, mbk=5, ps=0),
               "B": dict(c=cB, sbk=3, obk=6, mbk=1, ps=1)}
        for st in cfg.values():
            st["nkt"] = 4 * st["c"] + 4
            st["qcs"] = qc[:, 512 * (st["c"] % 2):512 * (st["c"] % 2 + 1)]
            st["ct_av"] = 0
            st["cs_prev"] = {2: 0, 3: 0}
        W("tensor", "cv", cv_vtm_done[0])
        W("tensor", "cs", max(cs_q[cA], cs_q[cB]))
        seq = []
        for i in range(cfg["B"]["nkt"]):
            if i < cfg["A"]["nkt"]:
                seq.append(("A", i))
            seq.append(("B", i))
        def emit_S(X, kt):
            st = cfg[X]
            c = st["c"]
            diag = kt >= 4 * c
            j0 = 128 * (kt - 4 * c) if diag else 0
            sbk = st["sbk"]
            if X == "B" and kt >= cfg["A"]["nkt"]:
                sbk = 2 + (kt - cfg["A"]["nkt"]) % 2
            ps = 2 * st["ps"] + (kt % 2)
            Wg("tensor", bg[sbk])
            W("tensor", "cs", st["cs_prev"][sbk])
            ct_s = t_mm(pb[sbk][:, j0:512], kb[:, 128 * kt:128 * (kt + 1)],
                        st["qcs"][:, j0:512], True, True, inc="ct")
            W("scalar", "ct", ct_s)
            cs_e = bd.op("scalar", lambda E,
                         o=pt[:, 512 * ps + j0:512 * (ps + 1)],
                         i=pb[sbk][:, j0:512]:
                         E.activation(o, i, AF.Exp, bias=cst[:, 0:1], scale=SC),
                         inc="cs")
            st["cs_prev"][sbk] = cs_e
            cv_msk = 0
            if diag:
                W("vector", "cs", cs_e)
                if first_diag[0]:
                    W("vector", "ss", ss_cm01)
                    first_diag[0] = False
                cv_msk = v_tt(pt[:, 512 * ps + j0:512 * ps + j0 + 128],
                              pt[:, 512 * ps + j0:512 * ps + j0 + 128],
                              cmask01_s[:], ALU.mult, inc="cv")
            return (X, kt, j0, cs_e, cv_msk, sbk)

        def emit_AV(pend):
            X, kt, j0, cs_e, cv_msk, sbk = pend
            st = cfg[X]
            c = st["c"]
            obk, mbk = st["obk"], st["mbk"]
            ps = 2 * st["ps"] + (kt % 2)
            if cv_msk:
                W("tensor", "cv", cv_msk)
            W("tensor", "cs", cs_e)
            if kt == 0:
                Wg("tensor", bg[obk])
                Wg("tensor", bg[mbk])
            t_mm(pb[obk][:, j0:512], vtm[:, 128 * kt:128 * (kt + 1)],
                 pt[:, 512 * ps + j0:512 * (ps + 1)], kt == 0, kt == st["nkt"] - 1)
            ct_av = t_mm(pb[mbk][:, j0:512], ones128[:, :],
                         pt[:, 512 * ps + j0:512 * (ps + 1)], kt == 0,
                         kt == st["nkt"] - 1, inc="ct")
            bg[sbk] = ("ct", ct_av)
            if kt == st["nkt"] - 1:
                es = st["ps"]
                W("scalar", "ct", ct_av)
                W("scalar", "cv", vtmp_cv[es])
                cs_ln = s_act(vtmp[:, 512 * es:512 * (es + 1)], pb[mbk][:, 0:512],
                              AF.Ln, inc="cs")
                bg[mbk] = ("cs", cs_ln)
                W("scalar", "cv", rb_free[es])
                cs_rb = s_act(nrmb[:, 512 * es:512 * (es + 1)],
                              vtmp[:, 512 * es:512 * (es + 1)], AF.Exp,
                              scale=-1.0, inc="cs")
                vtmp_free[es] = cs_rb
                W("vector", "cs", cs_rb)
                W("vector", "ss", ho_free[c])
                cv_ho = v_tt(ho[:, 512 * c:512 * (c + 1)], pb[obk][:, 0:512],
                             nrmb[:, 512 * es:512 * (es + 1)], ALU.mult, inc="cv")
                bg[obk] = ("cv", cv_ho)
                rb_free[es] = cv_ho
                W("sync", "cv", cv_ho)
                ss_ho[(h, c)] = s_dma(
                    buf2[c // 2, :, h, 512 * (c % 2):512 * (c % 2 + 1)],
                    ho[:, 512 * c:512 * (c + 1)])
                ho_free[c] = ss_ho[(h, c)]

        pend = None
        for item in seq:
            cur = emit_S(*item)
            if pend is not None:
                emit_AV(pend)
            pend = cur
        emit_AV(pend)

    dynv = {}
    for h in range(NH):
        for c in range(4):
            ct_l, _ = proj_group(3 * h, c, vfm[:, 512 * c:512 * (c + 1)],
                                 bqkv_s[:, 3 * h:3 * h + 1],
                                 tile_gates=(cv_center_l[c] if h == 0
                                             else None))
        slot_free[(3 * h) % 3] = ct_l
        if h + 1 < NH:
            load_ring(3 * (h + 1), wqkv[h + 1, 0, :, :])
        cs_v_done = bd.cnt["cs"]

        cv_tr = 0
        for c in range(4):
            ct_l, _ = proj_group(3 * h + 1, c, kb[:, 512 * c:512 * (c + 1)],
                                 bqkv_s[:, 3 * h + 1:3 * h + 2])
            if c < 2:
                W("tensor", "cs", cs_v_done)
                W("tensor", "cv", pbt_free)
                if first_tr[0]:
                    W("tensor", "ss", ss_id)
                    first_tr[0] = False
                for k in range(8):
                    kk = 8 * c + k
                    ct_tr = t_tr(pbt[:, 128 * k:128 * (k + 1)],
                                 vfm[:, 128 * kk:128 * (kk + 1)],
                                 inc=("ct" if k == 7 else None))
                W("vector", "ct", ct_tr)
                cv_tr = v_copy(vtm[:, 1024 * c:1024 * (c + 1)], pbt[:, 0:1024],
                               inc="cv")
                pbt_free = cv_tr
        slot_free[(3 * h + 1) % 3] = ct_l
        if h + 1 < NH:
            load_ring(3 * (h + 1) + 1, wqkv[h + 1, 1, :, :])
        cv_vtm_done[0] = cv_tr

        cs_q = {}
        for c in range(4):
            if c == 2:
                attn_pair(h, 0)
            ct_q, cs_q[c] = proj_group(3 * h + 2, c,
                                       qc[:, 512 * (c % 2):512 * (c % 2 + 1)],
                                       bqkv_s[:, 3 * h + 2:3 * h + 3])
            if c == 3:
                slot_free[(3 * h + 2) % 3] = ct_q
                if h + 1 < NH:
                    load_ring(3 * (h + 1) + 2, wqkv[h + 1, 2, :, :])
                if h == NH - 1:
                    ct_xdead = ct_q
        if h == NH - 1:
            W("sync", "ct", ct_xdead)
            W("sync", "ss", max(ss_ho[(hh, cc_)] for hh in range(15)
                                for cc_ in range(4)))
            sa_own = bd.op("sync", lambda E: E.dma_start(
                out=hT[:, 0:15 * TOK].rearrange("p (t c) -> p t c", t=15),
                in_=buf2[bass.ds(dynv["mb_sp"], 1), :, 0:15, :].squeeze(0)),
                inc="sa", dv=16)
            W("sync", "cc", 1)
            sa_p1 = bd.op("sync", lambda E: E.dma_start(
                out=hT[:, 16 * TOK:24 * TOK].rearrange("p (t c) -> p t c", t=8),
                in_=agout1[bass.ds(dynv["pb_sp"], 1), :, :, :].squeeze(0)),
                inc="sa2", dv=16)
            W("sync", "cc2", 1)
            sa_p2 = bd.op("sync", lambda E: E.dma_start(
                out=hT[:, 24 * TOK:30 * TOK].rearrange("p (t c) -> p t c", t=6),
                in_=agout2[bass.ds(dynv["pb_sp"], 1), :, :, :].squeeze(0)),
                inc="sa3", dv=16)
            for m_ in range(3):
                load_ring(48 + m_, wo[m_, :, :])
        attn_pair(h, 1)

        if h == 7:
            W("gpsimd", "ss", max(ss_ho[(hh, c)] for hh in range(8)
                                  for c in range(4)))
            W("gpsimd", "ss", ss_pk)

            def _dyn(E):
                _, v1 = nc.values_load_multi_w_load_instructions(
                    pk_s[0:1, 0:1], engines=[Pool], min_val=0, max_val=1,
                    skip_runtime_bounds_check=True)
                dynv["pb"] = v1[0]
            bd.q["gpsimd"].append(_dyn)
            sg_a1 = bd.op("gpsimd", lambda E: E.dma_start(
                out=agin1[:],
                in_=buf2[bass.ds(dynv["pb"], 1), :, 0:8, :].squeeze(0)),
                inc="sg", dv=16)
            W("gpsimd", "sg", sg_a1)
            bd.op("gpsimd", lambda E: E.collective_compute(
                "AllGather", ALU.bypass, replica_groups=RG,
                ins=[agin1[:]], outs=[agout1[:]]), inc="cc")

            W("sync", "ss", ss_pk)

            def _dyn_sp(E):
                _, v1 = nc.values_load_multi_w_load_instructions(
                    pk_s[0:1, 0:1], engines=[SPE], min_val=0, max_val=1,
                    skip_runtime_bounds_check=True)
                dynv["pb_sp"] = v1[0]
                _, v2 = nc.values_load_multi_w_load_instructions(
                    pk_s[0:1, 1:2], engines=[SPE], min_val=0, max_val=1,
                    skip_runtime_bounds_check=True)
                dynv["mb_sp"] = v2[0]
            bd.q["sync"].append(_dyn_sp)

        if h == 13:
            W("sync", "ss", max(ss_ho[(hh, c)] for hh in range(8, 14)
                                for c in range(4)))
            ss_a2 = bd.op("sync", lambda E: E.dma_start(
                out=agin2[:],
                in_=buf2[bass.ds(dynv["pb_sp"], 1), :, 8:14, :].squeeze(0)),
                inc="ss", dv=16)
            W("sync", "ss", ss_a2)
            W("gpsimd", "ss", ss_a2)
            bd.op("gpsimd", lambda E: E.collective_compute(
                "AllGather", ALU.bypass, replica_groups=RG,
                ins=[agin2[:]], outs=[agout2[:]]), inc="cc2")


    ct_A_end = bd.cnt["ct"]
    cv_A_end = bd.cnt["cv"]
    W("sync", "ss", max(ss_ho[(15, c)] for c in range(4)))
    ss_a3 = bd.op("sync", lambda E: E.dma_start(
        out=agin3[:],
        in_=buf2[bass.ds(dynv["pb_sp"], 1), :, 14:16, :].squeeze(0)),
        inc="ss", dv=16)
    W("sync", "ss", ss_a3)
    W("gpsimd", "ss", ss_a3)
    bd.op("gpsimd", lambda E: E.collective_compute(
        "AllGather", ALU.bypass, replica_groups=RG,
        ins=[agin3[:]], outs=[agout3[:]]), inc="cc3")
    ss_own15 = bd.op("sync", lambda E: E.dma_start(
        out=hT[:, 15 * TOK:16 * TOK],
        in_=buf2[bass.ds(dynv["mb_sp"], 1), :, 15:16, :]
        .squeeze(0).squeeze(1)), inc="ss", dv=16)
    W("sync", "ss", ss_own15)
    W("sync", "cc3", 1)
    ss_p3 = bd.op("sync", lambda E: E.dma_start(
        out=hT[:, 30 * TOK:32 * TOK].rearrange("p (t c) -> p t c", t=2),
        in_=agout3[bass.ds(dynv["pb_sp"], 1), :, :, :].squeeze(0)),
        inc="ss", dv=16)
    W("sync", "ss", ss_p3)
    ss_A_end = bd.cnt["ss"]
    W("gpsimd", "ct", ct_A_end)
    W("gpsimd", "cv", cv_A_end)
    W("gpsimd", "ss", ss_A_end)
    sg_lora = g_dma(loraT[:], lora[:])
    sg_bcat = g_dma(bcat_s[:], bcat[:])

    W("tensor", "ct", ct_A_end)
    W("tensor", "cv", cv_A_end)
    W("tensor", "cs", bd.cnt["cs"])
    ob_free = [0, 0, 0]
    xst_free = 0
    outst_free = [0, 0]
    gi = 0
    wb_ct_gate = {}

    for m in range(NT):
        W("sync", "cv", xst_free)
        ss_x = s_dma(xst[:], xr[:, m, :])
        for c in range(2):
            bank = pb[gi % 3]
            W("tensor", "cs", ob_free[gi % 3])
            if m == 0 and c == 0:
                W("tensor", "sa", sa_own)
                W("tensor", "sg", sg_w.pop(48))
            elif (48 + m) in sg_w:
                W("tensor", "sg", sg_w.pop(48 + m))
            for t in range(NT):
                if m == 0 and c == 0 and t == 15:
                    W("tensor", "ss", ss_own15)
                if m == 0 and c == 0 and t == 16:
                    W("tensor", "sa2", sa_p1)
                if m == 0 and c == 0 and t == 24:
                    W("tensor", "sa3", sa_p2)
                if m == 0 and c == 0 and t == 30:
                    W("tensor", "ss", ss_p3)
                ct_l = t_mm(bank[:, 0:T2], wslot(48 + m)[:, 128 * t:128 * (t + 1)],
                            htile(t)[:, T2 * c:T2 * (c + 1)], t == 0, t == NT - 1,
                            inc=("ct" if t == NT - 1 else None))
            sl = gi % 2
            W("vector", "ct", ct_l)
            W("vector", "ss", ss_x)
            W("vector", "cs", vtmp_free[sl])
            cv_a = v_tt(vtmp[:, 512 * sl:512 * (sl + 1)], bank[:, 0:T2],
                        xst[:, T2 * c:T2 * (c + 1)], ALU.add, inc="cv")
            if c == 1:
                xst_free = cv_a
            W("scalar", "cv", cv_a)
            W("scalar", "ss", ss_bo)
            cs_x1 = bd.op("scalar", lambda E,
                          o=x1tl(m)[:, T2 * c:T2 * (c + 1)],
                          i=vtmp[:, 512 * sl:512 * (sl + 1)], b=bo_s[:, m:m + 1]:
                          E.activation(o, i, AF.Identity, bias=b, scale=1.0), inc="cs")
            ob_free[gi % 3] = cs_x1
            vtmp_free[sl] = cs_x1
            gi += 1
        slot_free[(48 + m) % 3] = ct_l
        nxt = 48 + m + 3
        if nxt < 80:
            load_ring(nxt, wo[nxt - 48, :, :])
        else:
            load_ring(nxt, wb[nxt - 80, :, :])

    cs_stage1 = bd.cnt["cs"]

    pb3_gate = bg[3]
    pb4_gate = bg[4]
    pb5_gate = bg[5]
    pb6_gate = bg[6]
    rb2_free = max(rb_free)
    cs2_prev = [0]
    cs_ex_prev = 0
    cv_zw = {}
    for c in range(2):
        c0 = T2 * c
        W("tensor", "cs", cs_stage1)
        W("vector", "cs", cs_stage1)
        ct_st = 0
        hist = [0, 0]
        for t in range(NT):
            sl_ = t % 2
            xsl = x1tl(t)[:, c0:c0 + T2]
            xo_ = xsqt[:, 512 * sl_:512 * sl_ + T2]
            if sl_ == 0:
                W("vector", "ct", hist[0])
                sq_v = v_tt(xo_, xsl, xsl, ALU.mult, inc="cv")
            else:
                W("scalar", "ct", hist[1])
                sq_s = s_act(xo_, xsl, AF.Square, inc="cs")
            if t == 0:
                W("tensor", pb3_gate[0], pb3_gate[1])
            t_mm(pb[3][:, 0:T2], ones128[:, :], xsl, t == 0, t == NT - 1)
            if sl_ == 0:
                W("tensor", "cv", sq_v)
            else:
                W("tensor", "cs", sq_s)
            if t == 0:
                W("tensor", pb4_gate[0], pb4_gate[1])
            ct_st = t_mm(pb[4][:, 0:T2], ones128[:, :], xo_, t == 0, t == NT - 1,
                         inc="ct")
            hist[sl_] = ct_st
        W("vector", "ct", ct_st)
        cv_m = bd.op("vector", lambda E: E.tensor_scalar_mul(
            outst[:, 0:T2], pb[3][:, 0:T2], cst[:, 4:5]), inc="cv")
        pb3_gate = ("cv", cv_m)
        cv_mb = v_copy(m2b[:, c0:c0 + T2], outst[:, 0:T2], inc="cv")
        v_tt(vtmp[:, 0:T2], outst[:, 0:T2], outst[:, 0:T2], ALU.mult)
        W("vector", "cs", cs2_prev[0])
        cv_e2 = bd.op("vector", lambda E: E.tensor_scalar_mul(
            outst[:, 512:512 + T2], pb[4][:, 0:T2], cst[:, 4:5]), inc="cv")
        pb4_gate = ("cv", cv_e2)
        cv_var = v_tt(outst[:, 512:512 + T2], outst[:, 512:512 + T2],
                      vtmp[:, 0:T2], ALU.subtract, inc="cv")
        W("scalar", "cv", cv_var)
        s_act(vtmp[:, 512:512 + T2], outst[:, 512:512 + T2], AF.Ln,
              bias=cst[:, 1:2])
        W("scalar", "cv", rb2_free)
        s_act(r2bc[:, c0:c0 + T2], vtmp[:, 512:512 + T2], AF.Exp, scale=-0.5)
        cs_s2 = s_act(nrmb[:, 0:T2], vtmp[:, 512:512 + T2], AF.Exp, scale=0.5,
                      inc="cs")
        cs2_prev[0] = cs_s2
        W("vector", "cv", cv_mb)
        W("vector", "cs", cs_s2)
        cv_c2l = []
        for t in range(NT):
            cv_c2l.append(v_tt(x1tl(t)[:, c0:c0 + T2], x1tl(t)[:, c0:c0 + T2],
                               m2b[:, c0:c0 + T2], ALU.subtract, inc="cv"))
        W("tensor", pb5_gate[0], pb5_gate[1])
        if c == 0:
            W("tensor", "sg", sg_lora)
        for t in range(NT):
            W("tensor", "cv", cv_c2l[t])
            ct_z = t_mm(pb[5][0:36, 0:T2], loraT[:, 36 * t:36 * (t + 1)],
                        x1tl(t)[:, c0:c0 + T2], t == 0, t == NT - 1,
                        inc=("ct" if t == NT - 1 else None))
        W("vector", "ct", ct_z)
        W("vector", "cs", cs_ex_prev)
        cv_zr = v_tt(zr[:, :], pb[5][0:36, 0:T2], r2bc[0:36, c0:c0 + T2],
                     ALU.mult, inc="cv")
        pb5_gate = ("cv", cv_zr)
        W("scalar", "cv", cv_zr)
        W("scalar", "cv", cv_ab)
        cs_ex = bd.op("scalar", lambda E: E.activation(
            exps[32:36, :], zr[32:36, :], AF.Exp, bias=abias_s[32:36, 0:1],
            scale=1.0), inc="cs")
        cs_ex_prev = cs_ex
        W("tensor", "cs", cs_ex)
        W("tensor", pb6_gate[0], pb6_gate[1])
        ct_se = t_mm(pb[6][:, 0:T2], ones128[32:36, :], exps[32:36, :], True,
                     True, inc="ct")
        W("scalar", "ct", ct_se)
        cs_li = s_act(vtmp[:, 0:T2], pb[6][:, 0:T2], AF.Ln, inc="cs")
        pb6_gate = ("cs", cs_li)
        cs_inv = s_act(nrmb[:, 512:512 + T2], vtmp[:, 0:T2], AF.Exp, scale=-1.0,
                       inc="cs")
        W("vector", "cs", cs_inv)
        v_tt(nrmb[:, 512:512 + T2], nrmb[:, 0:T2], nrmb[:, 512:512 + T2],
             ALU.mult)
        cv_q2 = bd.op("vector", lambda E: E.tensor_scalar_mul(
            nrmb[:, 512:512 + T2], nrmb[:, 512:512 + T2], cst[:, 3:4]), inc="cv")
        W("tensor", pb4_gate[0], pb4_gate[1])
        W("tensor", "ss", ss_rep)
        ct_ar = t_mm(pb[4][0:32, 0:T2], rep4_s[32:36, :], exps[32:36, :], True,
                     True, inc="ct")
        W("vector", "ct", ct_ar)
        cv_asc = v_tt(exps[0:32, :], pb[4][0:32, 0:T2],
                      nrmb[0:32, 512:512 + T2], ALU.mult, inc="cv")
        pb4_gate = ("cv", cv_asc)
        cv_zw[c] = v_tt(zw[:, c0:c0 + T2], zr[0:32, :], exps[0:32, :], ALU.mult,
                        inc="cv")
        rb2_free = cv_asc

    for m in range(NT):
        for c in range(2):
            c0 = T2 * c
            bank = pb[gi % 3]
            W("tensor", "cs", ob_free[gi % 3])
            if (80 + m) in sg_w:
                W("tensor", "sg", sg_w.pop(80 + m))
            if m == 0 and c == 0:
                W("tensor", "sg", sg_bcat)
            for t in range(NT):
                t_mm(bank[:, 0:T2], wslot(80 + m)[:, 128 * t:128 * (t + 1)],
                     x1tl(t)[:, c0:c0 + T2], t == 0, False)
            W("tensor", "cv", cv_zw[c])
            ct_l = t_mm(bank[:, 0:T2], bcat_s[:, 128 * m:128 * (m + 1)],
                        zw[:, c0:c0 + T2], False, True, inc="ct")
            sl = gi % 2
            W("vector", "ct", ct_l)
            W("vector", "cs", vtmp_free[0])
            cv_y1 = v_tt(vtmp[:, 0:512], bank[:, 0:T2], r2bc[:, c0:c0 + T2],
                         ALU.mult, inc="cv")
            W("scalar", "cv", cv_y1)
            W("scalar", "ss", ss_bb)
            cs_y2 = bd.op("scalar", lambda E, b=bb_s[:, m:m + 1]: E.activation(
                vtmp[:, 512:1024], vtmp[:, 0:512], AF.Identity, bias=b, scale=1.0),
                inc="cs")
            ob_free[gi % 3] = cs_y2
            vtmp_free[0] = cs_y2
            W("vector", "cs", cs_y2)
            W("vector", "ss", outst_free[sl])
            cv_x2a = v_tt(outst[:, 512 * sl:512 * (sl + 1)], vtmp[:, 512:1024],
                          x1tl(m)[:, c0:c0 + T2], ALU.add, inc="cv")
            cv_x2 = v_tt(outst[:, 512 * sl:512 * (sl + 1)],
                         outst[:, 512 * sl:512 * (sl + 1)], m2b[:, c0:c0 + T2],
                         ALU.add, inc="cv")
            W("sync", "cv", cv_x2)
            ss_o = s_dma(out[:, m, c0:c0 + T2], outst[:, 512 * sl:512 * (sl + 1)])
            outst_free[sl] = ss_o
            gi += 1
        nxt = 80 + m + 3
        if nxt < 112:
            slot_free[(80 + m) % 3] = ct_l
            load_ring(nxt, wb[nxt - 80, :, :])

    bd.wait("sync", "ss", bd.cnt["ss"])

    with nc.Block() as block:
        for name, dec in (("gpsimd", block.gpsimd), ("sync", block.sync),
                          ("tensor", block.tensor), ("vector", block.vector),
                          ("scalar", block.scalar)):
            def make(fns):
                def emit(E):
                    for f in fns:
                        f(E)
                return emit
            dec(make(bd.q[name]))

    return nc


_CACHE = {}


def _get_program():
    if "nc" not in _CACHE:
        _CACHE["nc"] = build_program()
    return _CACHE["nc"]


def kernel(**inputs):
    x = np.ascontiguousarray(np.asarray(inputs["L0_attn_x"], dtype=np.float32))
    Wq = np.asarray(inputs["L0_attn_Wq"], dtype=np.float32)
    bq = np.asarray(inputs["L0_attn_bq"], dtype=np.float32)
    Wk = np.asarray(inputs["L0_attn_Wk"], dtype=np.float32)
    bk = np.asarray(inputs["L0_attn_bk"], dtype=np.float32)
    Wv = np.asarray(inputs["L0_attn_Wv"], dtype=np.float32)
    bv = np.asarray(inputs["L0_attn_bv"], dtype=np.float32)
    Wo = np.asarray(inputs["L0_attn_Wo"], dtype=np.float32)
    bo = np.asarray(inputs["L0_attn_bo"], dtype=np.float32)
    Wb = np.asarray(inputs["L0_ffn_W_base"], dtype=np.float32)
    bb = np.asarray(inputs["L0_ffn_b_base"], dtype=np.float32)
    active = np.asarray(inputs["L0_ffn_active"]).astype(np.int64)
    A = np.asarray(inputs["L0_ffn_lora_A"], dtype=np.float32)[active]
    Bm = np.asarray(inputs["L0_ffn_lora_B"], dtype=np.float32)[active]
    Wr = np.asarray(inputs["L0_ffn_router_W"], dtype=np.float32)[active]
    br = np.asarray(inputs["L0_ffn_router_b"], dtype=np.float32)[active]
    pr = np.asarray(inputs["L0_ffn_domain_prior"], dtype=np.float32)[active]

    WoT = np.ascontiguousarray(Wo.T)
    WbT = np.ascontiguousarray(Wb.T)
    loraT_full = np.concatenate(
        [A.transpose(2, 0, 1).reshape(D, 32), Wr.T], axis=1)
    bcat_full = np.ascontiguousarray(Bm.transpose(0, 2, 1).reshape(32, D))
    brpr_full = np.zeros((36, 2), np.float32)
    brpr_full[32:36, 0] = br
    brpr_full[32:36, 1] = pr
    rep4_full = np.zeros((36, 32), BF)
    for e in range(4):
        rep4_full[32 + e, 8 * e:8 * e + 8] = 1
    i_ = np.arange(128)
    cm = np.where(i_[None, :] >= i_[:, None], 0.0, -1e9).astype(np.float32)
    cm01 = np.where(i_[None, :] >= i_[:, None], 1.0, 0.0).astype(BF)
    idm_full = np.eye(128, dtype=BF)

    def head_slices(h):
        arr = np.empty((NH, 3, 128, NT * 128), np.float32)
        for hp in range(NH):
            for p, Wm in enumerate((Wv, Wk, Wq)):
                blk = Wm[2048 * h + 128 * hp:2048 * h + 128 * (hp + 1), :].T
                arr[hp, p] = _swz(np.ascontiguousarray(blk))
        return arr

    def fo_tiles(M):
        arr = np.empty((NT, 128, NT * 128), np.float32)
        for m in range(NT):
            arr[m] = _swz(np.ascontiguousarray(M[:, 128 * m:128 * (m + 1)]))
        return arr

    wb_tiles = fo_tiles(WbT)
    lora_swz = _swz(loraT_full)

    in_maps = []
    for c in range(8):
        b, h = c // 2, c % 2
        xtb = _swz(np.ascontiguousarray(x[b].T)).reshape(128, NT, L)
        xrb = _swz(np.ascontiguousarray(
            x[b, TOK * h:TOK * (h + 1), :].T)).reshape(128, NT, TOK)
        bcol = np.empty((128, NH * 3), np.float32)
        for hp in range(NH):
            for p, bvec in enumerate((bv, bk, bq)):
                bcol[:, 3 * hp + p] = bvec[2048 * h + 128 * hp:
                                           2048 * h + 128 * (hp + 1)]
        WoTp = WoT if h == 0 else np.ascontiguousarray(
            np.vstack([WoT[2048:], WoT[:2048]]))
        in_maps.append({
            "xt": xtb, "xr": xrb, "wqkv": head_slices(h), "bqkv_c": bcol,
            "wo": fo_tiles(WoTp),
            "bo_c": np.ascontiguousarray(bo.reshape(NT, 128).T),
            "wb": wb_tiles,
            "bb_c": np.ascontiguousarray(bb.reshape(NT, 128).T),
            "lora": lora_swz, "bcat": bcat_full, "brpr": brpr_full,
            "rep4": rep4_full, "cmsk": cm, "cmsk01": cm01,
            "idm": idm_full,
            "pk": np.array([[1 - h, h]], dtype=np.int32),
        })

    nc = _get_program()
    res = run_bass_kernel_spmd(nc, in_maps, core_ids=list(range(8)))
    final = np.empty((4, L, D), np.float32)
    for c in range(8):
        b, h = c // 2, c % 2
        o = np.asarray(res.results[c]["out"])
        final[b, TOK * h:TOK * (h + 1), :] = o.transpose(2, 1, 0).reshape(TOK, D)
    return final


# version 10
# speedup vs baseline: 1.1070x; 1.1070x over previous
import numpy as np
import ml_dtypes

import concourse.bass as bass
import concourse.mybir as mybir
from concourse.bass_utils import run_bass_kernel_spmd

F32 = mybir.dt.float32
BF16 = mybir.dt.bfloat16
I32 = mybir.dt.int32
AF = mybir.ActivationFunctionType
ALU = mybir.AluOpType
BF = ml_dtypes.bfloat16
Pool = mybir.EngineType.Pool
SPE = mybir.EngineType.SP
ACTE = mybir.EngineType.Activation

D = 4096
L = 2048
NT = 32
NH = 16
TOK = 1024
T2 = 512
SC = 0.08838834764831845
ENGS = ("tensor", "vector", "scalar", "sync", "gpsimd")
RG = [[0, 1], [2, 3], [4, 5], [6, 7]]


def _swz(m):
    c = m.shape[1]
    return np.ascontiguousarray(
        m.reshape(NT, 128, c).transpose(1, 0, 2)).reshape(128, NT * c)


class Build:
    def __init__(self, nc):
        self.nc = nc
        self.q = {e: [] for e in ENGS}
        self.sem = {}
        self.cnt = {}
        self.last_wait = {}

    def add_sem(self, name):
        self.sem[name] = self.nc.alloc_semaphore(name=name)
        self.cnt[name] = 0

    def op(self, eng, fn, inc=None, dv=1):
        if inc is None:
            self.q[eng].append(fn)
            return None
        self.cnt[inc] += dv
        s = self.sem[inc]

        def run(E, fn=fn, s=s, dv=dv):
            fn(E).then_inc(s, dv)
        self.q[eng].append(run)
        return self.cnt[inc]

    def wait(self, eng, name, val):
        if val is None or val <= 0:
            return
        k = (eng, name)
        if self.last_wait.get(k, 0) >= val:
            return
        self.last_wait[k] = val
        s = self.sem[name]
        self.q[eng].append(lambda E, s=s, v=val: E.wait_ge(s, v))


def build_program():
    nc = bass.Bass("TRN2", target_bir_lowering=False,
                   detect_race_conditions=False)

    din = lambda n, shp, dt=F32: nc.dram_tensor(n, shp, dt, kind="ExternalInput")
    xt = din("xt", [128, NT, L])
    xr = din("xr", [128, NT, TOK])
    wqkv = din("wqkv", [NH, 3, 128, NT * 128])
    bqkv_c = din("bqkv_c", [128, NH * 3])
    wo = din("wo", [NT, 128, NT * 128])
    bo_c = din("bo_c", [128, NT])
    wb = din("wb", [NT, 128, NT * 128])
    bb_c = din("bb_c", [128, NT])
    lora = din("lora", [128, NT * 36])
    bcat = din("bcat", [32, D])
    brpr = din("brpr", [36, 2])
    rep4 = din("rep4", [36, 32], BF16)
    cmsk = din("cmsk", [128, 128])
    cmsk01 = din("cmsk01", [128, 128], BF16)
    idm = din("idm", [128, 128], BF16)
    pk = din("pk", [1, 2], I32)

    out = nc.dram_tensor("out", [128, NT, TOK], F32, kind="ExternalOutput")
    buf2 = nc.dram_tensor("buf2", [2, 128, NH, TOK], BF16)
    agin1 = nc.dram_tensor("agin1", [128, 8, TOK], BF16)
    agin2 = nc.dram_tensor("agin2", [128, 6, TOK], BF16)
    agin3 = nc.dram_tensor("agin3", [128, 2, TOK], BF16)
    agout1 = nc.dram_tensor("agout1", [2, 128, 8, TOK], BF16)
    agout2 = nc.dram_tensor("agout2", [2, 128, 6, TOK], BF16)
    agout3 = nc.dram_tensor("agout3", [2, 128, 2, TOK], BF16)

    sb = nc.alloc_sbuf_tensor
    xbf = sb("xbf", [128, NT * L], BF16)
    rbc = sb("rbc", [128, L], BF16)
    kb = sb("kb", [128, L], BF16)
    qc = sb("qc", [128, 2 * 512], BF16)
    vfm = sb("vfm", [128, L], BF16)
    vtm = sb("vtm", [128, L], BF16)
    wr = sb("wr", [128, 3 * NT * 128], BF16)
    pt = sb("pt", [128, 4 * 512], BF16)
    ho = sb("ho", [128, L], BF16)
    cmask_s = sb("cmask_s", [128, 128], F32)
    cmask01_s = sb("cmask01_s", [128, 128], BF16)
    idm_s = sb("idm_s", [128, 128], BF16)
    cst = sb("cst", [128, 8], F32)
    onec = sb("onec", [128, 1], BF16)
    oner = sb("oner", [1, 128], BF16)
    bqkv_s = sb("bqkv_s", [128, NH * 3], F32)
    bo_s = sb("bo_s", [128, NT], F32)
    bb_s = sb("bb_s", [128, NT], F32)
    brpr_s = sb("brpr_s", [36, 2], F32)
    abias_s = sb("abias_s", [36, 1], F32)
    rep4_s = sb("rep4_s", [36, 32], BF16)
    pk_s = sb("pk_s", [1, 2], I32)
    ones128 = sb("ones128", [128, 128], BF16)
    vtmp = sb("vtmp", [128, 2 * 512], F32)
    xsqt = sb("xsqt", [128, 2 * 2048], BF16)
    prow = sb("prow", [1, 512], F32)
    r0w = sb("r0w", [1, 512], BF16)
    xst = sb("xst", [128, TOK], F32)
    outst = sb("outst", [128, 2 * 512], F32)
    nrmb = sb("nrmb", [128, 2 * 512], BF16)

    at = nc.alloc_sbuf_tensor_at
    adr = lambda t: nc.lookup_mloc(t).addr
    hT = at("hT", [128, NT * TOK], BF16, offset=adr(xbf))
    x1b = at("x1b", [128, NT * TOK], BF16, offset=adr(xbf) + 65536)
    bcat_s = at("bcat_s", [32, D], BF16, offset=adr(rbc))
    m2b = at("m2b", [128, TOK], BF16, offset=adr(qc))
    r2bc = at("r2bc", [128, TOK], BF16, offset=adr(qc) + 2048)
    s2row = at("s2row", [1, TOK], BF16, offset=adr(qc) + 4096)
    zw = at("zw", [32, TOK], BF16, offset=adr(qc) + 6144)
    zr = at("zr", [36, T2], F32, offset=adr(pt))
    exps = at("exps", [36, T2], BF16, offset=adr(pt) + 2048)
    loraT = at("loraT", [128, NT * 36], BF16, offset=adr(pt) + 3072)

    pb = [nc.alloc_psum_tensor(f"pb{i}", [128, 512], F32) for i in range(7)]
    pbt = nc.alloc_psum_tensor("pbt", [128, 1024], BF16)

    bd = Build(nc)
    for s in ("sg", "ss", "ct", "cv", "cs", "cg", "cc", "cc2", "cc3",
              "sa", "sa2", "sa3"):
        bd.add_sem(s)
    W = bd.wait

    def g_dma(dst, src, fence=True):
        v = bd.op("gpsimd", lambda E, d=dst, s=src: E.dma_start(out=d, in_=s),
                  inc="sg", dv=16)
        if fence:
            W("gpsimd", "sg", v)
        return v

    def s_dma(dst, src, fence=True):
        v = bd.op("sync", lambda E, d=dst, s=src: E.dma_start(out=d, in_=s),
                  inc="ss", dv=16)
        if fence:
            W("sync", "ss", v)
        return v

    def a_dma(dst, src, fence=True):
        v = bd.op("scalar", lambda E, d=dst, s=src: E.dma_start(out=d, in_=s),
                  inc="sa", dv=16)
        if fence:
            W("scalar", "sa", v)
        return v

    def t_mm(o, l, r, st, sp, inc=None):
        return bd.op("tensor", lambda E, o=o, l=l, r=r, st=st, sp=sp:
                     E.matmul(o, l, r, start=st, stop=sp), inc=inc)

    def t_tr(o, i, inc=None):
        return bd.op("tensor", lambda E, o=o, i=i: E.transpose(o, i, idm_s[:]),
                     inc=inc)

    def v_tt(o, a, b2, alu, inc=None):
        return bd.op("vector", lambda E, o=o, a=a, b=b2, al=alu:
                     E.tensor_tensor(o, a, b, al), inc=inc)

    def v_copy(o, i, inc=None):
        return bd.op("vector", lambda E, o=o, i=i: E.tensor_copy(o, i), inc=inc)

    def v_tsc(o, i, c, alu, inc=None):
        return bd.op("vector", lambda E, o=o, i=i, c=c, al=alu:
                     E.tensor_single_scalar(o, i, c, al), inc=inc)

    def v_recip(o, i, inc=None):
        def f(E, o=o, i=i):
            with nc.allow_low_precision("bf16 softmax-normaliser reciprocal"):
                return E.reciprocal(o, i)
        return bd.op("vector", f, inc=inc)

    def s_act(o, i, f, bias=0.0, scale=1.0, inc=None):
        return bd.op("scalar", lambda E, o=o, i=i, f=f, b=bias, sc=scale:
                     E.activation(o, i, f, bias=b, scale=sc), inc=inc)

    xtile = lambda t: xbf[:, L * t:L * (t + 1)]
    htile = lambda t: hT[:, TOK * t:TOK * (t + 1)]
    x1tl = lambda t: x1b[:, TOK * t:TOK * (t + 1)]

    CN, CEPS, CE12, C2, CID = (cst[:, i:i + 1] for i in range(5))

    ss_cm = s_dma(cmask_s[:], cmsk[:])
    ss_cm01 = s_dma(cmask01_s[:], cmsk01[:])
    ss_id = s_dma(idm_s[:], idm[:])
    ss_bq = s_dma(bqkv_s[:], bqkv_c[:])
    ss_bo = s_dma(bo_s[:], bo_c[:])
    ss_bb = s_dma(bb_s[:], bb_c[:])
    ss_br = s_dma(brpr_s[:], brpr[:])
    ss_rep = s_dma(rep4_s[:], rep4[:])
    ss_pk = s_dma(pk_s[:], pk[:])
    for i, v in enumerate((-25.0, 1e-5, 1e-12, 2.0, 1.0 / D)):
        bd.op("gpsimd", lambda E, i=i, v=v: E.memset(cst[:, i:i + 1], v), inc="cg")
    bd.op("gpsimd", lambda E: E.memset(onec[:], 1.0), inc="cg")
    bd.op("gpsimd", lambda E: E.memset(oner[:], 1.0), inc="cg")
    cg_ones = bd.op("gpsimd", lambda E: E.memset(ones128[:], 1.0), inc="cg")
    W("vector", "ss", ss_br)
    W("vector", "cg", cg_ones)
    cv_ab0 = bd.op("vector", lambda E: E.tensor_scalar_add(
        abias_s[32:36, 0:1], brpr_s[32:36, 1:2], cst[32:36, 2:3]), inc="cv")
    W("scalar", "cv", cv_ab0)
    cs_ab = s_act(abias_s[32:36, 0:1], abias_s[32:36, 0:1], AF.Ln, inc="cs")
    W("vector", "cs", cs_ab)
    cv_ab = v_tt(abias_s[32:36, 0:1], abias_s[32:36, 0:1], brpr_s[32:36, 0:1],
                 ALU.add, inc="cv")

    sg_x = []
    for t in range(NT):
        v = g_dma(xtile(t), xt[:, t, :], fence=(t % 4 == 3))
        sg_x.append(v)
    for t in range(NT):
        sg_x[t] = sg_x[min(NT - 1, (t // 4) * 4 + 3)]

    slot_free = [0, 0, 0]

    def wslot(i):
        return wr[:, NT * 128 * (i % 3):NT * 128 * ((i % 3) + 1)]

    sg_w = {}

    def load_ring(i, src_ap):
        W("gpsimd", "ct", slot_free[i % 3])
        sg_w[i] = g_dma(wslot(i), src_ap)

    for p in range(3):
        load_ring(p, wqkv[0, p, :, :])

    ct_sq_hist = [0, 0]
    for t in range(NT):
        W("tensor", "sg", sg_x[t])
        if t == 0:
            W("tensor", "cg", cg_ones)
        for c in range(4):
            t_mm(pb[c][:, :], ones128[:, :],
                 xtile(t)[:, 512 * c:512 * (c + 1)], t == 0, t == NT - 1)
        sl_ = t % 2
        xo_ = xsqt[:, 2048 * sl_:2048 * (sl_ + 1)]
        if sl_ == 0:
            W("vector", "sg", sg_x[t])
            W("vector", "ct", ct_sq_hist[0])
            sq_v = v_tt(xo_, xtile(t)[:, :], xtile(t)[:, :], ALU.mult,
                        inc="cv")
            W("tensor", "cv", sq_v)
        else:
            W("scalar", "sg", sg_x[t])
            W("scalar", "ct", ct_sq_hist[1])
            sq_s = s_act(xo_, xtile(t)[:, :], AF.Square, inc="cs")
            W("tensor", "cs", sq_s)
        for c in range(4):
            rp = 32 * (c % 2)
            ct_sq = t_mm(pb[5 + c // 2][rp:rp + 1, :], onec[:, :],
                         xo_[:, 512 * c:512 * (c + 1)], t == 0, t == NT - 1,
                         inc=("ct" if c == 3 else None))
        ct_sq_hist[sl_] = ct_sq

    ct_stream = bd.cnt["ct"]
    cv_center_l = [None] * 4
    cs_rr_prev = [0]
    ct_bc_prev = [0]
    p0_state = {"pb4": 0}

    def phase0_chunk(c, bg):
        cs_ = 512 * c
        W("vector", "ct", ct_stream)
        W("vector", "cs", cs_rr_prev[0])
        cv_m = bd.op("vector", lambda E, c=c: E.tensor_scalar_mul(
            outst[:, 0:512], pb[c][:, :], cst[:, 4:5]), inc="cv")
        bg[c] = ("cv", cv_m)
        cv_mb = v_copy(m2b[:, 0:512], outst[:, 0:512], inc="cv")
        rp = 32 * (c % 2)
        cv_e2 = bd.op("vector", lambda E, c=c, rp=rp: E.tensor_scalar_mul(
            outst[0:1, 512:1024], pb[5 + c // 2][rp:rp + 1, :],
            cst[0:1, 4:5]), inc="cv")
        bg[5 + c // 2] = ("cv", cv_e2)
        v_tt(prow[0:1, :], outst[0:1, 0:512], outst[0:1, 0:512], ALU.mult)
        cv_var = v_tt(outst[0:1, 512:1024], outst[0:1, 512:1024], prow[0:1, :],
                      ALU.subtract, inc="cv")
        W("scalar", "cv", cv_var)
        W("scalar", "ct", ct_bc_prev[0])
        s_act(prow[0:1, :], outst[0:1, 512:1024], AF.Ln, bias=cst[0:1, 1:2])
        cs_rr = s_act(r0w[0:1, :], prow[0:1, :], AF.Exp, scale=-0.5, inc="cs")
        cs_rr_prev[0] = cs_rr
        W("tensor", "cs", cs_rr)
        W("tensor", "cv", p0_state["pb4"])
        ct_bc = t_mm(pb[4][:, :], oner[:, :], r0w[0:1, :], True, True,
                     inc="ct")
        ct_bc_prev[0] = ct_bc
        W("vector", "ct", ct_bc)
        cv_rbc = v_copy(rbc[:, cs_:cs_ + 512], pb[4][:, :], inc="cv")
        p0_state["pb4"] = cv_rbc
        bg[4] = ("cv", cv_rbc)
        W("vector", "cv", cv_mb)
        W("vector", "cs", cs_rr)
        lst = []
        for t in range(NT):
            lst.append(v_tt(xtile(t)[:, cs_:cs_ + 512],
                            xtile(t)[:, cs_:cs_ + 512], m2b[:, 0:512],
                            ALU.subtract, inc="cv"))
        cv_center_l[c] = lst


    bg = {0: ("cv", 0), 1: ("cv", 0), 2: ("cv", 0), 3: ("cv", 0),
          4: ("cv", 0), 5: ("cv", 0), 6: ("cv", 0)}
    rb_free = [0, 0]
    cv_vtm_done = [0]
    vtmp_cv = [0, 0]
    pbt_free = 0
    ho_free = [0, 0, 0, 0]
    vtmp_free = [0, 0]
    gidx = 0
    ss_ho = {}
    ct_xdead = 0
    first_diag = [True]
    first_tr = [True]

    def Wg(eng, gate):
        W(eng, gate[0], gate[1])

    def proj_group(ridx, c, dst, bias_col, tile_gates=None):
        nonlocal gidx
        bk = gidx % 2
        bank = pb[bk]
        Wg("tensor", bg[bk])
        if ridx in sg_w:
            W("tensor", "sg", sg_w.pop(ridx))
        for t in range(NT):
            if tile_gates is not None:
                W("tensor", "cv", tile_gates[t])
            ct_l = t_mm(bank[:, :], wslot(ridx)[:, 128 * t:128 * (t + 1)],
                        xtile(t)[:, 512 * c:512 * (c + 1)], t == 0, t == NT - 1,
                        inc=("ct" if t == NT - 1 else None))
        sl = gidx % 2
        W("vector", "ct", ct_l)
        W("vector", "cs", vtmp_free[sl])
        cv_e = v_tt(vtmp[:, 512 * sl:512 * (sl + 1)], bank[:, :],
                    rbc[:, 512 * c:512 * (c + 1)], ALU.mult, inc="cv")
        bg[bk] = ("cv", cv_e)
        vtmp_cv[sl] = cv_e
        W("scalar", "cv", cv_e)
        W("scalar", "ss", ss_bq)
        cs_e = bd.op("scalar", lambda E, o=dst, i=vtmp[:, 512 * sl:512 * (sl + 1)],
                     b=bias_col: E.activation(o, i, AF.Identity, bias=b, scale=1.0),
                     inc="cs")
        vtmp_free[sl] = cs_e
        gidx += 1
        return ct_l, cs_e

    def attn_pair(h, p):
        cA, cB = 2 * p, 2 * p + 1
        cfg = {"A": dict(c=cA, sbk=2, obk=4, mbk=5, ps=0),
               "B": dict(c=cB, sbk=3, obk=6, mbk=1, ps=1)}
        for st in cfg.values():
            st["nkt"] = 4 * st["c"] + 4
            st["qcs"] = qc[:, 512 * (st["c"] % 2):512 * (st["c"] % 2 + 1)]
            st["ct_av"] = 0
            st["cs_prev"] = {2: 0, 3: 0}
        W("tensor", "cv", cv_vtm_done[0])
        W("tensor", "cs", max(cs_q[cA], cs_q[cB]))
        seq = []
        for i in range(cfg["B"]["nkt"]):
            if i < cfg["A"]["nkt"]:
                seq.append(("A", i))
            seq.append(("B", i))
        def emit_S(X, kt):
            st = cfg[X]
            c = st["c"]
            diag = kt >= 4 * c
            j0 = 128 * (kt - 4 * c) if diag else 0
            sbk = st["sbk"]
            if X == "B" and kt >= cfg["A"]["nkt"]:
                sbk = 2 + (kt - cfg["A"]["nkt"]) % 2
            ps = 2 * st["ps"] + (kt % 2)
            Wg("tensor", bg[sbk])
            W("tensor", "cs", st["cs_prev"][sbk])
            ct_s = t_mm(pb[sbk][:, j0:512], kb[:, 128 * kt:128 * (kt + 1)],
                        st["qcs"][:, j0:512], True, True, inc="ct")
            W("scalar", "ct", ct_s)
            cs_e = bd.op("scalar", lambda E,
                         o=pt[:, 512 * ps + j0:512 * (ps + 1)],
                         i=pb[sbk][:, j0:512]:
                         E.activation(o, i, AF.Exp, bias=cst[:, 0:1], scale=SC),
                         inc="cs")
            st["cs_prev"][sbk] = cs_e
            cv_msk = 0
            if diag:
                W("vector", "cs", cs_e)
                if first_diag[0]:
                    W("vector", "ss", ss_cm01)
                    first_diag[0] = False
                cv_msk = v_tt(pt[:, 512 * ps + j0:512 * ps + j0 + 128],
                              pt[:, 512 * ps + j0:512 * ps + j0 + 128],
                              cmask01_s[:], ALU.mult, inc="cv")
            return (X, kt, j0, cs_e, cv_msk, sbk)

        def emit_AV(pend):
            X, kt, j0, cs_e, cv_msk, sbk = pend
            st = cfg[X]
            c = st["c"]
            obk, mbk = st["obk"], st["mbk"]
            ps = 2 * st["ps"] + (kt % 2)
            if cv_msk:
                W("tensor", "cv", cv_msk)
            W("tensor", "cs", cs_e)
            if kt == 0:
                Wg("tensor", bg[obk])
                Wg("tensor", bg[mbk])
            t_mm(pb[obk][:, j0:512], vtm[:, 128 * kt:128 * (kt + 1)],
                 pt[:, 512 * ps + j0:512 * (ps + 1)], kt == 0, kt == st["nkt"] - 1)
            ct_av = t_mm(pb[mbk][:, j0:512], ones128[:, :],
                         pt[:, 512 * ps + j0:512 * (ps + 1)], kt == 0,
                         kt == st["nkt"] - 1, inc="ct")
            bg[sbk] = ("ct", ct_av)
            if kt == st["nkt"] - 1:
                es = st["ps"]
                W("scalar", "ct", ct_av)
                W("scalar", "cv", vtmp_cv[es])
                cs_ln = s_act(vtmp[:, 512 * es:512 * (es + 1)], pb[mbk][:, 0:512],
                              AF.Ln, inc="cs")
                bg[mbk] = ("cs", cs_ln)
                W("scalar", "cv", rb_free[es])
                cs_rb = s_act(nrmb[:, 512 * es:512 * (es + 1)],
                              vtmp[:, 512 * es:512 * (es + 1)], AF.Exp,
                              scale=-1.0, inc="cs")
                vtmp_free[es] = cs_rb
                W("vector", "cs", cs_rb)
                W("vector", "ss", ho_free[c])
                cv_ho = v_tt(ho[:, 512 * c:512 * (c + 1)], pb[obk][:, 0:512],
                             nrmb[:, 512 * es:512 * (es + 1)], ALU.mult, inc="cv")
                bg[obk] = ("cv", cv_ho)
                rb_free[es] = cv_ho
                W("sync", "cv", cv_ho)
                ss_ho[(h, c)] = s_dma(
                    buf2[c // 2, :, h, 512 * (c % 2):512 * (c % 2 + 1)],
                    ho[:, 512 * c:512 * (c + 1)])
                ho_free[c] = ss_ho[(h, c)]

        pend = None
        for item in seq:
            cur = emit_S(*item)
            if pend is not None:
                emit_AV(pend)
            pend = cur
        emit_AV(pend)

    dynv = {}
    for h in range(NH):
        for c in range(4):
            if h == 0:
                phase0_chunk(c, bg)
            ct_l, _ = proj_group(3 * h, c, vfm[:, 512 * c:512 * (c + 1)],
                                 bqkv_s[:, 3 * h:3 * h + 1],
                                 tile_gates=(cv_center_l[c] if h == 0
                                             else None))
        slot_free[(3 * h) % 3] = ct_l
        if h + 1 < NH:
            load_ring(3 * (h + 1), wqkv[h + 1, 0, :, :])
        cs_v_done = bd.cnt["cs"]

        cv_tr = 0
        for c in range(4):
            ct_l, _ = proj_group(3 * h + 1, c, kb[:, 512 * c:512 * (c + 1)],
                                 bqkv_s[:, 3 * h + 1:3 * h + 2])
            if c < 2:
                W("tensor", "cs", cs_v_done)
                W("tensor", "cv", pbt_free)
                if first_tr[0]:
                    W("tensor", "ss", ss_id)
                    first_tr[0] = False
                for k in range(8):
                    kk = 8 * c + k
                    ct_tr = t_tr(pbt[:, 128 * k:128 * (k + 1)],
                                 vfm[:, 128 * kk:128 * (kk + 1)],
                                 inc=("ct" if k == 7 else None))
                W("vector", "ct", ct_tr)
                cv_tr = v_copy(vtm[:, 1024 * c:1024 * (c + 1)], pbt[:, 0:1024],
                               inc="cv")
                pbt_free = cv_tr
        slot_free[(3 * h + 1) % 3] = ct_l
        if h + 1 < NH:
            load_ring(3 * (h + 1) + 1, wqkv[h + 1, 1, :, :])
        cv_vtm_done[0] = cv_tr

        cs_q = {}
        for c in range(4):
            if c == 2:
                attn_pair(h, 0)
            ct_q, cs_q[c] = proj_group(3 * h + 2, c,
                                       qc[:, 512 * (c % 2):512 * (c % 2 + 1)],
                                       bqkv_s[:, 3 * h + 2:3 * h + 3])
            if c == 3:
                slot_free[(3 * h + 2) % 3] = ct_q
                if h + 1 < NH:
                    load_ring(3 * (h + 1) + 2, wqkv[h + 1, 2, :, :])
                if h == NH - 1:
                    ct_xdead = ct_q
        if h == NH - 1:
            W("scalar", "ct", ct_xdead)
            W("scalar", "ss", max(ss_ho[(hh, cc_)] for hh in range(15)
                                  for cc_ in range(4)))
            sa_own = bd.op("scalar", lambda E: E.dma_start(
                out=hT[:, 0:15 * TOK].rearrange("p (t c) -> p t c", t=15),
                in_=buf2[bass.ds(dynv["mb_ac"], 1), :, 0:15, :].squeeze(0)),
                inc="sa", dv=16)
            W("scalar", "cc", 1)
            sa_p1 = bd.op("scalar", lambda E: E.dma_start(
                out=hT[:, 16 * TOK:24 * TOK].rearrange("p (t c) -> p t c", t=8),
                in_=agout1[bass.ds(dynv["pb_ac"], 1), :, :, :].squeeze(0)),
                inc="sa2", dv=16)
            W("scalar", "cc2", 1)
            sa_p2 = bd.op("scalar", lambda E: E.dma_start(
                out=hT[:, 24 * TOK:30 * TOK].rearrange("p (t c) -> p t c", t=6),
                in_=agout2[bass.ds(dynv["pb_ac"], 1), :, :, :].squeeze(0)),
                inc="sa3", dv=16)
            for m_ in range(3):
                load_ring(48 + m_, wo[m_, :, :])
        attn_pair(h, 1)

        if h == 7:
            W("gpsimd", "ss", max(ss_ho[(hh, c)] for hh in range(8)
                                  for c in range(4)))
            W("gpsimd", "ss", ss_pk)

            def _dyn(E):
                _, v1 = nc.values_load_multi_w_load_instructions(
                    pk_s[0:1, 0:1], engines=[Pool], min_val=0, max_val=1,
                    skip_runtime_bounds_check=True)
                dynv["pb"] = v1[0]
            bd.q["gpsimd"].append(_dyn)
            sg_a1 = bd.op("gpsimd", lambda E: E.dma_start(
                out=agin1[:],
                in_=buf2[bass.ds(dynv["pb"], 1), :, 0:8, :].squeeze(0)),
                inc="sg", dv=16)
            W("gpsimd", "sg", sg_a1)
            bd.op("gpsimd", lambda E: E.collective_compute(
                "AllGather", ALU.bypass, replica_groups=RG,
                ins=[agin1[:]], outs=[agout1[:]]), inc="cc")

            W("sync", "ss", ss_pk)

            def _dyn_sp(E):
                _, v1 = nc.values_load_multi_w_load_instructions(
                    pk_s[0:1, 0:1], engines=[SPE], min_val=0, max_val=1,
                    skip_runtime_bounds_check=True)
                dynv["pb_sp"] = v1[0]
                _, v2 = nc.values_load_multi_w_load_instructions(
                    pk_s[0:1, 1:2], engines=[SPE], min_val=0, max_val=1,
                    skip_runtime_bounds_check=True)
                dynv["mb_sp"] = v2[0]
            bd.q["sync"].append(_dyn_sp)
            W("scalar", "ss", ss_pk)

            def _dyn_ac(E):
                _, v1 = nc.values_load_multi_w_load_instructions(
                    pk_s[0:1, 0:1], engines=[ACTE], min_val=0, max_val=1,
                    skip_runtime_bounds_check=True)
                dynv["pb_ac"] = v1[0]
                _, v2 = nc.values_load_multi_w_load_instructions(
                    pk_s[0:1, 1:2], engines=[ACTE], min_val=0, max_val=1,
                    skip_runtime_bounds_check=True)
                dynv["mb_ac"] = v2[0]
            bd.q["scalar"].append(_dyn_ac)

        if h == 13:
            W("sync", "ss", max(ss_ho[(hh, c)] for hh in range(8, 14)
                                for c in range(4)))
            ss_a2 = bd.op("sync", lambda E: E.dma_start(
                out=agin2[:],
                in_=buf2[bass.ds(dynv["pb_sp"], 1), :, 8:14, :].squeeze(0)),
                inc="ss", dv=16)
            W("sync", "ss", ss_a2)
            W("gpsimd", "ss", ss_a2)
            bd.op("gpsimd", lambda E: E.collective_compute(
                "AllGather", ALU.bypass, replica_groups=RG,
                ins=[agin2[:]], outs=[agout2[:]]), inc="cc2")


    ct_A_end = bd.cnt["ct"]
    cv_A_end = bd.cnt["cv"]
    W("sync", "ss", max(ss_ho[(15, c)] for c in range(4)))
    ss_a3 = bd.op("sync", lambda E: E.dma_start(
        out=agin3[:],
        in_=buf2[bass.ds(dynv["pb_sp"], 1), :, 14:16, :].squeeze(0)),
        inc="ss", dv=16)
    W("sync", "ss", ss_a3)
    W("gpsimd", "ss", ss_a3)
    bd.op("gpsimd", lambda E: E.collective_compute(
        "AllGather", ALU.bypass, replica_groups=RG,
        ins=[agin3[:]], outs=[agout3[:]]), inc="cc3")
    ss_own15 = bd.op("sync", lambda E: E.dma_start(
        out=hT[:, 15 * TOK:16 * TOK],
        in_=buf2[bass.ds(dynv["mb_sp"], 1), :, 15:16, :]
        .squeeze(0).squeeze(1)), inc="ss", dv=16)
    W("sync", "ss", ss_own15)
    W("sync", "cc3", 1)
    ss_p3 = bd.op("sync", lambda E: E.dma_start(
        out=hT[:, 30 * TOK:32 * TOK].rearrange("p (t c) -> p t c", t=2),
        in_=agout3[bass.ds(dynv["pb_sp"], 1), :, :, :].squeeze(0)),
        inc="ss", dv=16)
    W("sync", "ss", ss_p3)
    ss_A_end = bd.cnt["ss"]
    W("gpsimd", "ct", ct_A_end)
    W("gpsimd", "cv", cv_A_end)
    W("gpsimd", "ss", ss_A_end)
    sg_lora = g_dma(loraT[:], lora[:])
    sg_bcat = g_dma(bcat_s[:], bcat[:])

    W("tensor", "ct", ct_A_end)
    W("tensor", "cv", cv_A_end)
    W("tensor", "cs", bd.cnt["cs"])
    ob_free = [0, 0, 0]
    xst_free = 0
    outst_free = [0, 0]
    gi = 0
    wb_ct_gate = {}

    for m in range(NT):
        W("sync", "cv", xst_free)
        ss_x = s_dma(xst[:], xr[:, m, :])
        for c in range(2):
            bank = pb[gi % 3]
            W("tensor", "cs", ob_free[gi % 3])
            if m == 0 and c == 0:
                W("tensor", "sa", sa_own)
                W("tensor", "sg", sg_w.pop(48))
            elif (48 + m) in sg_w:
                W("tensor", "sg", sg_w.pop(48 + m))
            for t in range(NT):
                if m == 0 and c == 0 and t == 15:
                    W("tensor", "ss", ss_own15)
                if m == 0 and c == 0 and t == 16:
                    W("tensor", "sa2", sa_p1)
                if m == 0 and c == 0 and t == 24:
                    W("tensor", "sa3", sa_p2)
                if m == 0 and c == 0 and t == 30:
                    W("tensor", "ss", ss_p3)
                ct_l = t_mm(bank[:, 0:T2], wslot(48 + m)[:, 128 * t:128 * (t + 1)],
                            htile(t)[:, T2 * c:T2 * (c + 1)], t == 0, t == NT - 1,
                            inc=("ct" if t == NT - 1 else None))
            sl = gi % 2
            W("vector", "ct", ct_l)
            W("vector", "ss", ss_x)
            W("vector", "cs", vtmp_free[sl])
            cv_a = v_tt(vtmp[:, 512 * sl:512 * (sl + 1)], bank[:, 0:T2],
                        xst[:, T2 * c:T2 * (c + 1)], ALU.add, inc="cv")
            if c == 1:
                xst_free = cv_a
            W("scalar", "cv", cv_a)
            W("scalar", "ss", ss_bo)
            cs_x1 = bd.op("scalar", lambda E,
                          o=x1tl(m)[:, T2 * c:T2 * (c + 1)],
                          i=vtmp[:, 512 * sl:512 * (sl + 1)], b=bo_s[:, m:m + 1]:
                          E.activation(o, i, AF.Identity, bias=b, scale=1.0), inc="cs")
            ob_free[gi % 3] = cs_x1
            vtmp_free[sl] = cs_x1
            gi += 1
        slot_free[(48 + m) % 3] = ct_l
        nxt = 48 + m + 3
        if nxt < 80:
            load_ring(nxt, wo[nxt - 48, :, :])
        else:
            load_ring(nxt, wb[nxt - 80, :, :])

    cs_stage1 = bd.cnt["cs"]

    pb3_gate = bg[3]
    pb4_gate = bg[4]
    pb5_gate = bg[5]
    pb6_gate = bg[6]
    rb2_free = max(rb_free)
    cs2_prev = [0]
    cs_ex_prev = 0
    cv_zw = {}
    for c in range(2):
        c0 = T2 * c
        W("tensor", "cs", cs_stage1)
        W("vector", "cs", cs_stage1)
        ct_st = 0
        hist = [0, 0]
        for t in range(NT):
            sl_ = t % 2
            xsl = x1tl(t)[:, c0:c0 + T2]
            xo_ = xsqt[:, 512 * sl_:512 * sl_ + T2]
            if sl_ == 0:
                W("vector", "ct", hist[0])
                sq_v = v_tt(xo_, xsl, xsl, ALU.mult, inc="cv")
            else:
                W("scalar", "ct", hist[1])
                sq_s = s_act(xo_, xsl, AF.Square, inc="cs")
            if t == 0:
                W("tensor", pb3_gate[0], pb3_gate[1])
            t_mm(pb[3][:, 0:T2], ones128[:, :], xsl, t == 0, t == NT - 1)
            if sl_ == 0:
                W("tensor", "cv", sq_v)
            else:
                W("tensor", "cs", sq_s)
            if t == 0:
                W("tensor", pb4_gate[0], pb4_gate[1])
            ct_st = t_mm(pb[4][:, 0:T2], ones128[:, :], xo_, t == 0, t == NT - 1,
                         inc="ct")
            hist[sl_] = ct_st
        W("vector", "ct", ct_st)
        cv_m = bd.op("vector", lambda E: E.tensor_scalar_mul(
            outst[:, 0:T2], pb[3][:, 0:T2], cst[:, 4:5]), inc="cv")
        pb3_gate = ("cv", cv_m)
        cv_mb = v_copy(m2b[:, c0:c0 + T2], outst[:, 0:T2], inc="cv")
        v_tt(vtmp[:, 0:T2], outst[:, 0:T2], outst[:, 0:T2], ALU.mult)
        W("vector", "cs", cs2_prev[0])
        cv_e2 = bd.op("vector", lambda E: E.tensor_scalar_mul(
            outst[:, 512:512 + T2], pb[4][:, 0:T2], cst[:, 4:5]), inc="cv")
        pb4_gate = ("cv", cv_e2)
        cv_var = v_tt(outst[:, 512:512 + T2], outst[:, 512:512 + T2],
                      vtmp[:, 0:T2], ALU.subtract, inc="cv")
        W("scalar", "cv", cv_var)
        s_act(vtmp[:, 512:512 + T2], outst[:, 512:512 + T2], AF.Ln,
              bias=cst[:, 1:2])
        W("scalar", "cv", rb2_free)
        s_act(r2bc[:, c0:c0 + T2], vtmp[:, 512:512 + T2], AF.Exp, scale=-0.5)
        cs_s2 = s_act(nrmb[:, 0:T2], vtmp[:, 512:512 + T2], AF.Exp, scale=0.5,
                      inc="cs")
        cs2_prev[0] = cs_s2
        W("vector", "cv", cv_mb)
        W("vector", "cs", cs_s2)
        cv_c2l = []
        for t in range(NT):
            cv_c2l.append(v_tt(x1tl(t)[:, c0:c0 + T2], x1tl(t)[:, c0:c0 + T2],
                               m2b[:, c0:c0 + T2], ALU.subtract, inc="cv"))
        W("tensor", pb5_gate[0], pb5_gate[1])
        if c == 0:
            W("tensor", "sg", sg_lora)
        for t in range(NT):
            W("tensor", "cv", cv_c2l[t])
            ct_z = t_mm(pb[5][0:36, 0:T2], loraT[:, 36 * t:36 * (t + 1)],
                        x1tl(t)[:, c0:c0 + T2], t == 0, t == NT - 1,
                        inc=("ct" if t == NT - 1 else None))
        W("vector", "ct", ct_z)
        W("vector", "cs", cs_ex_prev)
        cv_zr = v_tt(zr[:, :], pb[5][0:36, 0:T2], r2bc[0:36, c0:c0 + T2],
                     ALU.mult, inc="cv")
        pb5_gate = ("cv", cv_zr)
        W("scalar", "cv", cv_zr)
        W("scalar", "cv", cv_ab)
        cs_ex = bd.op("scalar", lambda E: E.activation(
            exps[32:36, :], zr[32:36, :], AF.Exp, bias=abias_s[32:36, 0:1],
            scale=1.0), inc="cs")
        cs_ex_prev = cs_ex
        W("tensor", "cs", cs_ex)
        W("tensor", pb6_gate[0], pb6_gate[1])
        ct_se = t_mm(pb[6][:, 0:T2], ones128[32:36, :], exps[32:36, :], True,
                     True, inc="ct")
        W("scalar", "ct", ct_se)
        cs_li = s_act(vtmp[:, 0:T2], pb[6][:, 0:T2], AF.Ln, inc="cs")
        pb6_gate = ("cs", cs_li)
        cs_inv = s_act(nrmb[:, 512:512 + T2], vtmp[:, 0:T2], AF.Exp, scale=-1.0,
                       inc="cs")
        W("vector", "cs", cs_inv)
        v_tt(nrmb[:, 512:512 + T2], nrmb[:, 0:T2], nrmb[:, 512:512 + T2],
             ALU.mult)
        cv_q2 = bd.op("vector", lambda E: E.tensor_scalar_mul(
            nrmb[:, 512:512 + T2], nrmb[:, 512:512 + T2], cst[:, 3:4]), inc="cv")
        W("tensor", pb4_gate[0], pb4_gate[1])
        W("tensor", "ss", ss_rep)
        ct_ar = t_mm(pb[4][0:32, 0:T2], rep4_s[32:36, :], exps[32:36, :], True,
                     True, inc="ct")
        W("vector", "ct", ct_ar)
        cv_asc = v_tt(exps[0:32, :], pb[4][0:32, 0:T2],
                      nrmb[0:32, 512:512 + T2], ALU.mult, inc="cv")
        pb4_gate = ("cv", cv_asc)
        cv_zw[c] = v_tt(zw[:, c0:c0 + T2], zr[0:32, :], exps[0:32, :], ALU.mult,
                        inc="cv")
        rb2_free = cv_asc

    for m in range(NT):
        for c in range(2):
            c0 = T2 * c
            bank = pb[gi % 3]
            W("tensor", "cs", ob_free[gi % 3])
            if (80 + m) in sg_w:
                W("tensor", "sg", sg_w.pop(80 + m))
            if m == 0 and c == 0:
                W("tensor", "sg", sg_bcat)
            for t in range(NT):
                t_mm(bank[:, 0:T2], wslot(80 + m)[:, 128 * t:128 * (t + 1)],
                     x1tl(t)[:, c0:c0 + T2], t == 0, False)
            W("tensor", "cv", cv_zw[c])
            ct_l = t_mm(bank[:, 0:T2], bcat_s[:, 128 * m:128 * (m + 1)],
                        zw[:, c0:c0 + T2], False, True, inc="ct")
            sl = gi % 2
            W("vector", "ct", ct_l)
            W("vector", "cs", vtmp_free[0])
            cv_y1 = v_tt(vtmp[:, 0:512], bank[:, 0:T2], r2bc[:, c0:c0 + T2],
                         ALU.mult, inc="cv")
            W("scalar", "cv", cv_y1)
            W("scalar", "ss", ss_bb)
            cs_y2 = bd.op("scalar", lambda E, b=bb_s[:, m:m + 1]: E.activation(
                vtmp[:, 512:1024], vtmp[:, 0:512], AF.Identity, bias=b, scale=1.0),
                inc="cs")
            ob_free[gi % 3] = cs_y2
            vtmp_free[0] = cs_y2
            W("vector", "cs", cs_y2)
            W("vector", "ss", outst_free[sl])
            cv_x2a = v_tt(outst[:, 512 * sl:512 * (sl + 1)], vtmp[:, 512:1024],
                          x1tl(m)[:, c0:c0 + T2], ALU.add, inc="cv")
            cv_x2 = v_tt(outst[:, 512 * sl:512 * (sl + 1)],
                         outst[:, 512 * sl:512 * (sl + 1)], m2b[:, c0:c0 + T2],
                         ALU.add, inc="cv")
            W("sync", "cv", cv_x2)
            ss_o = s_dma(out[:, m, c0:c0 + T2], outst[:, 512 * sl:512 * (sl + 1)])
            outst_free[sl] = ss_o
            gi += 1
        nxt = 80 + m + 3
        if nxt < 112:
            slot_free[(80 + m) % 3] = ct_l
            load_ring(nxt, wb[nxt - 80, :, :])

    bd.wait("sync", "ss", bd.cnt["ss"])

    with nc.Block() as block:
        for name, dec in (("gpsimd", block.gpsimd), ("sync", block.sync),
                          ("tensor", block.tensor), ("vector", block.vector),
                          ("scalar", block.scalar)):
            def make(fns):
                def emit(E):
                    for f in fns:
                        f(E)
                return emit
            dec(make(bd.q[name]))

    return nc


_CACHE = {}


def _get_program():
    if "nc" not in _CACHE:
        _CACHE["nc"] = build_program()
    return _CACHE["nc"]


def kernel(**inputs):
    x = np.ascontiguousarray(np.asarray(inputs["L0_attn_x"], dtype=np.float32))
    Wq = np.asarray(inputs["L0_attn_Wq"], dtype=np.float32)
    bq = np.asarray(inputs["L0_attn_bq"], dtype=np.float32)
    Wk = np.asarray(inputs["L0_attn_Wk"], dtype=np.float32)
    bk = np.asarray(inputs["L0_attn_bk"], dtype=np.float32)
    Wv = np.asarray(inputs["L0_attn_Wv"], dtype=np.float32)
    bv = np.asarray(inputs["L0_attn_bv"], dtype=np.float32)
    Wo = np.asarray(inputs["L0_attn_Wo"], dtype=np.float32)
    bo = np.asarray(inputs["L0_attn_bo"], dtype=np.float32)
    Wb = np.asarray(inputs["L0_ffn_W_base"], dtype=np.float32)
    bb = np.asarray(inputs["L0_ffn_b_base"], dtype=np.float32)
    active = np.asarray(inputs["L0_ffn_active"]).astype(np.int64)
    A = np.asarray(inputs["L0_ffn_lora_A"], dtype=np.float32)[active]
    Bm = np.asarray(inputs["L0_ffn_lora_B"], dtype=np.float32)[active]
    Wr = np.asarray(inputs["L0_ffn_router_W"], dtype=np.float32)[active]
    br = np.asarray(inputs["L0_ffn_router_b"], dtype=np.float32)[active]
    pr = np.asarray(inputs["L0_ffn_domain_prior"], dtype=np.float32)[active]

    WoT = np.ascontiguousarray(Wo.T)
    WbT = np.ascontiguousarray(Wb.T)
    loraT_full = np.concatenate(
        [A.transpose(2, 0, 1).reshape(D, 32), Wr.T], axis=1)
    bcat_full = np.ascontiguousarray(Bm.transpose(0, 2, 1).reshape(32, D))
    brpr_full = np.zeros((36, 2), np.float32)
    brpr_full[32:36, 0] = br
    brpr_full[32:36, 1] = pr
    rep4_full = np.zeros((36, 32), BF)
    for e in range(4):
        rep4_full[32 + e, 8 * e:8 * e + 8] = 1
    i_ = np.arange(128)
    cm = np.where(i_[None, :] >= i_[:, None], 0.0, -1e9).astype(np.float32)
    cm01 = np.where(i_[None, :] >= i_[:, None], 1.0, 0.0).astype(BF)
    idm_full = np.eye(128, dtype=BF)

    def head_slices(h):
        arr = np.empty((NH, 3, 128, NT * 128), np.float32)
        for hp in range(NH):
            for p, Wm in enumerate((Wv, Wk, Wq)):
                blk = Wm[2048 * h + 128 * hp:2048 * h + 128 * (hp + 1), :].T
                arr[hp, p] = _swz(np.ascontiguousarray(blk))
        return arr

    def fo_tiles(M):
        arr = np.empty((NT, 128, NT * 128), np.float32)
        for m in range(NT):
            arr[m] = _swz(np.ascontiguousarray(M[:, 128 * m:128 * (m + 1)]))
        return arr

    wb_tiles = fo_tiles(WbT)
    lora_swz = _swz(loraT_full)

    in_maps = []
    for c in range(8):
        b, h = c // 2, c % 2
        xtb = _swz(np.ascontiguousarray(x[b].T)).reshape(128, NT, L)
        xrb = _swz(np.ascontiguousarray(
            x[b, TOK * h:TOK * (h + 1), :].T)).reshape(128, NT, TOK)
        bcol = np.empty((128, NH * 3), np.float32)
        for hp in range(NH):
            for p, bvec in enumerate((bv, bk, bq)):
                bcol[:, 3 * hp + p] = bvec[2048 * h + 128 * hp:
                                           2048 * h + 128 * (hp + 1)]
        WoTp = WoT if h == 0 else np.ascontiguousarray(
            np.vstack([WoT[2048:], WoT[:2048]]))
        in_maps.append({
            "xt": xtb, "xr": xrb, "wqkv": head_slices(h), "bqkv_c": bcol,
            "wo": fo_tiles(WoTp),
            "bo_c": np.ascontiguousarray(bo.reshape(NT, 128).T),
            "wb": wb_tiles,
            "bb_c": np.ascontiguousarray(bb.reshape(NT, 128).T),
            "lora": lora_swz, "bcat": bcat_full, "brpr": brpr_full,
            "rep4": rep4_full, "cmsk": cm, "cmsk01": cm01,
            "idm": idm_full,
            "pk": np.array([[1 - h, h]], dtype=np.int32),
        })

    nc = _get_program()
    res = run_bass_kernel_spmd(nc, in_maps, core_ids=list(range(8)))
    final = np.empty((4, L, D), np.float32)
    for c in range(8):
        b, h = c // 2, c % 2
        o = np.asarray(res.results[c]["out"])
        final[b, TOK * h:TOK * (h + 1), :] = o.transpose(2, 1, 0).reshape(TOK, D)
    return final


# version 11
# speedup vs baseline: 1.1357x; 1.0259x over previous
import numpy as np
import ml_dtypes

import concourse.bass as bass
import concourse.mybir as mybir
from concourse.bass_utils import run_bass_kernel_spmd

F32 = mybir.dt.float32
BF16 = mybir.dt.bfloat16
I32 = mybir.dt.int32
AF = mybir.ActivationFunctionType
ALU = mybir.AluOpType
BF = ml_dtypes.bfloat16
Pool = mybir.EngineType.Pool
SPE = mybir.EngineType.SP
ACTE = mybir.EngineType.Activation

D = 4096
L = 2048
NT = 32
NH = 16
TOK = 1024
T2 = 512
SC = 0.08838834764831845
ENGS = ("tensor", "vector", "scalar", "sync", "gpsimd")
RG = [[0, 1], [2, 3], [4, 5], [6, 7]]


def _swz(m):
    c = m.shape[1]
    return np.ascontiguousarray(
        m.reshape(NT, 128, c).transpose(1, 0, 2)).reshape(128, NT * c)


class Build:
    def __init__(self, nc):
        self.nc = nc
        self.q = {e: [] for e in ENGS}
        self.sem = {}
        self.cnt = {}
        self.last_wait = {}

    def add_sem(self, name):
        self.sem[name] = self.nc.alloc_semaphore(name=name)
        self.cnt[name] = 0

    def op(self, eng, fn, inc=None, dv=1):
        if inc is None:
            self.q[eng].append(fn)
            return None
        self.cnt[inc] += dv
        s = self.sem[inc]

        def run(E, fn=fn, s=s, dv=dv):
            fn(E).then_inc(s, dv)
        self.q[eng].append(run)
        return self.cnt[inc]

    def wait(self, eng, name, val):
        if val is None or val <= 0:
            return
        k = (eng, name)
        if self.last_wait.get(k, 0) >= val:
            return
        self.last_wait[k] = val
        s = self.sem[name]
        self.q[eng].append(lambda E, s=s, v=val: E.wait_ge(s, v))


def build_program():
    nc = bass.Bass("TRN2", target_bir_lowering=False,
                   detect_race_conditions=False)

    din = lambda n, shp, dt=F32: nc.dram_tensor(n, shp, dt, kind="ExternalInput")
    xt = din("xt", [128, NT, L])
    xr = din("xr", [128, NT, TOK])
    wqkv = din("wqkv", [NH, 3, 128, NT * 128])
    bqkv_c = din("bqkv_c", [128, NH * 3])
    wo = din("wo", [NT, 128, NT * 128])
    bo_c = din("bo_c", [128, NT])
    wb = din("wb", [NT, 128, NT * 128])
    bb_c = din("bb_c", [128, NT])
    lora = din("lora", [128, NT * 36])
    bcat = din("bcat", [32, D])
    brpr = din("brpr", [36, 2])
    rep4 = din("rep4", [36, 32], BF16)
    cmsk = din("cmsk", [128, 128])
    cmsk01 = din("cmsk01", [128, 128], BF16)
    idm = din("idm", [128, 128], BF16)
    pk = din("pk", [1, 2], I32)

    out = nc.dram_tensor("out", [128, NT, TOK], F32, kind="ExternalOutput")
    buf2 = nc.dram_tensor("buf2", [2, 128, NH, TOK], BF16)
    agin1 = nc.dram_tensor("agin1", [128, 8, TOK], BF16)
    agin2 = nc.dram_tensor("agin2", [128, 6, TOK], BF16)
    agin3 = nc.dram_tensor("agin3", [128, 2, TOK], BF16)
    agout1 = nc.dram_tensor("agout1", [2, 128, 8, TOK], BF16)
    agout2 = nc.dram_tensor("agout2", [2, 128, 6, TOK], BF16)
    agout3 = nc.dram_tensor("agout3", [2, 128, 2, TOK], BF16)

    sb = nc.alloc_sbuf_tensor
    xbf = sb("xbf", [128, NT * L], BF16)
    rbc = sb("rbc", [128, L], BF16)
    kb = sb("kb", [128, L], BF16)
    qc = sb("qc", [128, 4 * 512], BF16)
    vfm = sb("vfm", [128, L], BF16)
    vtm = sb("vtm", [128, L], BF16)
    wr = sb("wr", [128, 3 * NT * 128], BF16)
    pt = sb("pt", [128, 4 * 512], BF16)
    ho = sb("ho", [128, L], BF16)
    cmask_s = sb("cmask_s", [128, 128], F32)
    cmask01_s = sb("cmask01_s", [128, 128], BF16)
    idm_s = sb("idm_s", [128, 128], BF16)
    cst = sb("cst", [128, 8], F32)
    onec = sb("onec", [128, 1], BF16)
    oner = sb("oner", [1, 128], BF16)
    bqkv_s = sb("bqkv_s", [128, NH * 3], F32)
    bo_s = sb("bo_s", [128, NT], F32)
    bb_s = sb("bb_s", [128, NT], F32)
    brpr_s = sb("brpr_s", [36, 2], F32)
    abias_s = sb("abias_s", [36, 1], F32)
    rep4_s = sb("rep4_s", [36, 32], BF16)
    pk_s = sb("pk_s", [1, 2], I32)
    ones128 = sb("ones128", [128, 128], BF16)
    vtmp = sb("vtmp", [128, 2 * 512], F32)
    xsqt = sb("xsqt", [128, 2 * 2048], BF16)
    prow = sb("prow", [1, 512], F32)
    r0w = sb("r0w", [1, 512], BF16)
    xst = sb("xst", [128, TOK], F32)
    outst = sb("outst", [128, 2 * 512], F32)
    nrmb = sb("nrmb", [128, 2 * 512], BF16)

    at = nc.alloc_sbuf_tensor_at
    adr = lambda t: nc.lookup_mloc(t).addr
    hT = at("hT", [128, NT * TOK], BF16, offset=adr(xbf))
    x1b = at("x1b", [128, NT * TOK], BF16, offset=adr(xbf) + 65536)
    bcat_s = at("bcat_s", [32, D], BF16, offset=adr(rbc))
    m2b = at("m2b", [128, TOK], BF16, offset=adr(qc))
    r2bc = at("r2bc", [128, TOK], BF16, offset=adr(qc) + 2048)
    s2row = at("s2row", [1, TOK], BF16, offset=adr(qc) + 4096)
    zw = at("zw", [32, TOK], BF16, offset=adr(qc) + 6144)
    zr = at("zr", [36, T2], F32, offset=adr(pt))
    exps = at("exps", [36, T2], BF16, offset=adr(pt) + 2048)
    loraT = at("loraT", [128, NT * 36], BF16, offset=adr(pt) + 3072)

    pb = [nc.alloc_psum_tensor(f"pb{i}", [128, 512], F32) for i in range(7)]
    pbt = nc.alloc_psum_tensor("pbt", [128, 1024], BF16)

    bd = Build(nc)
    for s in ("sg", "ss", "ct", "cv", "cs", "cg", "cc", "cc2", "cc3",
              "sa", "sa2", "sa3"):
        bd.add_sem(s)
    W = bd.wait

    def g_dma(dst, src, fence=True):
        v = bd.op("gpsimd", lambda E, d=dst, s=src: E.dma_start(out=d, in_=s),
                  inc="sg", dv=16)
        if fence:
            W("gpsimd", "sg", v)
        return v

    def s_dma(dst, src, fence=True):
        v = bd.op("sync", lambda E, d=dst, s=src: E.dma_start(out=d, in_=s),
                  inc="ss", dv=16)
        if fence:
            W("sync", "ss", v)
        return v

    def a_dma(dst, src, fence=True):
        v = bd.op("scalar", lambda E, d=dst, s=src: E.dma_start(out=d, in_=s),
                  inc="sa", dv=16)
        if fence:
            W("scalar", "sa", v)
        return v

    def t_mm(o, l, r, st, sp, inc=None):
        return bd.op("tensor", lambda E, o=o, l=l, r=r, st=st, sp=sp:
                     E.matmul(o, l, r, start=st, stop=sp), inc=inc)

    def t_tr(o, i, inc=None):
        return bd.op("tensor", lambda E, o=o, i=i: E.transpose(o, i, idm_s[:]),
                     inc=inc)

    def v_tt(o, a, b2, alu, inc=None):
        return bd.op("vector", lambda E, o=o, a=a, b=b2, al=alu:
                     E.tensor_tensor(o, a, b, al), inc=inc)

    def v_copy(o, i, inc=None):
        return bd.op("vector", lambda E, o=o, i=i: E.tensor_copy(o, i), inc=inc)

    def v_tsc(o, i, c, alu, inc=None):
        return bd.op("vector", lambda E, o=o, i=i, c=c, al=alu:
                     E.tensor_single_scalar(o, i, c, al), inc=inc)

    def v_recip(o, i, inc=None):
        def f(E, o=o, i=i):
            with nc.allow_low_precision("bf16 softmax-normaliser reciprocal"):
                return E.reciprocal(o, i)
        return bd.op("vector", f, inc=inc)

    def s_act(o, i, f, bias=0.0, scale=1.0, inc=None):
        return bd.op("scalar", lambda E, o=o, i=i, f=f, b=bias, sc=scale:
                     E.activation(o, i, f, bias=b, scale=sc), inc=inc)

    xtile = lambda t: xbf[:, L * t:L * (t + 1)]
    htile = lambda t: hT[:, TOK * t:TOK * (t + 1)]
    x1tl = lambda t: x1b[:, TOK * t:TOK * (t + 1)]

    CN, CEPS, CE12, C2, CID = (cst[:, i:i + 1] for i in range(5))

    ss_cm = s_dma(cmask_s[:], cmsk[:])
    ss_cm01 = s_dma(cmask01_s[:], cmsk01[:])
    ss_id = s_dma(idm_s[:], idm[:])
    ss_bq = s_dma(bqkv_s[:], bqkv_c[:])
    ss_bo = s_dma(bo_s[:], bo_c[:])
    ss_bb = s_dma(bb_s[:], bb_c[:])
    ss_br = s_dma(brpr_s[:], brpr[:])
    ss_rep = s_dma(rep4_s[:], rep4[:])
    ss_pk = s_dma(pk_s[:], pk[:])
    for i, v in enumerate((-25.0, 1e-5, 1e-12, 2.0, 1.0 / D)):
        bd.op("gpsimd", lambda E, i=i, v=v: E.memset(cst[:, i:i + 1], v), inc="cg")
    bd.op("gpsimd", lambda E: E.memset(onec[:], 1.0), inc="cg")
    bd.op("gpsimd", lambda E: E.memset(oner[:], 1.0), inc="cg")
    cg_ones = bd.op("gpsimd", lambda E: E.memset(ones128[:], 1.0), inc="cg")
    W("vector", "ss", ss_br)
    W("vector", "cg", cg_ones)
    cv_ab0 = bd.op("vector", lambda E: E.tensor_scalar_add(
        abias_s[32:36, 0:1], brpr_s[32:36, 1:2], cst[32:36, 2:3]), inc="cv")
    W("scalar", "cv", cv_ab0)
    cs_ab = s_act(abias_s[32:36, 0:1], abias_s[32:36, 0:1], AF.Ln, inc="cs")
    W("vector", "cs", cs_ab)
    cv_ab = v_tt(abias_s[32:36, 0:1], abias_s[32:36, 0:1], brpr_s[32:36, 0:1],
                 ALU.add, inc="cv")

    slot_free = [0, 0, 0]

    def wslot(i):
        return wr[:, NT * 128 * (i % 3):NT * 128 * ((i % 3) + 1)]

    sg_w = {}

    def load_ring(i, src_ap):
        W("gpsimd", "ct", slot_free[i % 3])
        sg_w[i] = g_dma(wslot(i), src_ap)

    sg_x = []
    for t in range(NT):
        v = g_dma(xtile(t), xt[:, t, :], fence=(t < 8 or t % 4 == 3))
        sg_x.append(v)
        if t in (13, 21, 29):
            load_ring((t - 13) // 8, wqkv[0, (t - 13) // 8, :, :])
    for t in range(8, NT):
        sg_x[t] = sg_x[min(NT - 1, (t // 4) * 4 + 3)]

    ct_sq_hist = [0, 0]
    for t in range(NT):
        W("tensor", "sg", sg_x[t])
        if t == 0:
            W("tensor", "cg", cg_ones)
        for c in range(4):
            t_mm(pb[c][:, :], ones128[:, :],
                 xtile(t)[:, 512 * c:512 * (c + 1)], t == 0, t == NT - 1)
        sl_ = t % 2
        xo_ = xsqt[:, 2048 * sl_:2048 * (sl_ + 1)]
        if sl_ == 0:
            W("vector", "sg", sg_x[t])
            W("vector", "ct", ct_sq_hist[0])
            sq_v = v_tt(xo_, xtile(t)[:, :], xtile(t)[:, :], ALU.mult,
                        inc="cv")
            W("tensor", "cv", sq_v)
        else:
            W("scalar", "sg", sg_x[t])
            W("scalar", "ct", ct_sq_hist[1])
            sq_s = s_act(xo_, xtile(t)[:, :], AF.Square, inc="cs")
            W("tensor", "cs", sq_s)
        for c in range(4):
            rp = 32 * (c % 2)
            ct_sq = t_mm(pb[5 + c // 2][rp:rp + 1, :], onec[:, :],
                         xo_[:, 512 * c:512 * (c + 1)], t == 0, t == NT - 1,
                         inc=("ct" if c == 3 else None))
        ct_sq_hist[sl_] = ct_sq

    ct_stream = bd.cnt["ct"]
    cv_center_l = [None] * 4
    cs_rr_prev = [0]
    ct_bc_prev = [0]
    p0_state = {"pb4": 0}

    def phase0_chunk(c, bg):
        cs_ = 512 * c
        W("vector", "ct", ct_stream)
        W("vector", "cs", cs_rr_prev[0])
        cv_m = bd.op("vector", lambda E, c=c: E.tensor_scalar_mul(
            outst[:, 0:512], pb[c][:, :], cst[:, 4:5]), inc="cv")
        bg[c] = ("cv", cv_m)
        cv_mb = v_copy(m2b[:, 0:512], outst[:, 0:512], inc="cv")
        rp = 32 * (c % 2)
        cv_e2 = bd.op("vector", lambda E, c=c, rp=rp: E.tensor_scalar_mul(
            outst[0:1, 512:1024], pb[5 + c // 2][rp:rp + 1, :],
            cst[0:1, 4:5]), inc="cv")
        bg[5 + c // 2] = ("cv", cv_e2)
        v_tt(prow[0:1, :], outst[0:1, 0:512], outst[0:1, 0:512], ALU.mult)
        cv_var = v_tt(outst[0:1, 512:1024], outst[0:1, 512:1024], prow[0:1, :],
                      ALU.subtract, inc="cv")
        W("scalar", "cv", cv_var)
        W("scalar", "ct", ct_bc_prev[0])
        s_act(prow[0:1, :], outst[0:1, 512:1024], AF.Ln, bias=cst[0:1, 1:2])
        cs_rr = s_act(r0w[0:1, :], prow[0:1, :], AF.Exp, scale=-0.5, inc="cs")
        cs_rr_prev[0] = cs_rr
        W("tensor", "cs", cs_rr)
        W("tensor", "cv", p0_state["pb4"])
        ct_bc = t_mm(pb[4][:, :], oner[:, :], r0w[0:1, :], True, True,
                     inc="ct")
        ct_bc_prev[0] = ct_bc
        W("vector", "ct", ct_bc)
        cv_rbc = v_copy(rbc[:, cs_:cs_ + 512], pb[4][:, :], inc="cv")
        p0_state["pb4"] = cv_rbc
        bg[4] = ("cv", cv_rbc)
        W("vector", "cv", cv_mb)
        W("vector", "cs", cs_rr)
        lst = []
        for t in range(NT):
            lst.append(v_tt(xtile(t)[:, cs_:cs_ + 512],
                            xtile(t)[:, cs_:cs_ + 512], m2b[:, 0:512],
                            ALU.subtract, inc="cv"))
        cv_center_l[c] = lst


    bg = {0: ("cv", 0), 1: ("cv", 0), 2: ("cv", 0), 3: ("cv", 0),
          4: ("cv", 0), 5: ("cv", 0), 6: ("cv", 0)}
    rb_free = [0, 0]
    cv_vtm_done = [0]
    vtmp_cv = [0, 0]
    pbt_free = 0
    ho_free = [0, 0, 0, 0]
    vtmp_free = [0, 0]
    gidx = 0
    ss_ho = {}
    ct_xdead = 0
    first_diag = [True]
    first_tr = [True]

    def Wg(eng, gate):
        W(eng, gate[0], gate[1])

    def proj_group(ridx, c, dst, bias_col, tile_gates=None):
        nonlocal gidx
        bk = gidx % 2
        bank = pb[bk]
        Wg("tensor", bg[bk])
        if ridx in sg_w:
            W("tensor", "sg", sg_w.pop(ridx))
        for t in range(NT):
            if tile_gates is not None:
                W("tensor", "cv", tile_gates[t])
            ct_l = t_mm(bank[:, :], wslot(ridx)[:, 128 * t:128 * (t + 1)],
                        xtile(t)[:, 512 * c:512 * (c + 1)], t == 0, t == NT - 1,
                        inc=("ct" if t == NT - 1 else None))
        sl = gidx % 2
        W("vector", "ct", ct_l)
        W("vector", "cs", vtmp_free[sl])
        cv_e = v_tt(vtmp[:, 512 * sl:512 * (sl + 1)], bank[:, :],
                    rbc[:, 512 * c:512 * (c + 1)], ALU.mult, inc="cv")
        bg[bk] = ("cv", cv_e)
        vtmp_cv[sl] = cv_e
        W("scalar", "cv", cv_e)
        W("scalar", "ss", ss_bq)
        cs_e = bd.op("scalar", lambda E, o=dst, i=vtmp[:, 512 * sl:512 * (sl + 1)],
                     b=bias_col: E.activation(o, i, AF.Identity, bias=b, scale=1.0),
                     inc="cs")
        vtmp_free[sl] = cs_e
        gidx += 1
        return ct_l, cs_e

    def attn_pair(h, p):
        cA, cB = 2 * p, 2 * p + 1
        cfg = {"A": dict(c=cA, sbk=2, obk=4, mbk=5, ps=0),
               "B": dict(c=cB, sbk=3, obk=6, mbk=1, ps=1)}
        for st in cfg.values():
            st["nkt"] = 4 * st["c"] + 4
            st["qcs"] = qc[:, 512 * st["c"]:512 * (st["c"] + 1)]
            st["ct_av"] = 0
            st["cs_prev"] = {2: 0, 3: 0}
        W("tensor", "cv", cv_vtm_done[0])
        W("tensor", "cs", max(cs_q[cA], cs_q[cB]))
        seq = []
        for i in range(cfg["B"]["nkt"]):
            if i < cfg["A"]["nkt"]:
                seq.append(("A", i))
            seq.append(("B", i))
        def emit_S(X, kt):
            st = cfg[X]
            c = st["c"]
            diag = kt >= 4 * c
            j0 = 128 * (kt - 4 * c) if diag else 0
            sbk = st["sbk"]
            if X == "B" and kt >= cfg["A"]["nkt"]:
                sbk = 2 + (kt - cfg["A"]["nkt"]) % 2
            ps = 2 * st["ps"] + (kt % 2)
            Wg("tensor", bg[sbk])
            W("tensor", "cs", st["cs_prev"][sbk])
            ct_s = t_mm(pb[sbk][:, j0:512], kb[:, 128 * kt:128 * (kt + 1)],
                        st["qcs"][:, j0:512], True, True, inc="ct")
            W("scalar", "ct", ct_s)
            cs_e = bd.op("scalar", lambda E,
                         o=pt[:, 512 * ps + j0:512 * (ps + 1)],
                         i=pb[sbk][:, j0:512]:
                         E.activation(o, i, AF.Exp, bias=cst[:, 0:1], scale=SC),
                         inc="cs")
            st["cs_prev"][sbk] = cs_e
            cv_msk = 0
            if diag:
                W("vector", "cs", cs_e)
                if first_diag[0]:
                    W("vector", "ss", ss_cm01)
                    first_diag[0] = False
                cv_msk = v_tt(pt[:, 512 * ps + j0:512 * ps + j0 + 128],
                              pt[:, 512 * ps + j0:512 * ps + j0 + 128],
                              cmask01_s[:], ALU.mult, inc="cv")
            return (X, kt, j0, cs_e, cv_msk, sbk)

        def emit_AV(pend):
            X, kt, j0, cs_e, cv_msk, sbk = pend
            st = cfg[X]
            c = st["c"]
            obk, mbk = st["obk"], st["mbk"]
            ps = 2 * st["ps"] + (kt % 2)
            if cv_msk:
                W("tensor", "cv", cv_msk)
            W("tensor", "cs", cs_e)
            if kt == 0:
                Wg("tensor", bg[obk])
                Wg("tensor", bg[mbk])
            t_mm(pb[obk][:, j0:512], vtm[:, 128 * kt:128 * (kt + 1)],
                 pt[:, 512 * ps + j0:512 * (ps + 1)], kt == 0, kt == st["nkt"] - 1)
            ct_av = t_mm(pb[mbk][:, j0:512], ones128[:, :],
                         pt[:, 512 * ps + j0:512 * (ps + 1)], kt == 0,
                         kt == st["nkt"] - 1, inc="ct")
            bg[sbk] = ("ct", ct_av)
            if kt == st["nkt"] - 1:
                es = st["ps"]
                W("scalar", "ct", ct_av)
                W("scalar", "cv", vtmp_cv[es])
                cs_ln = s_act(vtmp[:, 512 * es:512 * (es + 1)], pb[mbk][:, 0:512],
                              AF.Ln, inc="cs")
                bg[mbk] = ("cs", cs_ln)
                W("scalar", "cv", rb_free[es])
                cs_rb = s_act(nrmb[:, 512 * es:512 * (es + 1)],
                              vtmp[:, 512 * es:512 * (es + 1)], AF.Exp,
                              scale=-1.0, inc="cs")
                vtmp_free[es] = cs_rb
                W("vector", "cs", cs_rb)
                W("vector", "ss", ho_free[c])
                cv_ho = v_tt(ho[:, 512 * c:512 * (c + 1)], pb[obk][:, 0:512],
                             nrmb[:, 512 * es:512 * (es + 1)], ALU.mult, inc="cv")
                bg[obk] = ("cv", cv_ho)
                rb_free[es] = cv_ho
                W("sync", "cv", cv_ho)
                ss_ho[(h, c)] = s_dma(
                    buf2[c // 2, :, h, 512 * (c % 2):512 * (c % 2 + 1)],
                    ho[:, 512 * c:512 * (c + 1)])
                ho_free[c] = ss_ho[(h, c)]

        pend = None
        for item in seq:
            cur = emit_S(*item)
            if pend is not None:
                emit_AV(pend)
            pend = cur
        emit_AV(pend)

    dynv = {}
    for h in range(NH):
        for c in range(4):
            if h == 0:
                phase0_chunk(c, bg)
            ct_l, _ = proj_group(3 * h, c, vfm[:, 512 * c:512 * (c + 1)],
                                 bqkv_s[:, 3 * h:3 * h + 1],
                                 tile_gates=(cv_center_l[c] if h == 0
                                             else None))
        slot_free[(3 * h) % 3] = ct_l
        if h + 1 < NH:
            load_ring(3 * (h + 1), wqkv[h + 1, 0, :, :])
        cs_v_done = bd.cnt["cs"]

        cv_tr = 0
        for c in range(4):
            ct_l, _ = proj_group(3 * h + 1, c, kb[:, 512 * c:512 * (c + 1)],
                                 bqkv_s[:, 3 * h + 1:3 * h + 2])
            if c < 2:
                W("tensor", "cs", cs_v_done)
                W("tensor", "cv", pbt_free)
                if first_tr[0]:
                    W("tensor", "ss", ss_id)
                    first_tr[0] = False
                for k in range(8):
                    kk = 8 * c + k
                    ct_tr = t_tr(pbt[:, 128 * k:128 * (k + 1)],
                                 vfm[:, 128 * kk:128 * (kk + 1)],
                                 inc=("ct" if k == 7 else None))
                W("vector", "ct", ct_tr)
                cv_tr = v_copy(vtm[:, 1024 * c:1024 * (c + 1)], pbt[:, 0:1024],
                               inc="cv")
                pbt_free = cv_tr
        slot_free[(3 * h + 1) % 3] = ct_l
        if h + 1 < NH:
            load_ring(3 * (h + 1) + 1, wqkv[h + 1, 1, :, :])
        cv_vtm_done[0] = cv_tr

        cs_q = {}
        for c in range(4):
            ct_q, cs_q[c] = proj_group(3 * h + 2, c,
                                       qc[:, 512 * c:512 * (c + 1)],
                                       bqkv_s[:, 3 * h + 2:3 * h + 3])
            if c == 2:
                attn_pair(h, 0)
            if c == 3:
                slot_free[(3 * h + 2) % 3] = ct_q
                if h + 1 < NH:
                    load_ring(3 * (h + 1) + 2, wqkv[h + 1, 2, :, :])
                if h == NH - 1:
                    ct_xdead = ct_q
        if h == NH - 1:
            W("scalar", "ct", ct_xdead)
            W("scalar", "ss", max(ss_ho[(hh, cc_)] for hh in range(15)
                                  for cc_ in range(4)))
            sa_own = bd.op("scalar", lambda E: E.dma_start(
                out=hT[:, 0:15 * TOK].rearrange("p (t c) -> p t c", t=15),
                in_=buf2[bass.ds(dynv["mb_ac"], 1), :, 0:15, :].squeeze(0)),
                inc="sa", dv=16)
            W("scalar", "cc", 1)
            sa_p1 = bd.op("scalar", lambda E: E.dma_start(
                out=hT[:, 16 * TOK:24 * TOK].rearrange("p (t c) -> p t c", t=8),
                in_=agout1[bass.ds(dynv["pb_ac"], 1), :, :, :].squeeze(0)),
                inc="sa2", dv=16)
            W("scalar", "cc2", 1)
            sa_p2 = bd.op("scalar", lambda E: E.dma_start(
                out=hT[:, 24 * TOK:30 * TOK].rearrange("p (t c) -> p t c", t=6),
                in_=agout2[bass.ds(dynv["pb_ac"], 1), :, :, :].squeeze(0)),
                inc="sa3", dv=16)
            for m_ in range(3):
                load_ring(48 + m_, wo[m_, :, :])
        attn_pair(h, 1)

        if h == 7:
            W("gpsimd", "ss", max(ss_ho[(hh, c)] for hh in range(8)
                                  for c in range(4)))
            W("gpsimd", "ss", ss_pk)

            def _dyn(E):
                _, v1 = nc.values_load_multi_w_load_instructions(
                    pk_s[0:1, 0:1], engines=[Pool], min_val=0, max_val=1,
                    skip_runtime_bounds_check=True)
                dynv["pb"] = v1[0]
            bd.q["gpsimd"].append(_dyn)
            sg_a1 = bd.op("gpsimd", lambda E: E.dma_start(
                out=agin1[:],
                in_=buf2[bass.ds(dynv["pb"], 1), :, 0:8, :].squeeze(0)),
                inc="sg", dv=16)
            W("gpsimd", "sg", sg_a1)
            bd.op("gpsimd", lambda E: E.collective_compute(
                "AllGather", ALU.bypass, replica_groups=RG,
                ins=[agin1[:]], outs=[agout1[:]]), inc="cc")

            W("sync", "ss", ss_pk)

            def _dyn_sp(E):
                _, v1 = nc.values_load_multi_w_load_instructions(
                    pk_s[0:1, 0:1], engines=[SPE], min_val=0, max_val=1,
                    skip_runtime_bounds_check=True)
                dynv["pb_sp"] = v1[0]
                _, v2 = nc.values_load_multi_w_load_instructions(
                    pk_s[0:1, 1:2], engines=[SPE], min_val=0, max_val=1,
                    skip_runtime_bounds_check=True)
                dynv["mb_sp"] = v2[0]
            bd.q["sync"].append(_dyn_sp)
            W("scalar", "ss", ss_pk)

            def _dyn_ac(E):
                _, v1 = nc.values_load_multi_w_load_instructions(
                    pk_s[0:1, 0:1], engines=[ACTE], min_val=0, max_val=1,
                    skip_runtime_bounds_check=True)
                dynv["pb_ac"] = v1[0]
                _, v2 = nc.values_load_multi_w_load_instructions(
                    pk_s[0:1, 1:2], engines=[ACTE], min_val=0, max_val=1,
                    skip_runtime_bounds_check=True)
                dynv["mb_ac"] = v2[0]
            bd.q["scalar"].append(_dyn_ac)

        if h == 13:
            W("sync", "ss", max(ss_ho[(hh, c)] for hh in range(8, 14)
                                for c in range(4)))
            ss_a2 = bd.op("sync", lambda E: E.dma_start(
                out=agin2[:],
                in_=buf2[bass.ds(dynv["pb_sp"], 1), :, 8:14, :].squeeze(0)),
                inc="ss", dv=16)
            W("sync", "ss", ss_a2)
            W("gpsimd", "ss", ss_a2)
            bd.op("gpsimd", lambda E: E.collective_compute(
                "AllGather", ALU.bypass, replica_groups=RG,
                ins=[agin2[:]], outs=[agout2[:]]), inc="cc2")


    ct_A_end = bd.cnt["ct"]
    cv_A_end = bd.cnt["cv"]
    W("sync", "ss", max(ss_ho[(15, c)] for c in range(4)))
    ss_a3 = bd.op("sync", lambda E: E.dma_start(
        out=agin3[:],
        in_=buf2[bass.ds(dynv["pb_sp"], 1), :, 14:16, :].squeeze(0)),
        inc="ss", dv=16)
    W("sync", "ss", ss_a3)
    W("gpsimd", "ss", ss_a3)
    bd.op("gpsimd", lambda E: E.collective_compute(
        "AllGather", ALU.bypass, replica_groups=RG,
        ins=[agin3[:]], outs=[agout3[:]]), inc="cc3")
    ss_own15 = bd.op("sync", lambda E: E.dma_start(
        out=hT[:, 15 * TOK:16 * TOK],
        in_=buf2[bass.ds(dynv["mb_sp"], 1), :, 15:16, :]
        .squeeze(0).squeeze(1)), inc="ss", dv=16)
    W("sync", "ss", ss_own15)
    W("sync", "cc3", 1)
    ss_p3 = bd.op("sync", lambda E: E.dma_start(
        out=hT[:, 30 * TOK:32 * TOK].rearrange("p (t c) -> p t c", t=2),
        in_=agout3[bass.ds(dynv["pb_sp"], 1), :, :, :].squeeze(0)),
        inc="ss", dv=16)
    W("sync", "ss", ss_p3)
    ss_A_end = bd.cnt["ss"]
    W("gpsimd", "ct", ct_A_end)
    W("gpsimd", "cv", cv_A_end)
    W("gpsimd", "ss", ss_A_end)
    sg_lora = g_dma(loraT[:], lora[:])
    sg_bcat = g_dma(bcat_s[:], bcat[:])

    W("tensor", "ct", ct_A_end)
    W("tensor", "cv", cv_A_end)
    W("tensor", "cs", bd.cnt["cs"])
    ob_free = [0, 0, 0]
    xst_free = 0
    outst_free = [0, 0]
    gi = 0
    wb_ct_gate = {}

    for m in range(NT):
        W("sync", "cv", xst_free)
        ss_x = s_dma(xst[:], xr[:, m, :])
        for c in range(2):
            bank = pb[gi % 3]
            W("tensor", "cs", ob_free[gi % 3])
            if m == 0 and c == 0:
                W("tensor", "sa", sa_own)
                W("tensor", "sg", sg_w.pop(48))
            elif (48 + m) in sg_w:
                W("tensor", "sg", sg_w.pop(48 + m))
            for t in range(NT):
                if m == 0 and c == 0 and t == 15:
                    W("tensor", "ss", ss_own15)
                if m == 0 and c == 0 and t == 16:
                    W("tensor", "sa2", sa_p1)
                if m == 0 and c == 0 and t == 24:
                    W("tensor", "sa3", sa_p2)
                if m == 0 and c == 0 and t == 30:
                    W("tensor", "ss", ss_p3)
                ct_l = t_mm(bank[:, 0:T2], wslot(48 + m)[:, 128 * t:128 * (t + 1)],
                            htile(t)[:, T2 * c:T2 * (c + 1)], t == 0, t == NT - 1,
                            inc=("ct" if t == NT - 1 else None))
            sl = gi % 2
            W("vector", "ct", ct_l)
            W("vector", "ss", ss_x)
            W("vector", "cs", vtmp_free[sl])
            cv_a = v_tt(vtmp[:, 512 * sl:512 * (sl + 1)], bank[:, 0:T2],
                        xst[:, T2 * c:T2 * (c + 1)], ALU.add, inc="cv")
            if c == 1:
                xst_free = cv_a
            W("scalar", "cv", cv_a)
            W("scalar", "ss", ss_bo)
            cs_x1 = bd.op("scalar", lambda E,
                          o=x1tl(m)[:, T2 * c:T2 * (c + 1)],
                          i=vtmp[:, 512 * sl:512 * (sl + 1)], b=bo_s[:, m:m + 1]:
                          E.activation(o, i, AF.Identity, bias=b, scale=1.0), inc="cs")
            ob_free[gi % 3] = cs_x1
            vtmp_free[sl] = cs_x1
            gi += 1
        slot_free[(48 + m) % 3] = ct_l
        nxt = 48 + m + 3
        if nxt < 80:
            load_ring(nxt, wo[nxt - 48, :, :])
        else:
            load_ring(nxt, wb[nxt - 80, :, :])

    cs_stage1 = bd.cnt["cs"]

    pb3_gate = bg[3]
    pb4_gate = bg[4]
    pb5_gate = bg[5]
    pb6_gate = bg[6]
    rb2_free = max(rb_free)
    cs2_prev = [0]
    cs_ex_prev = 0
    cv_zw = {}
    for c in range(2):
        c0 = T2 * c
        W("tensor", "cs", cs_stage1)
        W("vector", "cs", cs_stage1)
        ct_st = 0
        hist = [0, 0]
        for t in range(NT):
            sl_ = t % 2
            xsl = x1tl(t)[:, c0:c0 + T2]
            xo_ = xsqt[:, 512 * sl_:512 * sl_ + T2]
            if sl_ == 0:
                W("vector", "ct", hist[0])
                sq_v = v_tt(xo_, xsl, xsl, ALU.mult, inc="cv")
            else:
                W("scalar", "ct", hist[1])
                sq_s = s_act(xo_, xsl, AF.Square, inc="cs")
            if t == 0:
                W("tensor", pb3_gate[0], pb3_gate[1])
            t_mm(pb[3][:, 0:T2], ones128[:, :], xsl, t == 0, t == NT - 1)
            if sl_ == 0:
                W("tensor", "cv", sq_v)
            else:
                W("tensor", "cs", sq_s)
            if t == 0:
                W("tensor", pb4_gate[0], pb4_gate[1])
            ct_st = t_mm(pb[4][:, 0:T2], ones128[:, :], xo_, t == 0, t == NT - 1,
                         inc="ct")
            hist[sl_] = ct_st
        W("vector", "ct", ct_st)
        cv_m = bd.op("vector", lambda E: E.tensor_scalar_mul(
            outst[:, 0:T2], pb[3][:, 0:T2], cst[:, 4:5]), inc="cv")
        pb3_gate = ("cv", cv_m)
        cv_mb = v_copy(m2b[:, c0:c0 + T2], outst[:, 0:T2], inc="cv")
        v_tt(vtmp[:, 0:T2], outst[:, 0:T2], outst[:, 0:T2], ALU.mult)
        W("vector", "cs", cs2_prev[0])
        cv_e2 = bd.op("vector", lambda E: E.tensor_scalar_mul(
            outst[:, 512:512 + T2], pb[4][:, 0:T2], cst[:, 4:5]), inc="cv")
        pb4_gate = ("cv", cv_e2)
        cv_var = v_tt(outst[:, 512:512 + T2], outst[:, 512:512 + T2],
                      vtmp[:, 0:T2], ALU.subtract, inc="cv")
        W("scalar", "cv", cv_var)
        s_act(vtmp[:, 512:512 + T2], outst[:, 512:512 + T2], AF.Ln,
              bias=cst[:, 1:2])
        W("scalar", "cv", rb2_free)
        s_act(r2bc[:, c0:c0 + T2], vtmp[:, 512:512 + T2], AF.Exp, scale=-0.5)
        cs_s2 = s_act(nrmb[:, 0:T2], vtmp[:, 512:512 + T2], AF.Exp, scale=0.5,
                      inc="cs")
        cs2_prev[0] = cs_s2
        W("vector", "cv", cv_mb)
        W("vector", "cs", cs_s2)
        cv_c2l = []
        for t in range(NT):
            cv_c2l.append(v_tt(x1tl(t)[:, c0:c0 + T2], x1tl(t)[:, c0:c0 + T2],
                               m2b[:, c0:c0 + T2], ALU.subtract, inc="cv"))
        W("tensor", pb5_gate[0], pb5_gate[1])
        if c == 0:
            W("tensor", "sg", sg_lora)
        for t in range(NT):
            W("tensor", "cv", cv_c2l[t])
            ct_z = t_mm(pb[5][0:36, 0:T2], loraT[:, 36 * t:36 * (t + 1)],
                        x1tl(t)[:, c0:c0 + T2], t == 0, t == NT - 1,
                        inc=("ct" if t == NT - 1 else None))
        W("vector", "ct", ct_z)
        W("vector", "cs", cs_ex_prev)
        cv_zr = v_tt(zr[:, :], pb[5][0:36, 0:T2], r2bc[0:36, c0:c0 + T2],
                     ALU.mult, inc="cv")
        pb5_gate = ("cv", cv_zr)
        W("scalar", "cv", cv_zr)
        W("scalar", "cv", cv_ab)
        cs_ex = bd.op("scalar", lambda E: E.activation(
            exps[32:36, :], zr[32:36, :], AF.Exp, bias=abias_s[32:36, 0:1],
            scale=1.0), inc="cs")
        cs_ex_prev = cs_ex
        W("tensor", "cs", cs_ex)
        W("tensor", pb6_gate[0], pb6_gate[1])
        ct_se = t_mm(pb[6][:, 0:T2], ones128[32:36, :], exps[32:36, :], True,
                     True, inc="ct")
        W("scalar", "ct", ct_se)
        cs_li = s_act(vtmp[:, 0:T2], pb[6][:, 0:T2], AF.Ln, inc="cs")
        pb6_gate = ("cs", cs_li)
        cs_inv = s_act(nrmb[:, 512:512 + T2], vtmp[:, 0:T2], AF.Exp, scale=-1.0,
                       inc="cs")
        W("vector", "cs", cs_inv)
        v_tt(nrmb[:, 512:512 + T2], nrmb[:, 0:T2], nrmb[:, 512:512 + T2],
             ALU.mult)
        cv_q2 = bd.op("vector", lambda E: E.tensor_scalar_mul(
            nrmb[:, 512:512 + T2], nrmb[:, 512:512 + T2], cst[:, 3:4]), inc="cv")
        W("tensor", pb4_gate[0], pb4_gate[1])
        W("tensor", "ss", ss_rep)
        ct_ar = t_mm(pb[4][0:32, 0:T2], rep4_s[32:36, :], exps[32:36, :], True,
                     True, inc="ct")
        W("vector", "ct", ct_ar)
        cv_asc = v_tt(exps[0:32, :], pb[4][0:32, 0:T2],
                      nrmb[0:32, 512:512 + T2], ALU.mult, inc="cv")
        pb4_gate = ("cv", cv_asc)
        cv_zw[c] = v_tt(zw[:, c0:c0 + T2], zr[0:32, :], exps[0:32, :], ALU.mult,
                        inc="cv")
        rb2_free = cv_asc

    for m in range(NT):
        for c in range(2):
            c0 = T2 * c
            bank = pb[gi % 3]
            W("tensor", "cs", ob_free[gi % 3])
            if (80 + m) in sg_w:
                W("tensor", "sg", sg_w.pop(80 + m))
            if m == 0 and c == 0:
                W("tensor", "sg", sg_bcat)
            for t in range(NT):
                t_mm(bank[:, 0:T2], wslot(80 + m)[:, 128 * t:128 * (t + 1)],
                     x1tl(t)[:, c0:c0 + T2], t == 0, False)
            W("tensor", "cv", cv_zw[c])
            ct_l = t_mm(bank[:, 0:T2], bcat_s[:, 128 * m:128 * (m + 1)],
                        zw[:, c0:c0 + T2], False, True, inc="ct")
            sl = gi % 2
            W("vector", "ct", ct_l)
            W("vector", "cs", vtmp_free[0])
            cv_y1 = v_tt(vtmp[:, 0:512], bank[:, 0:T2], r2bc[:, c0:c0 + T2],
                         ALU.mult, inc="cv")
            W("scalar", "cv", cv_y1)
            W("scalar", "ss", ss_bb)
            cs_y2 = bd.op("scalar", lambda E, b=bb_s[:, m:m + 1]: E.activation(
                vtmp[:, 512:1024], vtmp[:, 0:512], AF.Identity, bias=b, scale=1.0),
                inc="cs")
            ob_free[gi % 3] = cs_y2
            vtmp_free[0] = cs_y2
            W("vector", "cs", cs_y2)
            W("vector", "ss", outst_free[sl])
            cv_x2a = v_tt(outst[:, 512 * sl:512 * (sl + 1)], vtmp[:, 512:1024],
                          x1tl(m)[:, c0:c0 + T2], ALU.add, inc="cv")
            cv_x2 = v_tt(outst[:, 512 * sl:512 * (sl + 1)],
                         outst[:, 512 * sl:512 * (sl + 1)], m2b[:, c0:c0 + T2],
                         ALU.add, inc="cv")
            W("sync", "cv", cv_x2)
            ss_o = s_dma(out[:, m, c0:c0 + T2], outst[:, 512 * sl:512 * (sl + 1)])
            outst_free[sl] = ss_o
            gi += 1
        nxt = 80 + m + 3
        if nxt < 112:
            slot_free[(80 + m) % 3] = ct_l
            load_ring(nxt, wb[nxt - 80, :, :])

    bd.wait("sync", "ss", bd.cnt["ss"])

    with nc.Block() as block:
        for name, dec in (("gpsimd", block.gpsimd), ("sync", block.sync),
                          ("tensor", block.tensor), ("vector", block.vector),
                          ("scalar", block.scalar)):
            def make(fns):
                def emit(E):
                    for f in fns:
                        f(E)
                return emit
            dec(make(bd.q[name]))

    return nc


_CACHE = {}


def _get_program():
    if "nc" not in _CACHE:
        _CACHE["nc"] = build_program()
    return _CACHE["nc"]


def kernel(**inputs):
    x = np.ascontiguousarray(np.asarray(inputs["L0_attn_x"], dtype=np.float32))
    Wq = np.asarray(inputs["L0_attn_Wq"], dtype=np.float32)
    bq = np.asarray(inputs["L0_attn_bq"], dtype=np.float32)
    Wk = np.asarray(inputs["L0_attn_Wk"], dtype=np.float32)
    bk = np.asarray(inputs["L0_attn_bk"], dtype=np.float32)
    Wv = np.asarray(inputs["L0_attn_Wv"], dtype=np.float32)
    bv = np.asarray(inputs["L0_attn_bv"], dtype=np.float32)
    Wo = np.asarray(inputs["L0_attn_Wo"], dtype=np.float32)
    bo = np.asarray(inputs["L0_attn_bo"], dtype=np.float32)
    Wb = np.asarray(inputs["L0_ffn_W_base"], dtype=np.float32)
    bb = np.asarray(inputs["L0_ffn_b_base"], dtype=np.float32)
    active = np.asarray(inputs["L0_ffn_active"]).astype(np.int64)
    A = np.asarray(inputs["L0_ffn_lora_A"], dtype=np.float32)[active]
    Bm = np.asarray(inputs["L0_ffn_lora_B"], dtype=np.float32)[active]
    Wr = np.asarray(inputs["L0_ffn_router_W"], dtype=np.float32)[active]
    br = np.asarray(inputs["L0_ffn_router_b"], dtype=np.float32)[active]
    pr = np.asarray(inputs["L0_ffn_domain_prior"], dtype=np.float32)[active]

    WoT = np.ascontiguousarray(Wo.T)
    WbT = np.ascontiguousarray(Wb.T)
    loraT_full = np.concatenate(
        [A.transpose(2, 0, 1).reshape(D, 32), Wr.T], axis=1)
    bcat_full = np.ascontiguousarray(Bm.transpose(0, 2, 1).reshape(32, D))
    brpr_full = np.zeros((36, 2), np.float32)
    brpr_full[32:36, 0] = br
    brpr_full[32:36, 1] = pr
    rep4_full = np.zeros((36, 32), BF)
    for e in range(4):
        rep4_full[32 + e, 8 * e:8 * e + 8] = 1
    i_ = np.arange(128)
    cm = np.where(i_[None, :] >= i_[:, None], 0.0, -1e9).astype(np.float32)
    cm01 = np.where(i_[None, :] >= i_[:, None], 1.0, 0.0).astype(BF)
    idm_full = np.eye(128, dtype=BF)

    def head_slices(h):
        arr = np.empty((NH, 3, 128, NT * 128), np.float32)
        for hp in range(NH):
            for p, Wm in enumerate((Wv, Wk, Wq)):
                blk = Wm[2048 * h + 128 * hp:2048 * h + 128 * (hp + 1), :].T
                arr[hp, p] = _swz(np.ascontiguousarray(blk))
        return arr

    def fo_tiles(M):
        arr = np.empty((NT, 128, NT * 128), np.float32)
        for m in range(NT):
            arr[m] = _swz(np.ascontiguousarray(M[:, 128 * m:128 * (m + 1)]))
        return arr

    wb_tiles = fo_tiles(WbT)
    lora_swz = _swz(loraT_full)

    in_maps = []
    for c in range(8):
        b, h = c // 2, c % 2
        xtb = _swz(np.ascontiguousarray(x[b].T)).reshape(128, NT, L)
        xrb = _swz(np.ascontiguousarray(
            x[b, TOK * h:TOK * (h + 1), :].T)).reshape(128, NT, TOK)
        bcol = np.empty((128, NH * 3), np.float32)
        for hp in range(NH):
            for p, bvec in enumerate((bv, bk, bq)):
                bcol[:, 3 * hp + p] = bvec[2048 * h + 128 * hp:
                                           2048 * h + 128 * (hp + 1)]
        WoTp = WoT if h == 0 else np.ascontiguousarray(
            np.vstack([WoT[2048:], WoT[:2048]]))
        in_maps.append({
            "xt": xtb, "xr": xrb, "wqkv": head_slices(h), "bqkv_c": bcol,
            "wo": fo_tiles(WoTp),
            "bo_c": np.ascontiguousarray(bo.reshape(NT, 128).T),
            "wb": wb_tiles,
            "bb_c": np.ascontiguousarray(bb.reshape(NT, 128).T),
            "lora": lora_swz, "bcat": bcat_full, "brpr": brpr_full,
            "rep4": rep4_full, "cmsk": cm, "cmsk01": cm01,
            "idm": idm_full,
            "pk": np.array([[1 - h, h]], dtype=np.int32),
        })

    nc = _get_program()
    res = run_bass_kernel_spmd(nc, in_maps, core_ids=list(range(8)))
    final = np.empty((4, L, D), np.float32)
    for c in range(8):
        b, h = c // 2, c % 2
        o = np.asarray(res.results[c]["out"])
        final[b, TOK * h:TOK * (h + 1), :] = o.transpose(2, 1, 0).reshape(TOK, D)
    return final


# version 12
# speedup vs baseline: 1.1956x; 1.0528x over previous
import numpy as np
import ml_dtypes

import concourse.bass as bass
import concourse.mybir as mybir
from concourse.bass_utils import run_bass_kernel_spmd

F32 = mybir.dt.float32
BF16 = mybir.dt.bfloat16
I32 = mybir.dt.int32
AF = mybir.ActivationFunctionType
ALU = mybir.AluOpType
BF = ml_dtypes.bfloat16
Pool = mybir.EngineType.Pool
SPE = mybir.EngineType.SP
ACTE = mybir.EngineType.Activation

D = 4096
L = 2048
NT = 32
NH = 16
TOK = 1024
T2 = 512
SC = 0.08838834764831845
ENGS = ("tensor", "vector", "scalar", "sync", "gpsimd")
RG = [[0, 1], [2, 3], [4, 5], [6, 7]]


def _swz(m):
    c = m.shape[1]
    return np.ascontiguousarray(
        m.reshape(NT, 128, c).transpose(1, 0, 2)).reshape(128, NT * c)


class Build:
    def __init__(self, nc):
        self.nc = nc
        self.q = {e: [] for e in ENGS}
        self.sem = {}
        self.cnt = {}
        self.last_wait = {}

    def add_sem(self, name):
        self.sem[name] = self.nc.alloc_semaphore(name=name)
        self.cnt[name] = 0

    def op(self, eng, fn, inc=None, dv=1):
        if inc is None:
            self.q[eng].append(fn)
            return None
        self.cnt[inc] += dv
        s = self.sem[inc]

        def run(E, fn=fn, s=s, dv=dv):
            fn(E).then_inc(s, dv)
        self.q[eng].append(run)
        return self.cnt[inc]

    def wait(self, eng, name, val):
        if val is None or val <= 0:
            return
        k = (eng, name)
        if self.last_wait.get(k, 0) >= val:
            return
        self.last_wait[k] = val
        s = self.sem[name]
        self.q[eng].append(lambda E, s=s, v=val: E.wait_ge(s, v))


def build_program():
    nc = bass.Bass("TRN2", target_bir_lowering=False,
                   detect_race_conditions=False)

    din = lambda n, shp, dt=F32: nc.dram_tensor(n, shp, dt, kind="ExternalInput")
    xt = din("xt", [128, NT, L])
    xr = din("xr", [128, NT, TOK])
    wqkv = din("wqkv", [NH, 3, 128, NT * 128])
    bqkv_c = din("bqkv_c", [128, NH * 3])
    wo = din("wo", [NT, 128, NT * 128])
    bo_c = din("bo_c", [128, NT])
    wb = din("wb", [NT, 128, NT * 128])
    bb_c = din("bb_c", [128, NT])
    lora = din("lora", [128, NT * 36])
    bcat = din("bcat", [32, D])
    brpr = din("brpr", [36, 2])
    rep4 = din("rep4", [36, 32], BF16)
    cmsk = din("cmsk", [128, 128])
    cmsk01 = din("cmsk01", [128, 128], BF16)
    idm = din("idm", [128, 128], BF16)
    pk = din("pk", [1, 2], I32)

    out = nc.dram_tensor("out", [128, NT, TOK], F32, kind="ExternalOutput")
    buf2 = nc.dram_tensor("buf2", [2, 128, NH, TOK], BF16)
    agin1 = nc.dram_tensor("agin1", [128, 8, TOK], BF16)
    agin2 = nc.dram_tensor("agin2", [128, 6, TOK], BF16)
    agin3 = nc.dram_tensor("agin3", [128, 2, TOK], BF16)
    agout1 = nc.dram_tensor("agout1", [2, 128, 8, TOK], BF16)
    agout2 = nc.dram_tensor("agout2", [2, 128, 6, TOK], BF16)
    agout3 = nc.dram_tensor("agout3", [2, 128, 2, TOK], BF16)

    sb = nc.alloc_sbuf_tensor
    xbf = sb("xbf", [128, NT * L], BF16)
    rbc = sb("rbc", [128, L], BF16)
    kb = sb("kb", [128, L], BF16)
    qc = sb("qc", [128, 4 * 512], BF16)
    vfm = sb("vfm", [128, L], BF16)
    vtm = sb("vtm", [128, L], BF16)
    wr = sb("wr", [128, 3 * NT * 128], BF16)
    pt = sb("pt", [128, 4 * 512], BF16)
    ho = sb("ho", [128, L], BF16)
    cmask_s = sb("cmask_s", [128, 128], F32)
    cmask01_s = sb("cmask01_s", [128, 128], BF16)
    idm_s = sb("idm_s", [128, 128], BF16)
    cst = sb("cst", [128, 8], F32)
    onec = sb("onec", [128, 1], BF16)
    oner = sb("oner", [1, 128], BF16)
    bqkv_s = sb("bqkv_s", [128, NH * 3], F32)
    bo_s = sb("bo_s", [128, NT], F32)
    bb_s = sb("bb_s", [128, NT], F32)
    brpr_s = sb("brpr_s", [36, 2], F32)
    abias_s = sb("abias_s", [36, 1], F32)
    rep4_s = sb("rep4_s", [36, 32], BF16)
    pk_s = sb("pk_s", [1, 2], I32)
    ones128 = sb("ones128", [128, 128], BF16)
    vtmp = sb("vtmp", [128, 2 * 512], F32)
    xsqt = sb("xsqt", [128, 2 * 2048], BF16)
    prow = sb("prow", [1, 512], F32)
    r0w = sb("r0w", [1, 512], BF16)
    xst = sb("xst", [128, TOK], F32)
    outst = sb("outst", [128, 2 * 512], F32)
    nrmb = sb("nrmb", [128, 2 * 512], BF16)

    at = nc.alloc_sbuf_tensor_at
    adr = lambda t: nc.lookup_mloc(t).addr
    hT = at("hT", [128, NT * TOK], BF16, offset=adr(xbf))
    x1b = at("x1b", [128, NT * TOK], BF16, offset=adr(xbf) + 65536)
    bcat_s = at("bcat_s", [32, D], BF16, offset=adr(rbc))
    m2b = at("m2b", [128, TOK], BF16, offset=adr(qc))
    r2bc = at("r2bc", [128, TOK], BF16, offset=adr(qc) + 2048)
    s2row = at("s2row", [1, TOK], BF16, offset=adr(qc) + 4096)
    zw = at("zw", [32, TOK], BF16, offset=adr(qc) + 6144)
    zr = at("zr", [36, T2], F32, offset=adr(pt))
    exps = at("exps", [36, T2], BF16, offset=adr(pt) + 2048)
    loraT = at("loraT", [128, NT * 36], BF16, offset=adr(pt) + 3072)

    pb = [nc.alloc_psum_tensor(f"pb{i}", [128, 512], F32) for i in range(7)]
    pbt = nc.alloc_psum_tensor("pbt", [128, 1024], BF16)

    bd = Build(nc)
    for s in ("sg", "ss", "ct", "cv", "cs", "cg", "cc", "cc2", "cc3",
              "sa", "sa2", "sa3"):
        bd.add_sem(s)
    W = bd.wait

    def g_dma(dst, src, fence=True):
        v = bd.op("gpsimd", lambda E, d=dst, s=src: E.dma_start(out=d, in_=s),
                  inc="sg", dv=16)
        if fence:
            W("gpsimd", "sg", v)
        return v

    def s_dma(dst, src, fence=True):
        v = bd.op("sync", lambda E, d=dst, s=src: E.dma_start(out=d, in_=s),
                  inc="ss", dv=16)
        if fence:
            W("sync", "ss", v)
        return v

    def a_dma(dst, src, fence=True):
        v = bd.op("scalar", lambda E, d=dst, s=src: E.dma_start(out=d, in_=s),
                  inc="sa", dv=16)
        if fence:
            W("scalar", "sa", v)
        return v

    def t_mm(o, l, r, st, sp, inc=None):
        return bd.op("tensor", lambda E, o=o, l=l, r=r, st=st, sp=sp:
                     E.matmul(o, l, r, start=st, stop=sp), inc=inc)

    def t_tr(o, i, inc=None):
        return bd.op("tensor", lambda E, o=o, i=i: E.transpose(o, i, idm_s[:]),
                     inc=inc)

    def v_tt(o, a, b2, alu, inc=None):
        return bd.op("vector", lambda E, o=o, a=a, b=b2, al=alu:
                     E.tensor_tensor(o, a, b, al), inc=inc)

    def v_copy(o, i, inc=None):
        return bd.op("vector", lambda E, o=o, i=i: E.tensor_copy(o, i), inc=inc)

    def v_tsc(o, i, c, alu, inc=None):
        return bd.op("vector", lambda E, o=o, i=i, c=c, al=alu:
                     E.tensor_single_scalar(o, i, c, al), inc=inc)

    def v_recip(o, i, inc=None):
        def f(E, o=o, i=i):
            with nc.allow_low_precision("bf16 softmax-normaliser reciprocal"):
                return E.reciprocal(o, i)
        return bd.op("vector", f, inc=inc)

    def s_act(o, i, f, bias=0.0, scale=1.0, inc=None):
        return bd.op("scalar", lambda E, o=o, i=i, f=f, b=bias, sc=scale:
                     E.activation(o, i, f, bias=b, scale=sc), inc=inc)

    xtile = lambda t: xbf[:, L * t:L * (t + 1)]
    htile = lambda t: hT[:, TOK * t:TOK * (t + 1)]
    x1tl = lambda t: x1b[:, TOK * t:TOK * (t + 1)]

    CN, CEPS, CE12, C2, CID = (cst[:, i:i + 1] for i in range(5))

    ss_cm = s_dma(cmask_s[:], cmsk[:])
    ss_cm01 = s_dma(cmask01_s[:], cmsk01[:])
    ss_id = s_dma(idm_s[:], idm[:])
    ss_bq = s_dma(bqkv_s[:], bqkv_c[:])
    ss_bo = s_dma(bo_s[:], bo_c[:])
    ss_bb = s_dma(bb_s[:], bb_c[:])
    ss_br = s_dma(brpr_s[:], brpr[:])
    ss_rep = s_dma(rep4_s[:], rep4[:])
    ss_pk = s_dma(pk_s[:], pk[:])
    for i, v in enumerate((-25.0, 1e-5, 1e-12, 2.0, 1.0 / D)):
        bd.op("gpsimd", lambda E, i=i, v=v: E.memset(cst[:, i:i + 1], v), inc="cg")
    bd.op("gpsimd", lambda E: E.memset(onec[:], 1.0), inc="cg")
    bd.op("gpsimd", lambda E: E.memset(oner[:], 1.0), inc="cg")
    cg_ones = bd.op("gpsimd", lambda E: E.memset(ones128[:], 1.0), inc="cg")
    W("vector", "ss", ss_br)
    W("vector", "cg", cg_ones)
    cv_ab0 = bd.op("vector", lambda E: E.tensor_scalar_add(
        abias_s[32:36, 0:1], brpr_s[32:36, 1:2], cst[32:36, 2:3]), inc="cv")
    W("scalar", "cv", cv_ab0)
    cs_ab = s_act(abias_s[32:36, 0:1], abias_s[32:36, 0:1], AF.Ln, inc="cs")
    W("vector", "cs", cs_ab)
    cv_ab = v_tt(abias_s[32:36, 0:1], abias_s[32:36, 0:1], brpr_s[32:36, 0:1],
                 ALU.add, inc="cv")

    slot_free = [0, 0, 0]

    def wslot(i):
        return wr[:, NT * 128 * (i % 3):NT * 128 * ((i % 3) + 1)]

    sg_w = {}

    def load_ring(i, src_ap):
        W("gpsimd", "ct", slot_free[i % 3])
        sg_w[i] = g_dma(wslot(i), src_ap)

    sg_x = []
    for t in range(NT):
        v = g_dma(xtile(t), xt[:, t, :], fence=(t < 8 or t % 4 == 3))
        sg_x.append(v)
        if t in (13, 21, 29):
            load_ring((t - 13) // 8, wqkv[0, (t - 13) // 8, :, :])
    for t in range(8, NT):
        sg_x[t] = sg_x[min(NT - 1, (t // 4) * 4 + 3)]

    ct_sq_hist = [0, 0]
    for t in range(NT):
        W("tensor", "sg", sg_x[t])
        if t == 0:
            W("tensor", "cg", cg_ones)
        for c in range(4):
            t_mm(pb[c][:, :], ones128[:, :],
                 xtile(t)[:, 512 * c:512 * (c + 1)], t == 0, t == NT - 1)
        sl_ = t % 2
        xo_ = xsqt[:, 2048 * sl_:2048 * (sl_ + 1)]
        if sl_ == 0:
            W("vector", "sg", sg_x[t])
            W("vector", "ct", ct_sq_hist[0])
            sq_v = v_tt(xo_, xtile(t)[:, :], xtile(t)[:, :], ALU.mult,
                        inc="cv")
            W("tensor", "cv", sq_v)
        else:
            W("scalar", "sg", sg_x[t])
            W("scalar", "ct", ct_sq_hist[1])
            sq_s = s_act(xo_, xtile(t)[:, :], AF.Square, inc="cs")
            W("tensor", "cs", sq_s)
        for c in range(4):
            rp = 32 * (c % 2)
            ct_sq = t_mm(pb[5 + c // 2][rp:rp + 1, :], onec[:, :],
                         xo_[:, 512 * c:512 * (c + 1)], t == 0, t == NT - 1,
                         inc=("ct" if c == 3 else None))
        ct_sq_hist[sl_] = ct_sq

    ct_stream = bd.cnt["ct"]
    cv_center_l = [None] * 4
    cs_rr_prev = [0]
    ct_bc_prev = [0]
    p0_state = {"pb4": 0}

    def phase0_chunk(c, bg):
        cs_ = 512 * c
        W("vector", "ct", ct_stream)
        W("vector", "cs", cs_rr_prev[0])
        cv_m = bd.op("vector", lambda E, c=c: E.tensor_scalar_mul(
            outst[:, 0:512], pb[c][:, :], cst[:, 4:5]), inc="cv")
        bg[c] = ("cv", cv_m)
        cv_mb = v_copy(m2b[:, 0:512], outst[:, 0:512], inc="cv")
        rp = 32 * (c % 2)
        cv_e2 = bd.op("vector", lambda E, c=c, rp=rp: E.tensor_scalar_mul(
            outst[0:1, 512:1024], pb[5 + c // 2][rp:rp + 1, :],
            cst[0:1, 4:5]), inc="cv")
        bg[5 + c // 2] = ("cv", cv_e2)
        v_tt(prow[0:1, :], outst[0:1, 0:512], outst[0:1, 0:512], ALU.mult)
        cv_var = v_tt(outst[0:1, 512:1024], outst[0:1, 512:1024], prow[0:1, :],
                      ALU.subtract, inc="cv")
        W("scalar", "cv", cv_var)
        W("scalar", "ct", ct_bc_prev[0])
        s_act(prow[0:1, :], outst[0:1, 512:1024], AF.Ln, bias=cst[0:1, 1:2])
        cs_rr = s_act(r0w[0:1, :], prow[0:1, :], AF.Exp, scale=-0.5, inc="cs")
        cs_rr_prev[0] = cs_rr
        W("tensor", "cs", cs_rr)
        W("tensor", "cv", p0_state["pb4"])
        ct_bc = t_mm(pb[4][:, :], oner[:, :], r0w[0:1, :], True, True,
                     inc="ct")
        ct_bc_prev[0] = ct_bc
        W("vector", "ct", ct_bc)
        cv_rbc = v_copy(rbc[:, cs_:cs_ + 512], pb[4][:, :], inc="cv")
        p0_state["pb4"] = cv_rbc
        bg[4] = ("cv", cv_rbc)
        W("vector", "cv", cv_mb)
        W("vector", "cs", cs_rr)
        lst = []
        for t in range(NT):
            lst.append(v_tt(xtile(t)[:, cs_:cs_ + 512],
                            xtile(t)[:, cs_:cs_ + 512], m2b[:, 0:512],
                            ALU.subtract, inc="cv"))
        cv_center_l[c] = lst


    bg = {0: ("cv", 0), 1: ("cv", 0), 2: ("cv", 0), 3: ("cv", 0),
          4: ("cv", 0), 5: ("cv", 0), 6: ("cv", 0)}
    rb_free = [0, 0]
    cv_vtm_done = [0]
    vtmp_cv = [0, 0]
    pbt_free = 0
    ho_free = [0, 0, 0, 0]
    vtmp_free = [0, 0]
    gidx = 0
    ss_ho = {}
    ct_xdead = 0
    first_diag = [True]
    first_tr = [True]

    def Wg(eng, gate):
        W(eng, gate[0], gate[1])

    def proj_group(ridx, c, dst, bias_col, tile_gates=None):
        nonlocal gidx
        bk = gidx % 2
        bank = pb[bk]
        Wg("tensor", bg[bk])
        if ridx in sg_w:
            W("tensor", "sg", sg_w.pop(ridx))
        for t in range(NT):
            if tile_gates is not None:
                W("tensor", "cv", tile_gates[t])
            ct_l = t_mm(bank[:, :], wslot(ridx)[:, 128 * t:128 * (t + 1)],
                        xtile(t)[:, 512 * c:512 * (c + 1)], t == 0, t == NT - 1,
                        inc=("ct" if t == NT - 1 else None))
        sl = gidx % 2
        W("vector", "ct", ct_l)
        W("vector", "cs", vtmp_free[sl])
        cv_e = v_tt(vtmp[:, 512 * sl:512 * (sl + 1)], bank[:, :],
                    rbc[:, 512 * c:512 * (c + 1)], ALU.mult, inc="cv")
        bg[bk] = ("cv", cv_e)
        vtmp_cv[sl] = cv_e
        W("scalar", "cv", cv_e)
        W("scalar", "ss", ss_bq)
        cs_e = bd.op("scalar", lambda E, o=dst, i=vtmp[:, 512 * sl:512 * (sl + 1)],
                     b=bias_col: E.activation(o, i, AF.Identity, bias=b, scale=1.0),
                     inc="cs")
        vtmp_free[sl] = cs_e
        gidx += 1
        return ct_l, cs_e

    def attn_pair(h, p):
        cA, cB = 2 * p, 2 * p + 1
        cfg = {"A": dict(c=cA, sbk=2, obk=4, mbk=5, ps=0),
               "B": dict(c=cB, sbk=3, obk=6, mbk=1, ps=1)}
        for st in cfg.values():
            st["nkt"] = 4 * st["c"] + 4
            st["qcs"] = qc[:, 512 * st["c"]:512 * (st["c"] + 1)]
            st["ct_av"] = 0
            st["cs_prev"] = {2: 0, 3: 0}
        W("tensor", "cv", cv_vtm_done[0])
        W("tensor", "cs", max(cs_q[cA], cs_q[cB]))
        seq = []
        for i in range(cfg["B"]["nkt"]):
            if i < cfg["A"]["nkt"]:
                seq.append(("A", i))
            seq.append(("B", i))
        def emit_S(X, kt):
            st = cfg[X]
            c = st["c"]
            diag = kt >= 4 * c
            j0 = 128 * (kt - 4 * c) if diag else 0
            sbk = st["sbk"]
            if X == "B" and kt >= cfg["A"]["nkt"]:
                sbk = 2 + (kt - cfg["A"]["nkt"]) % 2
            ps = 2 * st["ps"] + (kt % 2)
            Wg("tensor", bg[sbk])
            W("tensor", "cs", st["cs_prev"][sbk])
            ct_s = t_mm(pb[sbk][:, j0:512], kb[:, 128 * kt:128 * (kt + 1)],
                        st["qcs"][:, j0:512], True, True, inc="ct")
            W("scalar", "ct", ct_s)
            cs_e = bd.op("scalar", lambda E,
                         o=pt[:, 512 * ps + j0:512 * (ps + 1)],
                         i=pb[sbk][:, j0:512]:
                         E.activation(o, i, AF.Exp, bias=cst[:, 0:1], scale=SC),
                         inc="cs")
            st["cs_prev"][sbk] = cs_e
            cv_msk = 0
            if diag:
                W("vector", "cs", cs_e)
                if first_diag[0]:
                    W("vector", "ss", ss_cm01)
                    first_diag[0] = False
                cv_msk = v_tt(pt[:, 512 * ps + j0:512 * ps + j0 + 128],
                              pt[:, 512 * ps + j0:512 * ps + j0 + 128],
                              cmask01_s[:], ALU.mult, inc="cv")
            return (X, kt, j0, cs_e, cv_msk, sbk)

        def emit_AV(pend):
            X, kt, j0, cs_e, cv_msk, sbk = pend
            st = cfg[X]
            c = st["c"]
            obk, mbk = st["obk"], st["mbk"]
            ps = 2 * st["ps"] + (kt % 2)
            if cv_msk:
                W("tensor", "cv", cv_msk)
            W("tensor", "cs", cs_e)
            if kt == 0:
                Wg("tensor", bg[obk])
                Wg("tensor", bg[mbk])
            t_mm(pb[obk][:, j0:512], vtm[:, 128 * kt:128 * (kt + 1)],
                 pt[:, 512 * ps + j0:512 * (ps + 1)], kt == 0, kt == st["nkt"] - 1)
            ct_av = t_mm(pb[mbk][:, j0:512], ones128[:, :],
                         pt[:, 512 * ps + j0:512 * (ps + 1)], kt == 0,
                         kt == st["nkt"] - 1, inc="ct")
            bg[sbk] = ("ct", ct_av)
            if kt == st["nkt"] - 1:
                es = st["ps"]
                W("scalar", "ct", ct_av)
                W("scalar", "cv", vtmp_cv[es])
                cs_ln = s_act(vtmp[:, 512 * es:512 * (es + 1)], pb[mbk][:, 0:512],
                              AF.Ln, inc="cs")
                bg[mbk] = ("cs", cs_ln)
                W("scalar", "cv", rb_free[es])
                cs_rb = s_act(nrmb[:, 512 * es:512 * (es + 1)],
                              vtmp[:, 512 * es:512 * (es + 1)], AF.Exp,
                              scale=-1.0, inc="cs")
                vtmp_free[es] = cs_rb
                W("vector", "cs", cs_rb)
                W("vector", "ss", ho_free[c])
                cv_ho = v_tt(ho[:, 512 * c:512 * (c + 1)], pb[obk][:, 0:512],
                             nrmb[:, 512 * es:512 * (es + 1)], ALU.mult, inc="cv")
                bg[obk] = ("cv", cv_ho)
                rb_free[es] = cv_ho
                W("sync", "cv", cv_ho)
                ss_ho[(h, c)] = s_dma(
                    buf2[c // 2, :, h, 512 * (c % 2):512 * (c % 2 + 1)],
                    ho[:, 512 * c:512 * (c + 1)])
                ho_free[c] = ss_ho[(h, c)]

        pend = None
        for item in seq:
            cur = emit_S(*item)
            if pend is not None:
                emit_AV(pend)
            pend = cur
        emit_AV(pend)

    dynv = {}
    for h in range(NH):
        for c in range(4):
            if h == 0:
                phase0_chunk(c, bg)
            ct_l, _ = proj_group(3 * h, c, vfm[:, 512 * c:512 * (c + 1)],
                                 bqkv_s[:, 3 * h:3 * h + 1],
                                 tile_gates=(cv_center_l[c] if h == 0
                                             else None))
        slot_free[(3 * h) % 3] = ct_l
        if h + 1 < NH:
            load_ring(3 * (h + 1), wqkv[h + 1, 0, :, :])
        cs_v_done = bd.cnt["cs"]

        cv_tr = 0
        for c in range(4):
            ct_l, _ = proj_group(3 * h + 1, c, kb[:, 512 * c:512 * (c + 1)],
                                 bqkv_s[:, 3 * h + 1:3 * h + 2])
            if c < 2:
                W("tensor", "cs", cs_v_done)
                W("tensor", "cv", pbt_free)
                if first_tr[0]:
                    W("tensor", "ss", ss_id)
                    first_tr[0] = False
                for k in range(8):
                    kk = 8 * c + k
                    ct_tr = t_tr(pbt[:, 128 * k:128 * (k + 1)],
                                 vfm[:, 128 * kk:128 * (kk + 1)],
                                 inc=("ct" if k == 7 else None))
                W("vector", "ct", ct_tr)
                cv_tr = v_copy(vtm[:, 1024 * c:1024 * (c + 1)], pbt[:, 0:1024],
                               inc="cv")
                pbt_free = cv_tr
        slot_free[(3 * h + 1) % 3] = ct_l
        if h + 1 < NH:
            load_ring(3 * (h + 1) + 1, wqkv[h + 1, 1, :, :])
        cv_vtm_done[0] = cv_tr

        cs_q = {}
        for c in range(4):
            ct_q, cs_q[c] = proj_group(3 * h + 2, c,
                                       qc[:, 512 * c:512 * (c + 1)],
                                       bqkv_s[:, 3 * h + 2:3 * h + 3])
            if c == 2:
                attn_pair(h, 0)
            if c == 3:
                slot_free[(3 * h + 2) % 3] = ct_q
                if h + 1 < NH:
                    load_ring(3 * (h + 1) + 2, wqkv[h + 1, 2, :, :])
                if h == NH - 1:
                    ct_xdead = ct_q
        if h == NH - 1:
            W("scalar", "ct", ct_xdead)
            W("scalar", "ss", max(ss_ho[(hh, cc_)] for hh in range(15)
                                  for cc_ in range(4)))
            sa_own = bd.op("scalar", lambda E: E.dma_start(
                out=hT[:, 0:15 * TOK].rearrange("p (t c) -> p t c", t=15),
                in_=buf2[bass.ds(dynv["mb_ac"], 1), :, 0:15, :].squeeze(0)),
                inc="sa", dv=16)
            W("scalar", "cc", 1)
            sa_p1 = bd.op("scalar", lambda E: E.dma_start(
                out=hT[:, 16 * TOK:24 * TOK].rearrange("p (t c) -> p t c", t=8),
                in_=agout1[bass.ds(dynv["pb_ac"], 1), :, :, :].squeeze(0)),
                inc="sa2", dv=16)
            W("scalar", "cc2", 1)
            sa_p2 = bd.op("scalar", lambda E: E.dma_start(
                out=hT[:, 24 * TOK:30 * TOK].rearrange("p (t c) -> p t c", t=6),
                in_=agout2[bass.ds(dynv["pb_ac"], 1), :, :, :].squeeze(0)),
                inc="sa3", dv=16)
            for m_ in range(3):
                load_ring(48 + m_, wo[m_, :, :])
        attn_pair(h, 1)

        if h == 7:
            W("gpsimd", "ss", max(ss_ho[(hh, c)] for hh in range(8)
                                  for c in range(4)))
            W("gpsimd", "ss", ss_pk)

            def _dyn(E):
                _, v1 = nc.values_load_multi_w_load_instructions(
                    pk_s[0:1, 0:1], engines=[Pool], min_val=0, max_val=1,
                    skip_runtime_bounds_check=True)
                dynv["pb"] = v1[0]
            bd.q["gpsimd"].append(_dyn)
            sg_a1 = bd.op("gpsimd", lambda E: E.dma_start(
                out=agin1[:],
                in_=buf2[bass.ds(dynv["pb"], 1), :, 0:8, :].squeeze(0)),
                inc="sg", dv=16)
            W("gpsimd", "sg", sg_a1)
            bd.op("gpsimd", lambda E: E.collective_compute(
                "AllGather", ALU.bypass, replica_groups=RG,
                ins=[agin1[:]], outs=[agout1[:]]), inc="cc")

            W("sync", "ss", ss_pk)

            def _dyn_sp(E):
                _, v1 = nc.values_load_multi_w_load_instructions(
                    pk_s[0:1, 0:1], engines=[SPE], min_val=0, max_val=1,
                    skip_runtime_bounds_check=True)
                dynv["pb_sp"] = v1[0]
                _, v2 = nc.values_load_multi_w_load_instructions(
                    pk_s[0:1, 1:2], engines=[SPE], min_val=0, max_val=1,
                    skip_runtime_bounds_check=True)
                dynv["mb_sp"] = v2[0]
            bd.q["sync"].append(_dyn_sp)
            W("scalar", "ss", ss_pk)

            def _dyn_ac(E):
                _, v1 = nc.values_load_multi_w_load_instructions(
                    pk_s[0:1, 0:1], engines=[ACTE], min_val=0, max_val=1,
                    skip_runtime_bounds_check=True)
                dynv["pb_ac"] = v1[0]
                _, v2 = nc.values_load_multi_w_load_instructions(
                    pk_s[0:1, 1:2], engines=[ACTE], min_val=0, max_val=1,
                    skip_runtime_bounds_check=True)
                dynv["mb_ac"] = v2[0]
            bd.q["scalar"].append(_dyn_ac)

        if h == 13:
            W("sync", "ss", max(ss_ho[(hh, c)] for hh in range(8, 14)
                                for c in range(4)))
            ss_a2 = bd.op("sync", lambda E: E.dma_start(
                out=agin2[:],
                in_=buf2[bass.ds(dynv["pb_sp"], 1), :, 8:14, :].squeeze(0)),
                inc="ss", dv=16)
            W("sync", "ss", ss_a2)
            W("gpsimd", "ss", ss_a2)
            bd.op("gpsimd", lambda E: E.collective_compute(
                "AllGather", ALU.bypass, replica_groups=RG,
                ins=[agin2[:]], outs=[agout2[:]]), inc="cc2")


    ct_A_end = bd.cnt["ct"]
    cv_A_end = bd.cnt["cv"]
    W("sync", "ss", max(ss_ho[(15, c)] for c in range(4)))
    ss_a3 = bd.op("sync", lambda E: E.dma_start(
        out=agin3[:],
        in_=buf2[bass.ds(dynv["pb_sp"], 1), :, 14:16, :].squeeze(0)),
        inc="ss", dv=16)
    W("sync", "ss", ss_a3)
    W("gpsimd", "ss", ss_a3)
    bd.op("gpsimd", lambda E: E.collective_compute(
        "AllGather", ALU.bypass, replica_groups=RG,
        ins=[agin3[:]], outs=[agout3[:]]), inc="cc3")
    ss_own15 = bd.op("sync", lambda E: E.dma_start(
        out=hT[:, 15 * TOK:16 * TOK],
        in_=buf2[bass.ds(dynv["mb_sp"], 1), :, 15:16, :]
        .squeeze(0).squeeze(1)), inc="ss", dv=16)
    W("sync", "ss", ss_own15)
    W("sync", "cc3", 1)
    ss_p3 = bd.op("sync", lambda E: E.dma_start(
        out=hT[:, 30 * TOK:32 * TOK].rearrange("p (t c) -> p t c", t=2),
        in_=agout3[bass.ds(dynv["pb_sp"], 1), :, :, :].squeeze(0)),
        inc="ss", dv=16)
    W("sync", "ss", ss_p3)
    ss_A_end = bd.cnt["ss"]
    W("gpsimd", "ct", ct_A_end)
    W("gpsimd", "cv", cv_A_end)
    W("gpsimd", "ss", ss_A_end)
    sg_lora = g_dma(loraT[:], lora[:])
    sg_bcat = g_dma(bcat_s[:], bcat[:])

    W("tensor", "ct", ct_A_end)
    W("tensor", "cv", cv_A_end)
    W("tensor", "cs", bd.cnt["cs"])
    ob_free = [0, 0, 0]
    xst_free = 0
    outst_free = [0, 0]
    gi = 0
    wb_ct_gate = {}

    def x1_evac(m, c, bank, ct_l, ss_x):
        nonlocal gi, xst_free
        sl = gi % 2
        W("vector", "ct", ct_l)
        W("vector", "ss", ss_x)
        W("vector", "cs", vtmp_free[sl])
        cv_a = v_tt(vtmp[:, 512 * sl:512 * (sl + 1)], bank[:, 0:T2],
                    xst[:, T2 * c:T2 * (c + 1)], ALU.add, inc="cv")
        if c == 1:
            xst_free = cv_a
        W("scalar", "cv", cv_a)
        W("scalar", "ss", ss_bo)
        cs_x1 = bd.op("scalar", lambda E,
                      o=x1tl(m)[:, T2 * c:T2 * (c + 1)],
                      i=vtmp[:, 512 * sl:512 * (sl + 1)], b=bo_s[:, m:m + 1]:
                      E.activation(o, i, AF.Identity, bias=b, scale=1.0),
                      inc="cs")
        ob_free[gi % 3] = cs_x1
        vtmp_free[sl] = cs_x1
        gi += 1
        return cs_x1

    ss_x = s_dma(xst[:], xr[:, 0, :])
    W("tensor", "sa", sa_own)
    W("tensor", "sg", sg_w.pop(48))
    for c in range(2):
        for t in range(30):
            if c == 0 and t == 15:
                W("tensor", "ss", ss_own15)
            if c == 0 and t == 16:
                W("tensor", "sa2", sa_p1)
            if c == 0 and t == 24:
                W("tensor", "sa3", sa_p2)
            t_mm(pb[c][:, 0:T2], wslot(48)[:, 128 * t:128 * (t + 1)],
                 htile(t)[:, T2 * c:T2 * (c + 1)], t == 0, False)
    W("tensor", "ss", ss_p3)
    ct01 = []
    for c in range(2):
        t_mm(pb[c][:, 0:T2], wslot(48)[:, 128 * 30:128 * 31],
             htile(30)[:, T2 * c:T2 * (c + 1)], False, False)
        ct01.append(t_mm(pb[c][:, 0:T2], wslot(48)[:, 128 * 31:128 * 32],
                         htile(31)[:, T2 * c:T2 * (c + 1)], False, True,
                         inc="ct"))
    for c in range(2):
        x1_evac(0, c, pb[c], ct01[c], ss_x)
    ct_l = ct01[1]
    slot_free[48 % 3] = ct_l
    load_ring(51, wo[3, :, :])

    for m in range(1, NT):
        W("sync", "cv", xst_free)
        ss_x = s_dma(xst[:], xr[:, m, :])
        for c in range(2):
            bank = pb[gi % 3]
            W("tensor", "cs", ob_free[gi % 3])
            if (48 + m) in sg_w:
                W("tensor", "sg", sg_w.pop(48 + m))
            for t in range(NT):
                ct_l = t_mm(bank[:, 0:T2], wslot(48 + m)[:, 128 * t:128 * (t + 1)],
                            htile(t)[:, T2 * c:T2 * (c + 1)], t == 0, t == NT - 1,
                            inc=("ct" if t == NT - 1 else None))
            x1_evac(m, c, bank, ct_l, ss_x)
        slot_free[(48 + m) % 3] = ct_l
        nxt = 48 + m + 3
        if nxt < 80:
            load_ring(nxt, wo[nxt - 48, :, :])
        else:
            load_ring(nxt, wb[nxt - 80, :, :])

    cs_stage1 = bd.cnt["cs"]

    pb3_gate = bg[3]
    pb4_gate = bg[4]
    pb5_gate = bg[5]
    pb6_gate = bg[6]
    rb2_free = max(rb_free)
    cs2_prev = [0]
    cs_ex_prev = 0
    cv_zw = {}
    for c in range(2):
        c0 = T2 * c
        W("tensor", "cs", cs_stage1)
        W("vector", "cs", cs_stage1)
        ct_st = 0
        hist = [0, 0]
        for t in range(NT):
            sl_ = t % 2
            xsl = x1tl(t)[:, c0:c0 + T2]
            xo_ = xsqt[:, 512 * sl_:512 * sl_ + T2]
            if sl_ == 0:
                W("vector", "ct", hist[0])
                sq_v = v_tt(xo_, xsl, xsl, ALU.mult, inc="cv")
            else:
                W("scalar", "ct", hist[1])
                sq_s = s_act(xo_, xsl, AF.Square, inc="cs")
            if t == 0:
                W("tensor", pb3_gate[0], pb3_gate[1])
            t_mm(pb[3][:, 0:T2], ones128[:, :], xsl, t == 0, t == NT - 1)
            if sl_ == 0:
                W("tensor", "cv", sq_v)
            else:
                W("tensor", "cs", sq_s)
            if t == 0:
                W("tensor", pb4_gate[0], pb4_gate[1])
            ct_st = t_mm(pb[4][:, 0:T2], ones128[:, :], xo_, t == 0, t == NT - 1,
                         inc="ct")
            hist[sl_] = ct_st
        W("vector", "ct", ct_st)
        cv_m = bd.op("vector", lambda E: E.tensor_scalar_mul(
            outst[:, 0:T2], pb[3][:, 0:T2], cst[:, 4:5]), inc="cv")
        pb3_gate = ("cv", cv_m)
        cv_mb = v_copy(m2b[:, c0:c0 + T2], outst[:, 0:T2], inc="cv")
        v_tt(vtmp[:, 0:T2], outst[:, 0:T2], outst[:, 0:T2], ALU.mult)
        W("vector", "cs", cs2_prev[0])
        cv_e2 = bd.op("vector", lambda E: E.tensor_scalar_mul(
            outst[:, 512:512 + T2], pb[4][:, 0:T2], cst[:, 4:5]), inc="cv")
        pb4_gate = ("cv", cv_e2)
        cv_var = v_tt(outst[:, 512:512 + T2], outst[:, 512:512 + T2],
                      vtmp[:, 0:T2], ALU.subtract, inc="cv")
        W("scalar", "cv", cv_var)
        s_act(vtmp[:, 512:512 + T2], outst[:, 512:512 + T2], AF.Ln,
              bias=cst[:, 1:2])
        W("scalar", "cv", rb2_free)
        s_act(r2bc[:, c0:c0 + T2], vtmp[:, 512:512 + T2], AF.Exp, scale=-0.5)
        cs_s2 = s_act(nrmb[:, 0:T2], vtmp[:, 512:512 + T2], AF.Exp, scale=0.5,
                      inc="cs")
        cs2_prev[0] = cs_s2
        W("vector", "cv", cv_mb)
        W("vector", "cs", cs_s2)
        cv_c2l = []
        for t in range(NT):
            cv_c2l.append(v_tt(x1tl(t)[:, c0:c0 + T2], x1tl(t)[:, c0:c0 + T2],
                               m2b[:, c0:c0 + T2], ALU.subtract, inc="cv"))
        W("tensor", pb5_gate[0], pb5_gate[1])
        if c == 0:
            W("tensor", "sg", sg_lora)
        for t in range(NT):
            W("tensor", "cv", cv_c2l[t])
            ct_z = t_mm(pb[5][0:36, 0:T2], loraT[:, 36 * t:36 * (t + 1)],
                        x1tl(t)[:, c0:c0 + T2], t == 0, t == NT - 1,
                        inc=("ct" if t == NT - 1 else None))
        W("vector", "ct", ct_z)
        W("vector", "cs", cs_ex_prev)
        cv_zr = v_tt(zr[:, :], pb[5][0:36, 0:T2], r2bc[0:36, c0:c0 + T2],
                     ALU.mult, inc="cv")
        pb5_gate = ("cv", cv_zr)
        W("scalar", "cv", cv_zr)
        W("scalar", "cv", cv_ab)
        cs_ex = bd.op("scalar", lambda E: E.activation(
            exps[32:36, :], zr[32:36, :], AF.Exp, bias=abias_s[32:36, 0:1],
            scale=1.0), inc="cs")
        cs_ex_prev = cs_ex
        W("tensor", "cs", cs_ex)
        W("tensor", pb6_gate[0], pb6_gate[1])
        ct_se = t_mm(pb[6][:, 0:T2], ones128[32:36, :], exps[32:36, :], True,
                     True, inc="ct")
        W("scalar", "ct", ct_se)
        cs_li = s_act(vtmp[:, 0:T2], pb[6][:, 0:T2], AF.Ln, inc="cs")
        pb6_gate = ("cs", cs_li)
        cs_inv = s_act(nrmb[:, 512:512 + T2], vtmp[:, 0:T2], AF.Exp, scale=-1.0,
                       inc="cs")
        W("vector", "cs", cs_inv)
        v_tt(nrmb[:, 512:512 + T2], nrmb[:, 0:T2], nrmb[:, 512:512 + T2],
             ALU.mult)
        cv_q2 = bd.op("vector", lambda E: E.tensor_scalar_mul(
            nrmb[:, 512:512 + T2], nrmb[:, 512:512 + T2], cst[:, 3:4]), inc="cv")
        W("tensor", pb4_gate[0], pb4_gate[1])
        W("tensor", "ss", ss_rep)
        ct_ar = t_mm(pb[4][0:32, 0:T2], rep4_s[32:36, :], exps[32:36, :], True,
                     True, inc="ct")
        W("vector", "ct", ct_ar)
        cv_asc = v_tt(exps[0:32, :], pb[4][0:32, 0:T2],
                      nrmb[0:32, 512:512 + T2], ALU.mult, inc="cv")
        pb4_gate = ("cv", cv_asc)
        cv_zw[c] = v_tt(zw[:, c0:c0 + T2], zr[0:32, :], exps[0:32, :], ALU.mult,
                        inc="cv")
        rb2_free = cv_asc

    for m in range(NT):
        for c in range(2):
            c0 = T2 * c
            bank = pb[gi % 3]
            W("tensor", "cs", ob_free[gi % 3])
            if (80 + m) in sg_w:
                W("tensor", "sg", sg_w.pop(80 + m))
            if m == 0 and c == 0:
                W("tensor", "sg", sg_bcat)
            for t in range(NT):
                t_mm(bank[:, 0:T2], wslot(80 + m)[:, 128 * t:128 * (t + 1)],
                     x1tl(t)[:, c0:c0 + T2], t == 0, False)
            W("tensor", "cv", cv_zw[c])
            ct_l = t_mm(bank[:, 0:T2], bcat_s[:, 128 * m:128 * (m + 1)],
                        zw[:, c0:c0 + T2], False, True, inc="ct")
            sl = gi % 2
            W("vector", "ct", ct_l)
            W("vector", "cs", vtmp_free[0])
            cv_y1 = v_tt(vtmp[:, 0:512], bank[:, 0:T2], r2bc[:, c0:c0 + T2],
                         ALU.mult, inc="cv")
            W("scalar", "cv", cv_y1)
            W("scalar", "ss", ss_bb)
            cs_y2 = bd.op("scalar", lambda E, b=bb_s[:, m:m + 1]: E.activation(
                vtmp[:, 512:1024], vtmp[:, 0:512], AF.Identity, bias=b, scale=1.0),
                inc="cs")
            ob_free[gi % 3] = cs_y2
            vtmp_free[0] = cs_y2
            W("vector", "cs", cs_y2)
            W("vector", "ss", outst_free[sl])
            cv_x2a = v_tt(outst[:, 512 * sl:512 * (sl + 1)], vtmp[:, 512:1024],
                          x1tl(m)[:, c0:c0 + T2], ALU.add, inc="cv")
            cv_x2 = v_tt(outst[:, 512 * sl:512 * (sl + 1)],
                         outst[:, 512 * sl:512 * (sl + 1)], m2b[:, c0:c0 + T2],
                         ALU.add, inc="cv")
            W("sync", "cv", cv_x2)
            ss_o = s_dma(out[:, m, c0:c0 + T2], outst[:, 512 * sl:512 * (sl + 1)])
            outst_free[sl] = ss_o
            gi += 1
        nxt = 80 + m + 3
        if nxt < 112:
            slot_free[(80 + m) % 3] = ct_l
            load_ring(nxt, wb[nxt - 80, :, :])

    bd.wait("sync", "ss", bd.cnt["ss"])

    with nc.Block() as block:
        for name, dec in (("gpsimd", block.gpsimd), ("sync", block.sync),
                          ("tensor", block.tensor), ("vector", block.vector),
                          ("scalar", block.scalar)):
            def make(fns):
                def emit(E):
                    for f in fns:
                        f(E)
                return emit
            dec(make(bd.q[name]))

    return nc


_CACHE = {}


def _get_program():
    if "nc" not in _CACHE:
        _CACHE["nc"] = build_program()
    return _CACHE["nc"]


def kernel(**inputs):
    x = np.ascontiguousarray(np.asarray(inputs["L0_attn_x"], dtype=np.float32))
    Wq = np.asarray(inputs["L0_attn_Wq"], dtype=np.float32)
    bq = np.asarray(inputs["L0_attn_bq"], dtype=np.float32)
    Wk = np.asarray(inputs["L0_attn_Wk"], dtype=np.float32)
    bk = np.asarray(inputs["L0_attn_bk"], dtype=np.float32)
    Wv = np.asarray(inputs["L0_attn_Wv"], dtype=np.float32)
    bv = np.asarray(inputs["L0_attn_bv"], dtype=np.float32)
    Wo = np.asarray(inputs["L0_attn_Wo"], dtype=np.float32)
    bo = np.asarray(inputs["L0_attn_bo"], dtype=np.float32)
    Wb = np.asarray(inputs["L0_ffn_W_base"], dtype=np.float32)
    bb = np.asarray(inputs["L0_ffn_b_base"], dtype=np.float32)
    active = np.asarray(inputs["L0_ffn_active"]).astype(np.int64)
    A = np.asarray(inputs["L0_ffn_lora_A"], dtype=np.float32)[active]
    Bm = np.asarray(inputs["L0_ffn_lora_B"], dtype=np.float32)[active]
    Wr = np.asarray(inputs["L0_ffn_router_W"], dtype=np.float32)[active]
    br = np.asarray(inputs["L0_ffn_router_b"], dtype=np.float32)[active]
    pr = np.asarray(inputs["L0_ffn_domain_prior"], dtype=np.float32)[active]

    WoT = np.ascontiguousarray(Wo.T)
    WbT = np.ascontiguousarray(Wb.T)
    loraT_full = np.concatenate(
        [A.transpose(2, 0, 1).reshape(D, 32), Wr.T], axis=1)
    bcat_full = np.ascontiguousarray(Bm.transpose(0, 2, 1).reshape(32, D))
    brpr_full = np.zeros((36, 2), np.float32)
    brpr_full[32:36, 0] = br
    brpr_full[32:36, 1] = pr
    rep4_full = np.zeros((36, 32), BF)
    for e in range(4):
        rep4_full[32 + e, 8 * e:8 * e + 8] = 1
    i_ = np.arange(128)
    cm = np.where(i_[None, :] >= i_[:, None], 0.0, -1e9).astype(np.float32)
    cm01 = np.where(i_[None, :] >= i_[:, None], 1.0, 0.0).astype(BF)
    idm_full = np.eye(128, dtype=BF)

    def head_slices(h):
        arr = np.empty((NH, 3, 128, NT * 128), np.float32)
        for hp in range(NH):
            for p, Wm in enumerate((Wv, Wk, Wq)):
                blk = Wm[2048 * h + 128 * hp:2048 * h + 128 * (hp + 1), :].T
                arr[hp, p] = _swz(np.ascontiguousarray(blk))
        return arr

    def fo_tiles(M):
        arr = np.empty((NT, 128, NT * 128), np.float32)
        for m in range(NT):
            arr[m] = _swz(np.ascontiguousarray(M[:, 128 * m:128 * (m + 1)]))
        return arr

    wb_tiles = fo_tiles(WbT)
    lora_swz = _swz(loraT_full)

    in_maps = []
    for c in range(8):
        b, h = c // 2, c % 2
        xtb = _swz(np.ascontiguousarray(x[b].T)).reshape(128, NT, L)
        xrb = _swz(np.ascontiguousarray(
            x[b, TOK * h:TOK * (h + 1), :].T)).reshape(128, NT, TOK)
        bcol = np.empty((128, NH * 3), np.float32)
        for hp in range(NH):
            for p, bvec in enumerate((bv, bk, bq)):
                bcol[:, 3 * hp + p] = bvec[2048 * h + 128 * hp:
                                           2048 * h + 128 * (hp + 1)]
        WoTp = WoT if h == 0 else np.ascontiguousarray(
            np.vstack([WoT[2048:], WoT[:2048]]))
        in_maps.append({
            "xt": xtb, "xr": xrb, "wqkv": head_slices(h), "bqkv_c": bcol,
            "wo": fo_tiles(WoTp),
            "bo_c": np.ascontiguousarray(bo.reshape(NT, 128).T),
            "wb": wb_tiles,
            "bb_c": np.ascontiguousarray(bb.reshape(NT, 128).T),
            "lora": lora_swz, "bcat": bcat_full, "brpr": brpr_full,
            "rep4": rep4_full, "cmsk": cm, "cmsk01": cm01,
            "idm": idm_full,
            "pk": np.array([[1 - h, h]], dtype=np.int32),
        })

    nc = _get_program()
    res = run_bass_kernel_spmd(nc, in_maps, core_ids=list(range(8)))
    final = np.empty((4, L, D), np.float32)
    for c in range(8):
        b, h = c // 2, c % 2
        o = np.asarray(res.results[c]["out"])
        final[b, TOK * h:TOK * (h + 1), :] = o.transpose(2, 1, 0).reshape(TOK, D)
    return final


# version 13
# speedup vs baseline: 1.2129x; 1.0145x over previous
import numpy as np
import ml_dtypes

import concourse.bass as bass
import concourse.mybir as mybir
from concourse.bass_utils import run_bass_kernel_spmd

F32 = mybir.dt.float32
BF16 = mybir.dt.bfloat16
I32 = mybir.dt.int32
AF = mybir.ActivationFunctionType
ALU = mybir.AluOpType
BF = ml_dtypes.bfloat16
Pool = mybir.EngineType.Pool
SPE = mybir.EngineType.SP
ACTE = mybir.EngineType.Activation

D = 4096
L = 2048
NT = 32
NH = 16
TOK = 1024
T2 = 512
SC = 0.08838834764831845
ENGS = ("tensor", "vector", "scalar", "sync", "gpsimd")
RG = [[0, 1], [2, 3], [4, 5], [6, 7]]


def _swz(m):
    c = m.shape[1]
    return np.ascontiguousarray(
        m.reshape(NT, 128, c).transpose(1, 0, 2)).reshape(128, NT * c)


class Build:
    def __init__(self, nc):
        self.nc = nc
        self.q = {e: [] for e in ENGS}
        self.sem = {}
        self.cnt = {}
        self.last_wait = {}

    def add_sem(self, name):
        self.sem[name] = self.nc.alloc_semaphore(name=name)
        self.cnt[name] = 0

    def op(self, eng, fn, inc=None, dv=1):
        if inc is None:
            self.q[eng].append(fn)
            return None
        self.cnt[inc] += dv
        s = self.sem[inc]

        def run(E, fn=fn, s=s, dv=dv):
            fn(E).then_inc(s, dv)
        self.q[eng].append(run)
        return self.cnt[inc]

    def wait(self, eng, name, val):
        if val is None or val <= 0:
            return
        k = (eng, name)
        if self.last_wait.get(k, 0) >= val:
            return
        self.last_wait[k] = val
        s = self.sem[name]
        self.q[eng].append(lambda E, s=s, v=val: E.wait_ge(s, v))


def build_program():
    nc = bass.Bass("TRN2", target_bir_lowering=False,
                   detect_race_conditions=False)

    din = lambda n, shp, dt=F32: nc.dram_tensor(n, shp, dt, kind="ExternalInput")
    xt = din("xt", [128, NT, L])
    xr = din("xr", [128, NT, TOK])
    wqkv = din("wqkv", [NH, 3, 128, NT * 128])
    bqkv_c = din("bqkv_c", [128, NH * 3])
    wo = din("wo", [NT, 128, NT * 128])
    bo_c = din("bo_c", [128, NT])
    wb = din("wb", [NT, 128, NT * 128])
    bb_c = din("bb_c", [128, NT])
    lora = din("lora", [128, NT * 36])
    bcat = din("bcat", [32, D])
    brpr = din("brpr", [36, 2])
    rep4 = din("rep4", [36, 32], BF16)
    cmsk = din("cmsk", [128, 128])
    cmsk01 = din("cmsk01", [128, 128], BF16)
    idm = din("idm", [128, 128], BF16)
    pk = din("pk", [1, 2], I32)

    out = nc.dram_tensor("out", [128, NT, TOK], F32, kind="ExternalOutput")
    buf2 = nc.dram_tensor("buf2", [2, 128, NH, TOK], BF16)
    agin1 = nc.dram_tensor("agin1", [128, 8, TOK], BF16)
    agin2 = nc.dram_tensor("agin2", [128, 6, TOK], BF16)
    agin3 = nc.dram_tensor("agin3", [128, 2, TOK], BF16)
    agout1 = nc.dram_tensor("agout1", [2, 128, 8, TOK], BF16)
    agout2 = nc.dram_tensor("agout2", [2, 128, 6, TOK], BF16)
    agout3 = nc.dram_tensor("agout3", [2, 128, 2, TOK], BF16)

    sb = nc.alloc_sbuf_tensor
    xbf = sb("xbf", [128, NT * L], BF16)
    rbc = sb("rbc", [128, L], BF16)
    kb = sb("kb", [128, L], BF16)
    qc = sb("qc", [128, 4 * 512], BF16)
    vfm = sb("vfm", [128, L], BF16)
    vtm = sb("vtm", [128, L], BF16)
    wr = sb("wr", [128, 3 * NT * 128], BF16)
    pt = sb("pt", [128, 4 * 512], BF16)
    ho = sb("ho", [128, L], BF16)
    cmask_s = sb("cmask_s", [128, 128], F32)
    cmask01_s = sb("cmask01_s", [128, 128], BF16)
    idm_s = sb("idm_s", [128, 128], BF16)
    cst = sb("cst", [128, 8], F32)
    onec = sb("onec", [128, 1], BF16)
    oner = sb("oner", [1, 128], BF16)
    bqkv_s = sb("bqkv_s", [128, NH * 3], F32)
    bo_s = sb("bo_s", [128, NT], F32)
    bb_s = sb("bb_s", [128, NT], F32)
    brpr_s = sb("brpr_s", [36, 2], F32)
    abias_s = sb("abias_s", [36, 1], F32)
    rep4_s = sb("rep4_s", [36, 32], BF16)
    pk_s = sb("pk_s", [1, 2], I32)
    ones128 = sb("ones128", [128, 128], BF16)
    vtmp = sb("vtmp", [128, 2 * 512], F32)
    xsqt = sb("xsqt", [128, 2 * 2048], BF16)
    prow = sb("prow", [1, 512], F32)
    r0w = sb("r0w", [1, 512], BF16)
    xst = sb("xst", [128, TOK], F32)
    outst = sb("outst", [128, 2 * 512], F32)
    nrmb = sb("nrmb", [128, 2 * 512], BF16)

    at = nc.alloc_sbuf_tensor_at
    adr = lambda t: nc.lookup_mloc(t).addr
    hT = at("hT", [128, NT * TOK], BF16, offset=adr(xbf))
    x1b = at("x1b", [128, NT * TOK], BF16, offset=adr(xbf) + 65536)
    bcat_s = at("bcat_s", [32, D], BF16, offset=adr(rbc))
    m2b = at("m2b", [128, TOK], BF16, offset=adr(qc))
    r2bc = at("r2bc", [128, TOK], BF16, offset=adr(qc) + 2048)
    s2row = at("s2row", [1, TOK], BF16, offset=adr(qc) + 4096)
    zw = at("zw", [32, TOK], BF16, offset=adr(qc) + 6144)
    zr = at("zr", [36, T2], F32, offset=adr(pt))
    exps = at("exps", [36, T2], BF16, offset=adr(pt) + 2048)
    loraT = at("loraT", [128, NT * 36], BF16, offset=adr(pt) + 3072)

    pb = [nc.alloc_psum_tensor(f"pb{i}", [128, 512], F32) for i in range(7)]
    pbt = nc.alloc_psum_tensor("pbt", [128, 1024], BF16)

    bd = Build(nc)
    for s in ("sg", "ss", "ct", "cv", "cs", "cg", "cc", "cc2", "cc3",
              "sa", "sa2", "sa3"):
        bd.add_sem(s)
    W = bd.wait

    def g_dma(dst, src, fence=True):
        v = bd.op("gpsimd", lambda E, d=dst, s=src: E.dma_start(out=d, in_=s),
                  inc="sg", dv=16)
        if fence:
            W("gpsimd", "sg", v)
        return v

    def s_dma(dst, src, fence=True):
        v = bd.op("sync", lambda E, d=dst, s=src: E.dma_start(out=d, in_=s),
                  inc="ss", dv=16)
        if fence:
            W("sync", "ss", v)
        return v

    def a_dma(dst, src, fence=True):
        v = bd.op("scalar", lambda E, d=dst, s=src: E.dma_start(out=d, in_=s),
                  inc="sa", dv=16)
        if fence:
            W("scalar", "sa", v)
        return v

    def t_mm(o, l, r, st, sp, inc=None):
        return bd.op("tensor", lambda E, o=o, l=l, r=r, st=st, sp=sp:
                     E.matmul(o, l, r, start=st, stop=sp), inc=inc)

    def t_tr(o, i, inc=None):
        return bd.op("tensor", lambda E, o=o, i=i: E.transpose(o, i, idm_s[:]),
                     inc=inc)

    def v_tt(o, a, b2, alu, inc=None):
        return bd.op("vector", lambda E, o=o, a=a, b=b2, al=alu:
                     E.tensor_tensor(o, a, b, al), inc=inc)

    def v_copy(o, i, inc=None):
        return bd.op("vector", lambda E, o=o, i=i: E.tensor_copy(o, i), inc=inc)

    def v_tsc(o, i, c, alu, inc=None):
        return bd.op("vector", lambda E, o=o, i=i, c=c, al=alu:
                     E.tensor_single_scalar(o, i, c, al), inc=inc)

    def v_recip(o, i, inc=None):
        def f(E, o=o, i=i):
            with nc.allow_low_precision("bf16 softmax-normaliser reciprocal"):
                return E.reciprocal(o, i)
        return bd.op("vector", f, inc=inc)

    def s_act(o, i, f, bias=0.0, scale=1.0, inc=None):
        return bd.op("scalar", lambda E, o=o, i=i, f=f, b=bias, sc=scale:
                     E.activation(o, i, f, bias=b, scale=sc), inc=inc)

    xtile = lambda t: xbf[:, L * t:L * (t + 1)]
    htile = lambda t: hT[:, TOK * t:TOK * (t + 1)]
    x1tl = lambda t: x1b[:, TOK * t:TOK * (t + 1)]

    CN, CEPS, CE12, C2, CID = (cst[:, i:i + 1] for i in range(5))

    ss_cm = s_dma(cmask_s[:], cmsk[:])
    ss_cm01 = s_dma(cmask01_s[:], cmsk01[:])
    ss_id = s_dma(idm_s[:], idm[:])
    ss_bq = s_dma(bqkv_s[:], bqkv_c[:])
    ss_bo = s_dma(bo_s[:], bo_c[:])
    ss_bb = s_dma(bb_s[:], bb_c[:])
    ss_br = s_dma(brpr_s[:], brpr[:])
    ss_rep = s_dma(rep4_s[:], rep4[:])
    ss_pk = s_dma(pk_s[:], pk[:])
    for i, v in enumerate((-25.0, 1e-5, 1e-12, 2.0, 1.0 / D)):
        bd.op("gpsimd", lambda E, i=i, v=v: E.memset(cst[:, i:i + 1], v), inc="cg")
    bd.op("gpsimd", lambda E: E.memset(onec[:], 1.0), inc="cg")
    bd.op("gpsimd", lambda E: E.memset(oner[:], 1.0), inc="cg")
    cg_ones = bd.op("gpsimd", lambda E: E.memset(ones128[:], 1.0), inc="cg")
    W("vector", "ss", ss_br)
    W("vector", "cg", cg_ones)
    cv_ab0 = bd.op("vector", lambda E: E.tensor_scalar_add(
        abias_s[32:36, 0:1], brpr_s[32:36, 1:2], cst[32:36, 2:3]), inc="cv")
    W("scalar", "cv", cv_ab0)
    cs_ab = s_act(abias_s[32:36, 0:1], abias_s[32:36, 0:1], AF.Ln, inc="cs")
    W("vector", "cs", cs_ab)
    cv_ab = v_tt(abias_s[32:36, 0:1], abias_s[32:36, 0:1], brpr_s[32:36, 0:1],
                 ALU.add, inc="cv")

    slot_free = [0, 0, 0]

    def wslot(i):
        return wr[:, NT * 128 * (i % 3):NT * 128 * ((i % 3) + 1)]

    sg_w = {}

    def load_ring(i, src_ap):
        W("gpsimd", "ct", slot_free[i % 3])
        sg_w[i] = g_dma(wslot(i), src_ap)

    sg_x = []
    for t in range(NT):
        v = g_dma(xtile(t), xt[:, t, :], fence=(t < 8 or t % 4 == 3))
        sg_x.append(v)
        if t in (13, 21, 29):
            load_ring((t - 13) // 8, wqkv[0, (t - 13) // 8, :, :])
    for t in range(8, NT):
        sg_x[t] = sg_x[min(NT - 1, (t // 4) * 4 + 3)]

    ct_sq_hist = [0, 0]
    for t in range(NT):
        W("tensor", "sg", sg_x[t])
        if t == 0:
            W("tensor", "cg", cg_ones)
        for c in range(4):
            t_mm(pb[c][:, :], ones128[:, :],
                 xtile(t)[:, 512 * c:512 * (c + 1)], t == 0, t == NT - 1)
        sl_ = t % 2
        xo_ = xsqt[:, 2048 * sl_:2048 * (sl_ + 1)]
        if sl_ == 0:
            W("vector", "sg", sg_x[t])
            W("vector", "ct", ct_sq_hist[0])
            sq_v = v_tt(xo_, xtile(t)[:, :], xtile(t)[:, :], ALU.mult,
                        inc="cv")
            W("tensor", "cv", sq_v)
        else:
            W("scalar", "sg", sg_x[t])
            W("scalar", "ct", ct_sq_hist[1])
            sq_s = s_act(xo_, xtile(t)[:, :], AF.Square, inc="cs")
            W("tensor", "cs", sq_s)
        for c in range(4):
            rp = 32 * (c % 2)
            ct_sq = t_mm(pb[5 + c // 2][rp:rp + 1, :], onec[:, :],
                         xo_[:, 512 * c:512 * (c + 1)], t == 0, t == NT - 1,
                         inc=("ct" if c == 3 else None))
        ct_sq_hist[sl_] = ct_sq

    ct_stream = bd.cnt["ct"]
    cv_center_l = [None] * 4
    cs_rr_prev = [0]
    ct_bc_prev = [0]
    p0_state = {"pb4": 0}

    def phase0_chunk(c, bg):
        cs_ = 512 * c
        W("vector", "ct", ct_stream)
        W("vector", "cs", cs_rr_prev[0])
        cv_m = bd.op("vector", lambda E, c=c: E.tensor_scalar_mul(
            outst[:, 0:512], pb[c][:, :], cst[:, 4:5]), inc="cv")
        bg[c] = ("cv", cv_m)
        cv_mb = v_copy(m2b[:, 0:512], outst[:, 0:512], inc="cv")
        rp = 32 * (c % 2)
        cv_e2 = bd.op("vector", lambda E, c=c, rp=rp: E.tensor_scalar_mul(
            outst[0:1, 512:1024], pb[5 + c // 2][rp:rp + 1, :],
            cst[0:1, 4:5]), inc="cv")
        bg[5 + c // 2] = ("cv", cv_e2)
        v_tt(prow[0:1, :], outst[0:1, 0:512], outst[0:1, 0:512], ALU.mult)
        cv_var = v_tt(outst[0:1, 512:1024], outst[0:1, 512:1024], prow[0:1, :],
                      ALU.subtract, inc="cv")
        W("scalar", "cv", cv_var)
        W("scalar", "ct", ct_bc_prev[0])
        s_act(prow[0:1, :], outst[0:1, 512:1024], AF.Ln, bias=cst[0:1, 1:2])
        cs_rr = s_act(r0w[0:1, :], prow[0:1, :], AF.Exp, scale=-0.5, inc="cs")
        cs_rr_prev[0] = cs_rr
        W("tensor", "cs", cs_rr)
        W("tensor", "cv", p0_state["pb4"])
        ct_bc = t_mm(pb[4][:, :], oner[:, :], r0w[0:1, :], True, True,
                     inc="ct")
        ct_bc_prev[0] = ct_bc
        W("vector", "ct", ct_bc)
        cv_rbc = v_copy(rbc[:, cs_:cs_ + 512], pb[4][:, :], inc="cv")
        p0_state["pb4"] = cv_rbc
        bg[4] = ("cv", cv_rbc)
        W("vector", "cv", cv_mb)
        W("vector", "cs", cs_rr)
        lst = []
        for t in range(NT):
            lst.append(v_tt(xtile(t)[:, cs_:cs_ + 512],
                            xtile(t)[:, cs_:cs_ + 512], m2b[:, 0:512],
                            ALU.subtract, inc="cv"))
        cv_center_l[c] = lst


    bg = {0: ("cv", 0), 1: ("cv", 0), 2: ("cv", 0), 3: ("cv", 0),
          4: ("cv", 0), 5: ("cv", 0), 6: ("cv", 0)}
    rb_free = [0, 0]
    cv_vtm_done = [0]
    vtmp_cv = [0, 0]
    pbt_free = 0
    ho_free = [0, 0, 0, 0]
    vtmp_free = [0, 0]
    gidx = 0
    ss_ho = {}
    ct_xdead = 0
    first_diag = [True]
    first_tr = [True]

    def Wg(eng, gate):
        W(eng, gate[0], gate[1])

    def proj_group(ridx, c, dst, bias_col, tile_gates=None):
        nonlocal gidx
        bk = gidx % 2
        bank = pb[bk]
        Wg("tensor", bg[bk])
        if ridx in sg_w:
            W("tensor", "sg", sg_w.pop(ridx))
        for t in range(NT):
            if tile_gates is not None:
                W("tensor", "cv", tile_gates[t])
            ct_l = t_mm(bank[:, :], wslot(ridx)[:, 128 * t:128 * (t + 1)],
                        xtile(t)[:, 512 * c:512 * (c + 1)], t == 0, t == NT - 1,
                        inc=("ct" if t == NT - 1 else None))
        sl = gidx % 2
        W("vector", "ct", ct_l)
        W("vector", "cs", vtmp_free[sl])
        cv_e = v_tt(vtmp[:, 512 * sl:512 * (sl + 1)], bank[:, :],
                    rbc[:, 512 * c:512 * (c + 1)], ALU.mult, inc="cv")
        bg[bk] = ("cv", cv_e)
        vtmp_cv[sl] = cv_e
        W("scalar", "cv", cv_e)
        W("scalar", "ss", ss_bq)
        cs_e = bd.op("scalar", lambda E, o=dst, i=vtmp[:, 512 * sl:512 * (sl + 1)],
                     b=bias_col: E.activation(o, i, AF.Identity, bias=b, scale=1.0),
                     inc="cs")
        vtmp_free[sl] = cs_e
        gidx += 1
        return ct_l, cs_e

    def attn_pair(h, p):
        cA, cB = 2 * p, 2 * p + 1
        cfg = {"A": dict(c=cA, sbk=2, obk=4, mbk=5, ps=0),
               "B": dict(c=cB, sbk=3, obk=6, mbk=1, ps=1)}
        for st in cfg.values():
            st["nkt"] = 4 * st["c"] + 4
            st["qcs"] = qc[:, 512 * st["c"]:512 * (st["c"] + 1)]
            st["ct_av"] = 0
            st["cs_prev"] = {2: 0, 3: 0}
        W("tensor", "cv", cv_vtm_done[0])
        W("tensor", "cs", max(cs_q[cA], cs_q[cB]))
        seq = []
        for i in range(cfg["B"]["nkt"]):
            if i < cfg["A"]["nkt"]:
                seq.append(("A", i))
            seq.append(("B", i))
        def emit_S(X, kt):
            st = cfg[X]
            c = st["c"]
            diag = kt >= 4 * c
            j0 = 128 * (kt - 4 * c) if diag else 0
            sbk = st["sbk"]
            if X == "B" and kt >= cfg["A"]["nkt"]:
                sbk = 2 + (kt - cfg["A"]["nkt"]) % 2
            ps = 2 * st["ps"] + (kt % 2)
            Wg("tensor", bg[sbk])
            W("tensor", "cs", st["cs_prev"][sbk])
            ct_s = t_mm(pb[sbk][:, j0:512], kb[:, 128 * kt:128 * (kt + 1)],
                        st["qcs"][:, j0:512], True, True, inc="ct")
            W("scalar", "ct", ct_s)
            cs_e = bd.op("scalar", lambda E,
                         o=pt[:, 512 * ps + j0:512 * (ps + 1)],
                         i=pb[sbk][:, j0:512]:
                         E.activation(o, i, AF.Exp, bias=cst[:, 0:1], scale=SC),
                         inc="cs")
            st["cs_prev"][sbk] = cs_e
            cv_msk = 0
            if diag:
                W("vector", "cs", cs_e)
                if first_diag[0]:
                    W("vector", "ss", ss_cm01)
                    first_diag[0] = False
                cv_msk = v_tt(pt[:, 512 * ps + j0:512 * ps + j0 + 128],
                              pt[:, 512 * ps + j0:512 * ps + j0 + 128],
                              cmask01_s[:], ALU.mult, inc="cv")
            return (X, kt, j0, cs_e, cv_msk, sbk)

        def emit_AV(pend):
            X, kt, j0, cs_e, cv_msk, sbk = pend
            st = cfg[X]
            c = st["c"]
            obk, mbk = st["obk"], st["mbk"]
            ps = 2 * st["ps"] + (kt % 2)
            if cv_msk:
                W("tensor", "cv", cv_msk)
            W("tensor", "cs", cs_e)
            if kt == 0:
                Wg("tensor", bg[obk])
                Wg("tensor", bg[mbk])
            t_mm(pb[obk][:, j0:512], vtm[:, 128 * kt:128 * (kt + 1)],
                 pt[:, 512 * ps + j0:512 * (ps + 1)], kt == 0, kt == st["nkt"] - 1)
            ct_av = t_mm(pb[mbk][:, j0:512], ones128[:, :],
                         pt[:, 512 * ps + j0:512 * (ps + 1)], kt == 0,
                         kt == st["nkt"] - 1, inc="ct")
            bg[sbk] = ("ct", ct_av)
            if kt == st["nkt"] - 1:
                es = st["ps"]
                W("scalar", "ct", ct_av)
                W("scalar", "cv", vtmp_cv[es])
                cs_ln = s_act(vtmp[:, 512 * es:512 * (es + 1)], pb[mbk][:, 0:512],
                              AF.Ln, inc="cs")
                bg[mbk] = ("cs", cs_ln)
                W("scalar", "cv", rb_free[es])
                cs_rb = s_act(nrmb[:, 512 * es:512 * (es + 1)],
                              vtmp[:, 512 * es:512 * (es + 1)], AF.Exp,
                              scale=-1.0, inc="cs")
                vtmp_free[es] = cs_rb
                W("vector", "cs", cs_rb)
                W("vector", "ss", ho_free[c])
                cv_ho = v_tt(ho[:, 512 * c:512 * (c + 1)], pb[obk][:, 0:512],
                             nrmb[:, 512 * es:512 * (es + 1)], ALU.mult, inc="cv")
                bg[obk] = ("cv", cv_ho)
                rb_free[es] = cv_ho
                W("sync", "cv", cv_ho)
                ss_ho[(h, c)] = s_dma(
                    buf2[c // 2, :, h, 512 * (c % 2):512 * (c % 2 + 1)],
                    ho[:, 512 * c:512 * (c + 1)])
                ho_free[c] = ss_ho[(h, c)]

        pend = None
        for item in seq:
            cur = emit_S(*item)
            if pend is not None:
                emit_AV(pend)
            pend = cur
        emit_AV(pend)

    dynv = {}
    for h in range(NH):
        for c in range(0 if h == 0 else 1, 4):
            if h == 0:
                phase0_chunk(c, bg)
            ct_l, _ = proj_group(3 * h, c, vfm[:, 512 * c:512 * (c + 1)],
                                 bqkv_s[:, 3 * h:3 * h + 1],
                                 tile_gates=(cv_center_l[c] if h == 0
                                             else None))
        slot_free[(3 * h) % 3] = ct_l
        if h + 1 < NH:
            load_ring(3 * (h + 1), wqkv[h + 1, 0, :, :])
        cs_v_done = bd.cnt["cs"]

        cv_tr = 0
        for c in range(4):
            ct_l, _ = proj_group(3 * h + 1, c, kb[:, 512 * c:512 * (c + 1)],
                                 bqkv_s[:, 3 * h + 1:3 * h + 2])
            if c < 2:
                W("tensor", "cs", cs_v_done)
                W("tensor", "cv", pbt_free)
                if first_tr[0]:
                    W("tensor", "ss", ss_id)
                    first_tr[0] = False
                for k in range(8):
                    kk = 8 * c + k
                    ct_tr = t_tr(pbt[:, 128 * k:128 * (k + 1)],
                                 vfm[:, 128 * kk:128 * (kk + 1)],
                                 inc=("ct" if k == 7 else None))
                W("vector", "ct", ct_tr)
                cv_tr = v_copy(vtm[:, 1024 * c:1024 * (c + 1)], pbt[:, 0:1024],
                               inc="cv")
                pbt_free = cv_tr
        slot_free[(3 * h + 1) % 3] = ct_l
        if h + 1 < NH:
            load_ring(3 * (h + 1) + 1, wqkv[h + 1, 1, :, :])
        cv_vtm_done[0] = cv_tr

        cs_q = {}
        for c in range(4):
            ct_q, cs_q[c] = proj_group(3 * h + 2, c,
                                       qc[:, 512 * c:512 * (c + 1)],
                                       bqkv_s[:, 3 * h + 2:3 * h + 3])
            if c == 2:
                attn_pair(h, 0)
            if c == 3:
                slot_free[(3 * h + 2) % 3] = ct_q
                if h + 1 < NH:
                    load_ring(3 * (h + 1) + 2, wqkv[h + 1, 2, :, :])
                if h == NH - 1:
                    ct_xdead = ct_q
        if h == NH - 1:
            W("scalar", "ct", ct_xdead)
            W("scalar", "ss", max(ss_ho[(hh, cc_)] for hh in range(15)
                                  for cc_ in range(4)))
            sa_own = bd.op("scalar", lambda E: E.dma_start(
                out=hT[:, 0:15 * TOK].rearrange("p (t c) -> p t c", t=15),
                in_=buf2[bass.ds(dynv["mb_ac"], 1), :, 0:15, :].squeeze(0)),
                inc="sa", dv=16)
            W("scalar", "cc", 1)
            sa_p1 = bd.op("scalar", lambda E: E.dma_start(
                out=hT[:, 16 * TOK:24 * TOK].rearrange("p (t c) -> p t c", t=8),
                in_=agout1[bass.ds(dynv["pb_ac"], 1), :, :, :].squeeze(0)),
                inc="sa2", dv=16)
            W("scalar", "cc2", 1)
            sa_p2 = bd.op("scalar", lambda E: E.dma_start(
                out=hT[:, 24 * TOK:30 * TOK].rearrange("p (t c) -> p t c", t=6),
                in_=agout2[bass.ds(dynv["pb_ac"], 1), :, :, :].squeeze(0)),
                inc="sa3", dv=16)
            for m_ in range(3):
                load_ring(48 + m_, wo[m_, :, :])
        if h + 1 < NH:
            proj_group(3 * (h + 1), 0, vfm[:, 0:512],
                       bqkv_s[:, 3 * (h + 1):3 * (h + 1) + 1])
        attn_pair(h, 1)

        if h == 7:
            W("gpsimd", "ss", max(ss_ho[(hh, c)] for hh in range(8)
                                  for c in range(4)))
            W("gpsimd", "ss", ss_pk)

            def _dyn(E):
                _, v1 = nc.values_load_multi_w_load_instructions(
                    pk_s[0:1, 0:1], engines=[Pool], min_val=0, max_val=1,
                    skip_runtime_bounds_check=True)
                dynv["pb"] = v1[0]
            bd.q["gpsimd"].append(_dyn)
            sg_a1 = bd.op("gpsimd", lambda E: E.dma_start(
                out=agin1[:],
                in_=buf2[bass.ds(dynv["pb"], 1), :, 0:8, :].squeeze(0)),
                inc="sg", dv=16)
            W("gpsimd", "sg", sg_a1)
            bd.op("gpsimd", lambda E: E.collective_compute(
                "AllGather", ALU.bypass, replica_groups=RG,
                ins=[agin1[:]], outs=[agout1[:]]), inc="cc")

            W("sync", "ss", ss_pk)

            def _dyn_sp(E):
                _, v1 = nc.values_load_multi_w_load_instructions(
                    pk_s[0:1, 0:1], engines=[SPE], min_val=0, max_val=1,
                    skip_runtime_bounds_check=True)
                dynv["pb_sp"] = v1[0]
                _, v2 = nc.values_load_multi_w_load_instructions(
                    pk_s[0:1, 1:2], engines=[SPE], min_val=0, max_val=1,
                    skip_runtime_bounds_check=True)
                dynv["mb_sp"] = v2[0]
            bd.q["sync"].append(_dyn_sp)
            W("scalar", "ss", ss_pk)

            def _dyn_ac(E):
                _, v1 = nc.values_load_multi_w_load_instructions(
                    pk_s[0:1, 0:1], engines=[ACTE], min_val=0, max_val=1,
                    skip_runtime_bounds_check=True)
                dynv["pb_ac"] = v1[0]
                _, v2 = nc.values_load_multi_w_load_instructions(
                    pk_s[0:1, 1:2], engines=[ACTE], min_val=0, max_val=1,
                    skip_runtime_bounds_check=True)
                dynv["mb_ac"] = v2[0]
            bd.q["scalar"].append(_dyn_ac)

        if h == 13:
            W("sync", "ss", max(ss_ho[(hh, c)] for hh in range(8, 14)
                                for c in range(4)))
            ss_a2 = bd.op("sync", lambda E: E.dma_start(
                out=agin2[:],
                in_=buf2[bass.ds(dynv["pb_sp"], 1), :, 8:14, :].squeeze(0)),
                inc="ss", dv=16)
            W("sync", "ss", ss_a2)
            W("gpsimd", "ss", ss_a2)
            bd.op("gpsimd", lambda E: E.collective_compute(
                "AllGather", ALU.bypass, replica_groups=RG,
                ins=[agin2[:]], outs=[agout2[:]]), inc="cc2")


    ct_A_end = bd.cnt["ct"]
    cv_A_end = bd.cnt["cv"]
    W("sync", "ss", max(ss_ho[(15, c)] for c in range(4)))
    ss_a3 = bd.op("sync", lambda E: E.dma_start(
        out=agin3[:],
        in_=buf2[bass.ds(dynv["pb_sp"], 1), :, 14:16, :].squeeze(0)),
        inc="ss", dv=16)
    W("sync", "ss", ss_a3)
    W("gpsimd", "ss", ss_a3)
    bd.op("gpsimd", lambda E: E.collective_compute(
        "AllGather", ALU.bypass, replica_groups=RG,
        ins=[agin3[:]], outs=[agout3[:]]), inc="cc3")
    ss_own15 = bd.op("sync", lambda E: E.dma_start(
        out=hT[:, 15 * TOK:16 * TOK],
        in_=buf2[bass.ds(dynv["mb_sp"], 1), :, 15:16, :]
        .squeeze(0).squeeze(1)), inc="ss", dv=16)
    W("sync", "ss", ss_own15)
    W("sync", "cc3", 1)
    ss_p3 = bd.op("sync", lambda E: E.dma_start(
        out=hT[:, 30 * TOK:32 * TOK].rearrange("p (t c) -> p t c", t=2),
        in_=agout3[bass.ds(dynv["pb_sp"], 1), :, :, :].squeeze(0)),
        inc="ss", dv=16)
    W("sync", "ss", ss_p3)
    ss_A_end = bd.cnt["ss"]
    W("gpsimd", "ct", ct_A_end)
    W("gpsimd", "cv", cv_A_end)
    W("gpsimd", "ss", ss_A_end)
    sg_lora = g_dma(loraT[:], lora[:])
    sg_bcat = g_dma(bcat_s[:], bcat[:])

    W("tensor", "ct", ct_A_end)
    W("tensor", "cv", cv_A_end)
    W("tensor", "cs", bd.cnt["cs"])
    ob_free = [0, 0, 0]
    xst_free = 0
    outst_free = [0, 0]
    gi = 0
    wb_ct_gate = {}

    def x1_evac(m, c, bank, ct_l, ss_x):
        nonlocal gi, xst_free
        sl = gi % 2
        W("vector", "ct", ct_l)
        W("vector", "ss", ss_x)
        W("vector", "cs", vtmp_free[sl])
        cv_a = v_tt(vtmp[:, 512 * sl:512 * (sl + 1)], bank[:, 0:T2],
                    xst[:, T2 * c:T2 * (c + 1)], ALU.add, inc="cv")
        if c == 1:
            xst_free = cv_a
        W("scalar", "cv", cv_a)
        W("scalar", "ss", ss_bo)
        cs_x1 = bd.op("scalar", lambda E,
                      o=x1tl(m)[:, T2 * c:T2 * (c + 1)],
                      i=vtmp[:, 512 * sl:512 * (sl + 1)], b=bo_s[:, m:m + 1]:
                      E.activation(o, i, AF.Identity, bias=b, scale=1.0),
                      inc="cs")
        ob_free[gi % 3] = cs_x1
        vtmp_free[sl] = cs_x1
        gi += 1
        return cs_x1

    ss_x = s_dma(xst[:], xr[:, 0, :])
    W("tensor", "sa", sa_own)
    W("tensor", "sg", sg_w.pop(48))
    for c in range(2):
        for t in range(30):
            if c == 0 and t == 15:
                W("tensor", "ss", ss_own15)
            if c == 0 and t == 16:
                W("tensor", "sa2", sa_p1)
            if c == 0 and t == 24:
                W("tensor", "sa3", sa_p2)
            t_mm(pb[c][:, 0:T2], wslot(48)[:, 128 * t:128 * (t + 1)],
                 htile(t)[:, T2 * c:T2 * (c + 1)], t == 0, False)
    W("tensor", "ss", ss_p3)
    ct01 = []
    for c in range(2):
        t_mm(pb[c][:, 0:T2], wslot(48)[:, 128 * 30:128 * 31],
             htile(30)[:, T2 * c:T2 * (c + 1)], False, False)
        ct01.append(t_mm(pb[c][:, 0:T2], wslot(48)[:, 128 * 31:128 * 32],
                         htile(31)[:, T2 * c:T2 * (c + 1)], False, True,
                         inc="ct"))
    for c in range(2):
        x1_evac(0, c, pb[c], ct01[c], ss_x)
    ct_l = ct01[1]
    slot_free[48 % 3] = ct_l
    load_ring(51, wo[3, :, :])

    for m in range(1, NT):
        W("sync", "cv", xst_free)
        ss_x = s_dma(xst[:], xr[:, m, :])
        for c in range(2):
            bank = pb[gi % 3]
            W("tensor", "cs", ob_free[gi % 3])
            if (48 + m) in sg_w:
                W("tensor", "sg", sg_w.pop(48 + m))
            for t in range(NT):
                ct_l = t_mm(bank[:, 0:T2], wslot(48 + m)[:, 128 * t:128 * (t + 1)],
                            htile(t)[:, T2 * c:T2 * (c + 1)], t == 0, t == NT - 1,
                            inc=("ct" if t == NT - 1 else None))
            x1_evac(m, c, bank, ct_l, ss_x)
        slot_free[(48 + m) % 3] = ct_l
        nxt = 48 + m + 3
        if nxt < 80:
            load_ring(nxt, wo[nxt - 48, :, :])
        else:
            load_ring(nxt, wb[nxt - 80, :, :])

    cs_stage1 = bd.cnt["cs"]

    pb3_gate = bg[3]
    pb4_gate = bg[4]
    pb5_gate = bg[5]
    pb6_gate = bg[6]
    rb2_free = max(rb_free)
    cs2_prev = [0]
    cs_ex_prev = 0
    cv_zw = {}
    for c in range(2):
        c0 = T2 * c
        W("tensor", "cs", cs_stage1)
        W("vector", "cs", cs_stage1)
        ct_st = 0
        hist = [0, 0]
        for t in range(NT):
            sl_ = t % 2
            xsl = x1tl(t)[:, c0:c0 + T2]
            xo_ = xsqt[:, 512 * sl_:512 * sl_ + T2]
            if sl_ == 0:
                W("vector", "ct", hist[0])
                sq_v = v_tt(xo_, xsl, xsl, ALU.mult, inc="cv")
            else:
                W("scalar", "ct", hist[1])
                sq_s = s_act(xo_, xsl, AF.Square, inc="cs")
            if t == 0:
                W("tensor", pb3_gate[0], pb3_gate[1])
            t_mm(pb[3][:, 0:T2], ones128[:, :], xsl, t == 0, t == NT - 1)
            if sl_ == 0:
                W("tensor", "cv", sq_v)
            else:
                W("tensor", "cs", sq_s)
            if t == 0:
                W("tensor", pb4_gate[0], pb4_gate[1])
            ct_st = t_mm(pb[4][:, 0:T2], ones128[:, :], xo_, t == 0, t == NT - 1,
                         inc="ct")
            hist[sl_] = ct_st
        W("vector", "ct", ct_st)
        cv_m = bd.op("vector", lambda E: E.tensor_scalar_mul(
            outst[:, 0:T2], pb[3][:, 0:T2], cst[:, 4:5]), inc="cv")
        pb3_gate = ("cv", cv_m)
        cv_mb = v_copy(m2b[:, c0:c0 + T2], outst[:, 0:T2], inc="cv")
        v_tt(vtmp[:, 0:T2], outst[:, 0:T2], outst[:, 0:T2], ALU.mult)
        W("vector", "cs", cs2_prev[0])
        cv_e2 = bd.op("vector", lambda E: E.tensor_scalar_mul(
            outst[:, 512:512 + T2], pb[4][:, 0:T2], cst[:, 4:5]), inc="cv")
        pb4_gate = ("cv", cv_e2)
        cv_var = v_tt(outst[:, 512:512 + T2], outst[:, 512:512 + T2],
                      vtmp[:, 0:T2], ALU.subtract, inc="cv")
        W("scalar", "cv", cv_var)
        s_act(vtmp[:, 512:512 + T2], outst[:, 512:512 + T2], AF.Ln,
              bias=cst[:, 1:2])
        W("scalar", "cv", rb2_free)
        s_act(r2bc[:, c0:c0 + T2], vtmp[:, 512:512 + T2], AF.Exp, scale=-0.5)
        cs_s2 = s_act(nrmb[:, 0:T2], vtmp[:, 512:512 + T2], AF.Exp, scale=0.5,
                      inc="cs")
        cs2_prev[0] = cs_s2
        W("vector", "cv", cv_mb)
        W("vector", "cs", cs_s2)
        cv_c2l = []
        for t in range(NT):
            cv_c2l.append(v_tt(x1tl(t)[:, c0:c0 + T2], x1tl(t)[:, c0:c0 + T2],
                               m2b[:, c0:c0 + T2], ALU.subtract, inc="cv"))
        W("tensor", pb5_gate[0], pb5_gate[1])
        if c == 0:
            W("tensor", "sg", sg_lora)
        for t in range(NT):
            W("tensor", "cv", cv_c2l[t])
            ct_z = t_mm(pb[5][0:36, 0:T2], loraT[:, 36 * t:36 * (t + 1)],
                        x1tl(t)[:, c0:c0 + T2], t == 0, t == NT - 1,
                        inc=("ct" if t == NT - 1 else None))
        W("vector", "ct", ct_z)
        W("vector", "cs", cs_ex_prev)
        cv_zr = v_tt(zr[:, :], pb[5][0:36, 0:T2], r2bc[0:36, c0:c0 + T2],
                     ALU.mult, inc="cv")
        pb5_gate = ("cv", cv_zr)
        W("scalar", "cv", cv_zr)
        W("scalar", "cv", cv_ab)
        cs_ex = bd.op("scalar", lambda E: E.activation(
            exps[32:36, :], zr[32:36, :], AF.Exp, bias=abias_s[32:36, 0:1],
            scale=1.0), inc="cs")
        cs_ex_prev = cs_ex
        W("tensor", "cs", cs_ex)
        W("tensor", pb6_gate[0], pb6_gate[1])
        ct_se = t_mm(pb[6][:, 0:T2], ones128[32:36, :], exps[32:36, :], True,
                     True, inc="ct")
        W("scalar", "ct", ct_se)
        cs_li = s_act(vtmp[:, 0:T2], pb[6][:, 0:T2], AF.Ln, inc="cs")
        pb6_gate = ("cs", cs_li)
        cs_inv = s_act(nrmb[:, 512:512 + T2], vtmp[:, 0:T2], AF.Exp, scale=-1.0,
                       inc="cs")
        W("vector", "cs", cs_inv)
        v_tt(nrmb[:, 512:512 + T2], nrmb[:, 0:T2], nrmb[:, 512:512 + T2],
             ALU.mult)
        cv_q2 = bd.op("vector", lambda E: E.tensor_scalar_mul(
            nrmb[:, 512:512 + T2], nrmb[:, 512:512 + T2], cst[:, 3:4]), inc="cv")
        W("tensor", pb4_gate[0], pb4_gate[1])
        W("tensor", "ss", ss_rep)
        ct_ar = t_mm(pb[4][0:32, 0:T2], rep4_s[32:36, :], exps[32:36, :], True,
                     True, inc="ct")
        W("vector", "ct", ct_ar)
        cv_asc = v_tt(exps[0:32, :], pb[4][0:32, 0:T2],
                      nrmb[0:32, 512:512 + T2], ALU.mult, inc="cv")
        pb4_gate = ("cv", cv_asc)
        cv_zw[c] = v_tt(zw[:, c0:c0 + T2], zr[0:32, :], exps[0:32, :], ALU.mult,
                        inc="cv")
        rb2_free = cv_asc

    for m in range(NT):
        for c in range(2):
            c0 = T2 * c
            bank = pb[gi % 3]
            W("tensor", "cs", ob_free[gi % 3])
            if (80 + m) in sg_w:
                W("tensor", "sg", sg_w.pop(80 + m))
            if m == 0 and c == 0:
                W("tensor", "sg", sg_bcat)
            for t in range(NT):
                t_mm(bank[:, 0:T2], wslot(80 + m)[:, 128 * t:128 * (t + 1)],
                     x1tl(t)[:, c0:c0 + T2], t == 0, False)
            W("tensor", "cv", cv_zw[c])
            ct_l = t_mm(bank[:, 0:T2], bcat_s[:, 128 * m:128 * (m + 1)],
                        zw[:, c0:c0 + T2], False, True, inc="ct")
            sl = gi % 2
            W("vector", "ct", ct_l)
            W("vector", "cs", vtmp_free[0])
            cv_y1 = v_tt(vtmp[:, 0:512], bank[:, 0:T2], r2bc[:, c0:c0 + T2],
                         ALU.mult, inc="cv")
            W("scalar", "cv", cv_y1)
            W("scalar", "ss", ss_bb)
            cs_y2 = bd.op("scalar", lambda E, b=bb_s[:, m:m + 1]: E.activation(
                vtmp[:, 512:1024], vtmp[:, 0:512], AF.Identity, bias=b, scale=1.0),
                inc="cs")
            ob_free[gi % 3] = cs_y2
            vtmp_free[0] = cs_y2
            W("vector", "cs", cs_y2)
            W("vector", "ss", outst_free[sl])
            cv_x2a = v_tt(outst[:, 512 * sl:512 * (sl + 1)], vtmp[:, 512:1024],
                          x1tl(m)[:, c0:c0 + T2], ALU.add, inc="cv")
            cv_x2 = v_tt(outst[:, 512 * sl:512 * (sl + 1)],
                         outst[:, 512 * sl:512 * (sl + 1)], m2b[:, c0:c0 + T2],
                         ALU.add, inc="cv")
            W("sync", "cv", cv_x2)
            ss_o = s_dma(out[:, m, c0:c0 + T2], outst[:, 512 * sl:512 * (sl + 1)])
            outst_free[sl] = ss_o
            gi += 1
        nxt = 80 + m + 3
        if nxt < 112:
            slot_free[(80 + m) % 3] = ct_l
            load_ring(nxt, wb[nxt - 80, :, :])

    bd.wait("sync", "ss", bd.cnt["ss"])

    with nc.Block() as block:
        for name, dec in (("gpsimd", block.gpsimd), ("sync", block.sync),
                          ("tensor", block.tensor), ("vector", block.vector),
                          ("scalar", block.scalar)):
            def make(fns):
                def emit(E):
                    for f in fns:
                        f(E)
                return emit
            dec(make(bd.q[name]))

    return nc


_CACHE = {}


def _get_program():
    if "nc" not in _CACHE:
        _CACHE["nc"] = build_program()
    return _CACHE["nc"]


def kernel(**inputs):
    x = np.ascontiguousarray(np.asarray(inputs["L0_attn_x"], dtype=np.float32))
    Wq = np.asarray(inputs["L0_attn_Wq"], dtype=np.float32)
    bq = np.asarray(inputs["L0_attn_bq"], dtype=np.float32)
    Wk = np.asarray(inputs["L0_attn_Wk"], dtype=np.float32)
    bk = np.asarray(inputs["L0_attn_bk"], dtype=np.float32)
    Wv = np.asarray(inputs["L0_attn_Wv"], dtype=np.float32)
    bv = np.asarray(inputs["L0_attn_bv"], dtype=np.float32)
    Wo = np.asarray(inputs["L0_attn_Wo"], dtype=np.float32)
    bo = np.asarray(inputs["L0_attn_bo"], dtype=np.float32)
    Wb = np.asarray(inputs["L0_ffn_W_base"], dtype=np.float32)
    bb = np.asarray(inputs["L0_ffn_b_base"], dtype=np.float32)
    active = np.asarray(inputs["L0_ffn_active"]).astype(np.int64)
    A = np.asarray(inputs["L0_ffn_lora_A"], dtype=np.float32)[active]
    Bm = np.asarray(inputs["L0_ffn_lora_B"], dtype=np.float32)[active]
    Wr = np.asarray(inputs["L0_ffn_router_W"], dtype=np.float32)[active]
    br = np.asarray(inputs["L0_ffn_router_b"], dtype=np.float32)[active]
    pr = np.asarray(inputs["L0_ffn_domain_prior"], dtype=np.float32)[active]

    WoT = np.ascontiguousarray(Wo.T)
    WbT = np.ascontiguousarray(Wb.T)
    loraT_full = np.concatenate(
        [A.transpose(2, 0, 1).reshape(D, 32), Wr.T], axis=1)
    bcat_full = np.ascontiguousarray(Bm.transpose(0, 2, 1).reshape(32, D))
    brpr_full = np.zeros((36, 2), np.float32)
    brpr_full[32:36, 0] = br
    brpr_full[32:36, 1] = pr
    rep4_full = np.zeros((36, 32), BF)
    for e in range(4):
        rep4_full[32 + e, 8 * e:8 * e + 8] = 1
    i_ = np.arange(128)
    cm = np.where(i_[None, :] >= i_[:, None], 0.0, -1e9).astype(np.float32)
    cm01 = np.where(i_[None, :] >= i_[:, None], 1.0, 0.0).astype(BF)
    idm_full = np.eye(128, dtype=BF)

    def head_slices(h):
        arr = np.empty((NH, 3, 128, NT * 128), np.float32)
        for hp in range(NH):
            for p, Wm in enumerate((Wv, Wk, Wq)):
                blk = Wm[2048 * h + 128 * hp:2048 * h + 128 * (hp + 1), :].T
                arr[hp, p] = _swz(np.ascontiguousarray(blk))
        return arr

    def fo_tiles(M):
        arr = np.empty((NT, 128, NT * 128), np.float32)
        for m in range(NT):
            arr[m] = _swz(np.ascontiguousarray(M[:, 128 * m:128 * (m + 1)]))
        return arr

    wb_tiles = fo_tiles(WbT)
    lora_swz = _swz(loraT_full)

    in_maps = []
    for c in range(8):
        b, h = c // 2, c % 2
        xtb = _swz(np.ascontiguousarray(x[b].T)).reshape(128, NT, L)
        xrb = _swz(np.ascontiguousarray(
            x[b, TOK * h:TOK * (h + 1), :].T)).reshape(128, NT, TOK)
        bcol = np.empty((128, NH * 3), np.float32)
        for hp in range(NH):
            for p, bvec in enumerate((bv, bk, bq)):
                bcol[:, 3 * hp + p] = bvec[2048 * h + 128 * hp:
                                           2048 * h + 128 * (hp + 1)]
        WoTp = WoT if h == 0 else np.ascontiguousarray(
            np.vstack([WoT[2048:], WoT[:2048]]))
        in_maps.append({
            "xt": xtb, "xr": xrb, "wqkv": head_slices(h), "bqkv_c": bcol,
            "wo": fo_tiles(WoTp),
            "bo_c": np.ascontiguousarray(bo.reshape(NT, 128).T),
            "wb": wb_tiles,
            "bb_c": np.ascontiguousarray(bb.reshape(NT, 128).T),
            "lora": lora_swz, "bcat": bcat_full, "brpr": brpr_full,
            "rep4": rep4_full, "cmsk": cm, "cmsk01": cm01,
            "idm": idm_full,
            "pk": np.array([[1 - h, h]], dtype=np.int32),
        })

    nc = _get_program()
    res = run_bass_kernel_spmd(nc, in_maps, core_ids=list(range(8)))
    final = np.empty((4, L, D), np.float32)
    for c in range(8):
        b, h = c // 2, c % 2
        o = np.asarray(res.results[c]["out"])
        final[b, TOK * h:TOK * (h + 1), :] = o.transpose(2, 1, 0).reshape(TOK, D)
    return final
